# version 06
# baseline (speedup 1.0000x reference)
import contextlib
import numpy as np
import ml_dtypes
import concourse.bass as bass
import concourse.mybir as mybir
from concourse.bass_utils import run_bass_kernel_spmd

F32 = mybir.dt.float32
BF16 = mybir.dt.bfloat16
I32 = mybir.dt.int32
AF = mybir.ActivationFunctionType
ALU = mybir.AluOpType
AX = mybir.AxisListType

ENGINES = ("tensor", "vector", "scalar", "gpsimd", "sync")

DEBUG_TAPS = False

LN_EPS = 1e-5
QK_SHIFT = 12.0
SC_SHIFT = 40.0
DSCALE = float(1.0 / np.sqrt(128.0))

SEQ, HID, NT = 2048, 2048, 1024
SHX_REG = 2048 * 1024
FLA0 = 0
BASE0 = 2 * SHX_REG
SHA_SIZE = 4 * SHX_REG


class Sem:
    def __init__(self, handle):
        self.h = handle
        self.n = 0

    def bump(self, k):
        self.n += k
        return self.n


class Prog:
    def __init__(self, nc):
        self.nc = nc
        self.ops = {e: [] for e in ENGINES}
        self.es = contextlib.ExitStack()
        self._semid = 0

    def sem(self, name=None):
        self._semid += 1
        h = self.es.enter_context(self.nc.semaphore(name or f"s{self._semid}"))
        return Sem(h)

    def sbuf(self, name, shape, dtype):
        return self.es.enter_context(self.nc.sbuf_tensor(name, shape, dtype))

    def psum(self, name, shape, dtype=F32):
        return self.es.enter_context(self.nc.psum_tensor(name, shape, dtype))

    def reg(self, name):
        return self.es.enter_context(self.nc.gpsimd.register(name))

    def wait(self, eng, sem, val=None):
        v = sem.n if val is None else val
        if v <= 0:
            return
        self.ops[eng].append(lambda e, h=sem.h, v=v: e.wait_ge(h, v))

    def emit(self, eng, fn, inc=None, inc_k=1):
        if inc is not None:
            inc.bump(inc_k)

            def wrapped(e, fn=fn, h=inc.h, k=inc_k):
                ins = fn(e)
                ins.then_inc(h, k)
            self.ops[eng].append(wrapped)
        else:
            self.ops[eng].append(fn)

    def mk_auto(self):
        self.auto = {e: self.sem(f"auto_{e}") for e in
                     ("tensor", "vector", "scalar", "gpsimd")}
        self.adma = {e: self.sem(f"adma_{e}") for e in ("sync", "gpsimd",
                                                        "scalar")}
        self.xch = self.sem("xch")

    def a(self, eng, fn):
        self.emit(eng, fn, inc=self.auto[eng])

    def admas(self, eng, out_ap, in_ap):
        self.emit(eng, lambda e, o=out_ap, i=in_ap: e.dma_start(o, i),
                  inc=self.adma[eng], inc_k=16)

    def w(self, eng, *srcs):
        for s in srcs:
            if s == "self":
                s = eng
            if s == "xch":
                self.wait(eng, self.xch)
                continue
            if s.startswith("d"):
                sem = self.adma[s[1:]]
            else:
                sem = self.auto[s]
            self.wait(eng, sem)

    def mark(self, eng):
        return self.auto[eng].n

    def dmark(self, eng):
        return self.adma[eng].n

    def wait_at(self, eng, kind, val):
        sem = self.adma[kind[1:]] if kind.startswith("d") else self.auto[kind]
        self.wait(eng, sem, val)

    def join(self):
        sems = list(self.auto.values()) + list(self.adma.values()) + [self.xch]
        for e in ENGINES:
            for s in sems:
                self.wait(e, s)

    def finalize(self):
        with self.nc.Block() as block:
            for eng in ENGINES:
                ops = self.ops[eng]
                if not ops:
                    continue
                getattr(block, eng)(lambda h, ops=ops: [f(h) for f in ops])
        self.es.close()


class Arena:

    def __init__(self, p, name, kb):
        self.t = p.sbuf(name, [128, kb * 256], F32)
        self.kb = kb
        self.off = 0

    def reset(self):
        self.off = 0

    def take(self, shape, dtype):
        esz = 4 if dtype == F32 else (2 if dtype == BF16 else 4)
        nb = shape[1] * esz
        off = (self.off + 63) & ~63
        self.off = off + nb
        assert self.off <= self.kb * 1024, \
            f"arena overflow: {self.off} > {self.kb * 1024}"
        sl = self.t[0:shape[0], off // 4: off // 4 + ((nb + 3) // 4)]
        if dtype == F32:
            return sl[:, 0:shape[1]]
        return sl.bitcast(dtype)[:, 0:shape[1]]


def host_constants():
    c = {}
    c["c_idn"] = np.eye(128, dtype=ml_dtypes.bfloat16)
    c["c_idn_f32"] = np.eye(128, dtype=np.float32)
    c["c_uincl"] = np.triu(np.ones((128, 128), np.float32))
    c["c_ones_row"] = np.ones((1, 128), np.float32)
    c["c_ones_row_bf"] = np.ones((1, 128), dtype=ml_dtypes.bfloat16)
    c["c_ones_col_bf"] = np.ones((128, 1), dtype=ml_dtypes.bfloat16)
    NEG = -1e30
    c["c_m_sl"] = np.where(np.tri(128, 128, -1, dtype=bool), 0., NEG).astype(np.float32)
    c["c_m_su"] = np.where(np.triu(np.ones((128, 128), bool), 1), 0., NEG).astype(np.float32)
    c["c_m_iu"] = np.where(np.triu(np.ones((128, 128), bool), 0), 0., NEG).astype(np.float32)
    for r in range(4):
        k_idx = np.arange(128)[:, None]
        q_idx = np.arange(512)[None, :]
        m = np.where(k_idx <= q_idx - 128 * r, 0., NEG)
        c[f"c_bmask{r}"] = m.astype(ml_dtypes.bfloat16)
    return c



def make_ln_tmp(p, W):
    return {
        "sq": p.sbuf("ln_sq", [128, W], F32),
        "s1": p.sbuf("ln_s1", [128, 1], F32),
        "ssq": p.sbuf("ln_ssq", [128, 1], F32),
        "m": p.sbuf("ln_m", [128, 1], F32),
        "m2": p.sbuf("ln_m2", [128, 1], F32),
        "var": p.sbuf("ln_var", [128, 1], F32),
        "rstd": p.sbuf("ln_rstd", [128, 1], F32),
        "weps": p.sbuf("ln_weps", [128, 1], F32),
        "lnwh": p.sbuf("ln_lnwh", [128, 1], F32),
        "W": W,
    }


def emit_ln(p, xt, out_tile, W, tmp, sv, ss, sums=None, out_slice=None):
    inv = 1.0 / W
    if sums is None:
        p.emit("vector", lambda v: v.reduce_sum(
            tmp["s1"][:], xt, axis=AX.X), inc=sv)
        sums = tmp["s1"]
    p.wait("scalar", sv)
    p.emit("scalar", lambda s: s.activation(
        tmp["sq"][:, 0:W], xt, AF.Square, accum_out=tmp["ssq"][:]), inc=ss)
    p.wait("vector", sv)
    p.wait("vector", ss)
    p.emit("vector", lambda v: v.tensor_scalar(
        tmp["m"][:], sums[:], inv, None, ALU.mult))
    p.emit("vector", lambda v: v.tensor_tensor(
        tmp["m2"][:], sums[:], sums[:], ALU.mult), inc=sv)
    p.wait("vector", sv)
    p.emit("vector", lambda v: v.scalar_tensor_tensor(
        tmp["var"][:], tmp["m2"][:], -inv, tmp["ssq"][:], ALU.mult, ALU.add),
        inc=sv)
    p.wait("scalar", sv)
    p.emit("scalar", lambda s: s.activation(
        tmp["var"][:], tmp["var"][:], AF.Ln, bias=tmp["weps"][:]), inc=ss)
    p.wait("scalar", ss)
    p.emit("scalar", lambda s: s.activation(
        tmp["rstd"][:], tmp["var"][:], AF.Exp, scale=-0.5,
        bias=tmp["lnwh"][:]), inc=ss)
    p.wait("vector", ss)
    dst = out_tile if out_slice is None else out_slice
    p.emit("vector", lambda v: v.tensor_scalar(
        dst, xt, tmp["m"][:], tmp["rstd"][:], ALU.subtract, ALU.mult), inc=sv)



def build_fused(nc):
    p = Prog(nc)
    p.mk_auto()
    T, V, S, G, SY = "tensor", "vector", "scalar", "gpsimd", "sync"

    E, I_, SI = 16, 1408, 5632

    ins = {n: nc.dram_tensor(n, list(s), d, kind="ExternalInput")
           for n, (s, d) in {
        "xin": ([NT, HID], F32),
        "nonce": ([1, 1], I32),
        "wq0": ([HID, 1024], F32),
        "wk0": ([HID, 512], F32),
        "wv0": ([HID, 512], F32),
        "wg": ([128, 1], F32),
        "bg": ([1, 1], F32),
        "wo0": ([HID, HID], F32),
        "gw_t": ([HID, 16], F32),
        "egu_t": ([E, HID, 2 * I_], F32),
        "edn_t": ([E, I_, HID], F32),
        "sgu_t": ([HID, 2 * SI], F32),
        "sdn_t": ([SI, HID], F32),
        "sgv_l": ([128, 16], F32),
        "wq1": ([HID, 1024], F32),
        "wk1": ([HID, 1024], F32),
        "wv1": ([HID, 1024], F32),
        "bq_l": ([128, 8], F32),
        "bk_l": ([128, 8], F32),
        "bv_l": ([128, 8], F32),
        "alibi_c": ([128, 8 * 16], F32),
        "negbd": ([8, SEQ], F32),
        "wo1": ([HID, HID], F32),
        "bo": ([1, HID], F32),
        "w1t": ([HID, HID], F32),
        "scale_r": ([1, HID], F32),
        "bias_r": ([1, HID], F32),
        "c_idn": ([128, 128], BF16),
        "c_idn_f32": ([128, 128], F32),
        "c_uincl": ([128, 128], F32),
        "c_ones_row": ([1, 128], F32),
        "c_ones_row_bf": ([1, 128], BF16),
        "c_ones_col_bf": ([128, 1], BF16),
        "c_m_sl": ([128, 128], F32),
        "c_m_su": ([128, 128], F32),
        "c_m_iu": ([128, 128], F32),
        "c_bmask0": ([128, 512], BF16),
        "c_bmask1": ([128, 512], BF16),
        "c_bmask2": ([128, 512], BF16),
        "c_bmask3": ([128, 512], BF16),
    }.items()}

    xout = nc.dram_tensor("xout", [NT, HID], F32, kind="ExternalOutput")
    xres1 = nc.dram_tensor("xres1", [NT, HID], F32)
    xres2 = nc.dram_tensor("xres2", [NT, HID], F32)
    xres3 = nc.dram_tensor("xres3", [NT, HID], F32)
    flaloc = nc.dram_tensor("flaloc", [SEQ * NT], BF16)
    baseloc = nc.dram_tensor("baseloc", [NT * SEQ], BF16)
    shx = nc.dram_tensor("shx", [2 * SHX_REG], BF16, addr_space="Shared")
    sha = nc.dram_tensor("sha", [SHA_SIZE], BF16, addr_space="Shared")
    flags = nc.dram_tensor("flags", [2, 8], I32, addr_space="Shared")

    pid = nc.partition_id()
    half = pid % 2
    oth = 1 - half

    A1 = Arena(p, "A1", 64)
    A2 = Arena(p, "A2", 32)
    A4 = Arena(p, "A4", 60)
    AW = Arena(p, "AW", 16)
    xrt = [p.sbuf(f"xrt{i}", [128, HID], F32) for i in range(2)]
    tmp = make_ln_tmp(p, HID)
    nt_sb = p.sbuf("nt_sb", [1, 1], I32)
    idn_bf = p.sbuf("idn_bf", [128, 128], BF16)
    idn_f = p.sbuf("idn_f", [128, 128], F32)
    uincl = p.sbuf("uincl_s", [128, 128], F32)
    ones_r = p.sbuf("ones_r", [1, 128], F32)
    ones_rb = p.sbuf("ones_rb", [1, 128], BF16)
    ones_cb = p.sbuf("ones_cb", [128, 1], BF16)
    m_sl = p.sbuf("m_sl", [128, 128], F32)
    m_su = p.sbuf("m_su", [128, 128], F32)
    m_iu = p.sbuf("m_iu", [128, 128], F32)
    bmask = [p.sbuf(f"bmask{r}", [128, 512], BF16) for r in range(4)]
    PS = [p.psum(f"PS{i}", [128, 512], F32) for i in range(8)]
    PTB = [PS[6][:, 0:64].bitcast(BF16), PS[7][:, 0:64].bitcast(BF16)]

    nr = p.reg("nr")
    pr = p.reg("pr")
    pr2 = p.reg("pr2")
    fr = p.reg("fr")
    f2r = p.reg("f2r")
    cr = p.reg("cr")

    def mm(out_ap, l_ap, r_ap, start=True, stop=True):
        p.a(T, lambda t, o=out_ap, a=l_ap, b=r_ap, st=start, sp=stop:
            t.matmul(o, a, b, start=st, stop=sp))

    def scp(out_ap, in_ap):
        p.a(S, lambda s, o=out_ap, i=in_ap: s.copy(o, i))

    def act(out_ap, in_ap, f, bias=0.0, scale=1.0, accum=None):
        if accum is None:
            p.a(S, lambda s, o=out_ap, i=in_ap, ff=f, b=bias, sc=scale:
                s.activation(o, i, ff, bias=b, scale=sc))
        else:
            p.a(S, lambda s, o=out_ap, i=in_ap, ff=f, b=bias, sc=scale,
                ac=accum: s.activation(o, i, ff, bias=b, scale=sc,
                                       accum_out=ac))

    def dynap(handle, off_rv, rows, cols, row_stride):
        return bass.AP(handle, off_rv, [[row_stride, rows], [1, cols]])

    def exchange(phase):
        p.w(G, "dgpsimd")

        def do_x(g, ph=phase):
            g.reg_add(pr, nr, ph)
            g.store(dynap(flags, half * 8 + (ph - 1), 1, 1, 8), pr)
            g.reg_add(pr2, pr, pr)
            g.reg_mov(cr, 1)
            with g.While(cr):
                g.load(fr, flags[0:1, ph - 1:ph])
                g.load(f2r, flags[1:2, ph - 1:ph])
                g.reg_add(cr, fr, f2r)
                g.reg_sub(cr, cr, pr2)
        p.ops[G].append(do_x)
        p.emit(G, lambda g: g.nop(), inc=p.xch)
        p.join()

    for nm, dst in [("c_idn", idn_bf), ("c_idn_f32", idn_f),
                    ("c_uincl", uincl), ("c_ones_row", ones_r),
                    ("c_ones_row_bf", ones_rb), ("c_ones_col_bf", ones_cb),
                    ("c_m_sl", m_sl), ("c_m_su", m_su), ("c_m_iu", m_iu)]:
        p.admas(SY, dst[:], ins[nm][:, :])
    for r in range(4):
        p.admas(SY, bmask[r][:], ins[f"c_bmask{r}"][:, :])
    p.admas(G, nt_sb[:], ins["nonce"][:, :])
    p.w(G, "dgpsimd")
    p.ops[G].append(lambda g: g.reg_load(nr, nt_sb[0:1, 0:1]))
    p.a(V, lambda v: v.memset(tmp["weps"][:], HID * LN_EPS))
    p.a(V, lambda v: v.memset(tmp["lnwh"][:], 0.5 * float(np.log(HID))))

    def ph_lntr(src, phase):
        A4.reset(); A2.reset()
        xln = A4.take([128, HID], BF16)
        strips = A2.take([128, 16 * NT], BF16)
        sv, ss = p.auto["vector"], p.auto["scalar"]
        tr_done = [0]
        for t in range(8):
            buf = xrt[t % 2]
            p.w(SY, "vector", "scalar")
            p.admas(SY, buf[:], src[t * 128:(t + 1) * 128, :])
            p.w(V, "dsync")
            p.w(S, "dsync")
            if t > 0:
                p.w(V, "tensor")
            emit_ln(p, buf[:], xln[:], HID, tmp, sv, ss)
            p.w(T, "vector")
            if t == 0:
                p.w(T, "dsync")
            for hc in range(16):
                bk = tr_done[0] % 2
                p.w(T, "scalar")
                p.a(T, lambda tt, bk=bk, hc=hc: tt.transpose(
                    PTB[bk], xln[:, hc * 128:(hc + 1) * 128], idn_bf[:]))
                tr_done[0] += 1
                p.w(S, "tensor")
                scp(strips[:, hc * NT + t * 128: hc * NT + (t + 1) * 128],
                    PTB[bk])
        p.w(G, "scalar")
        p.admas(G, bass.AP(shx, half * SHX_REG,
                           [[NT, 128], [128 * NT, 16], [1, NT]]),
                strips.rearrange("p (hc t) -> p hc t", hc=16))
        exchange(phase)

    def load_xl(xl):
        for hh in range(2):
            for c in range(16):
                p.admas(G, xl[:, c * SEQ + hh * NT: c * SEQ + hh * NT + NT],
                        bass.AP(shx, hh * SHX_REG + c * 128 * NT,
                                [[NT, 128], [1, NT]]))

    def ph_l0_attn():
        A4.reset(); A2.reset(); AW.reset(); A1.reset()
        xl = A1.take([128, 16 * SEQ], BF16)
        wq_sb = [AW.take([128, 16 * 128], BF16) for _ in range(2)]
        wk_sb = AW.take([128, 16 * 128], BF16)
        wv_sb = AW.take([128, 16 * 128], BF16)
        k_fm = A2.take([128, SEQ], BF16)
        v_fm = A2.take([128, SEQ], BF16)
        v_tm = A2.take([128, SEQ], BF16)
        q_fm = [A2.take([128, SEQ], BF16) for _ in range(2)]
        eq_fm = [A2.take([128, SEQ], BF16) for _ in range(2)]
        ek_fm = A2.take([128, SEQ], BF16)
        ek_tm = A4.take([128, SEQ], BF16)
        fla_j = A4.take([128, 16 * 256], BF16)
        obase = A4.take([128, SEQ], BF16)
        lnk = A4.take([1, SEQ], F32)
        lnq = [A4.take([1, SEQ], F32) for _ in range(2)]
        ptile = [A4.take([128, 512], BF16) for _ in range(2)]
        e_sb = A4.take([128, 512], F32)
        rbc = A4.take([128, 512], F32)
        pow_sb = A4.take([128, 5 * 128], BF16)
        powt_sb = A4.take([128, 6 * 128], BF16)
        s_f32 = [A4.take([128, 128], F32) for _ in range(2)]
        xbuf = [A4.take([128, 128], BF16) for _ in range(2)]
        u_bf = A4.take([128, 128], BF16)
        r_bf = A4.take([128, 128], BF16)
        kt_bf = A4.take([128, 128], BF16)
        t_bf = A4.take([128, 128], BF16)
        tt_bf = A4.take([128, 128], BF16)
        pt_bf = [A4.take([128, 128], BF16) for _ in range(2)]
        s_bf = A4.take([128, 128], BF16)
        zt_sb = A4.take([128, 128], F32)
        rrow = A4.take([1, 512], F32)
        arow = A4.take([1, 128], F32)
        bk_r = A4.take([1, 128], F32)
        nck_r = A4.take([1, 128], F32)
        bq_r = [A4.take([1, 128], F32) for _ in range(2)]
        g_sb = A4.take([128, 16], F32)
        gt1 = A4.take([128, 16], F32)
        gt2 = A4.take([128, 16], F32)
        gt3 = A4.take([128, 16], F32)
        gs_sb = A4.take([128, 16], F32)
        cols_sb = A4.take([128, 6], F32)
        negb = A4.take([128, 1], F32)
        kscale = A4.take([128, 1], F32)
        bqe = [A4.take([128, 1], F32) for _ in range(2)]
        ace = A4.take([128, 1], F32)
        neg12 = A4.take([128, 1], F32)
        neg40 = A4.take([128, 1], F32)
        one_t = A4.take([128, 1], F32)
        wg_bf = A4.take([128, 1], BF16)
        bgb = A4.take([128, 1], F32)

        load_xl(xl)
        p.admas(G, wg_bf[:], ins["wg"][:, :])
        p.admas(SY, bgb[:], ins["bg"][:, :].broadcast_to([128, 1]))
        p.a(V, lambda v: v.memset(neg12[:], -QK_SHIFT))
        p.a(V, lambda v: v.memset(neg40[:], -SC_SHIFT))
        p.a(V, lambda v: v.memset(one_t[:], 1.0))

        wload = [0]

        def proj(dst_sb, w_sb, nblocks=4):
            for n in range(nblocks):
                p.wait_at(T, "dgpsimd", wload[0])
                p.w(T, "scalar")
                for c in range(16):
                    mm(PS[0][:, 0:512],
                       w_sb[:, c * 128:(c + 1) * 128],
                       xl[:, c * SEQ + n * 512: c * SEQ + n * 512 + 512],
                       start=(c == 0), stop=(c == 15))
                p.w(S, "tensor")
                scp(dst_sb[:, n * 512:(n + 1) * 512], PS[0][:, 0:512])

        wgt_use = [0]
        for j in range(4):
            p.wait_at(G, "tensor", wgt_use[0])
            p.admas(G, wk_sb[:].rearrange("p (c j) -> p c j", c=16),
                    ins["wk0"][:, j * 128:(j + 1) * 128]
                    .rearrange("(c p) j -> p c j", p=128))
            p.admas(G, wv_sb[:].rearrange("p (c j) -> p c j", c=16),
                    ins["wv0"][:, j * 128:(j + 1) * 128]
                    .rearrange("(c p) j -> p c j", p=128))
            for dh in range(2):
                p.admas(G, wq_sb[dh][:].rearrange("p (c j) -> p c j", c=16),
                        ins["wq0"][:, (2 * j + dh) * 128:(2 * j + dh + 1) * 128]
                        .rearrange("(c p) j -> p c j", p=128))
            wload[0] = p.dmark("gpsimd")
            if j == 0:
                p.w(T, "dsync")
            proj(k_fm, wk_sb)
            proj(v_fm, wv_sb)
            proj(q_fm[0], wq_sb[0])
            proj(q_fm[1], wq_sb[1])
            wgt_use[0] = p.mark("tensor")
            for c in range(16):
                p.w(T, "scalar")
                p.a(T, lambda t, c=c: t.transpose(
                    PTB[c % 2], v_fm[:, c * 128:(c + 1) * 128], idn_bf[:]))
                p.w(S, "tensor")
                scp(v_tm[:, c * 128:(c + 1) * 128], PTB[c % 2])
            p.w(S, "tensor", "vector")
            act(ek_fm[:], k_fm[:], AF.Exp, bias=neg12[:])
            act(eq_fm[0][:], q_fm[0][:], AF.Exp, bias=neg12[:])
            act(eq_fm[1][:], q_fm[1][:], AF.Exp, bias=neg12[:])
            for c in range(16):
                p.w(T, "scalar")
                p.a(T, lambda t, c=c: t.transpose(
                    PTB[c % 2], ek_fm[:, c * 128:(c + 1) * 128], idn_bf[:]))
                p.w(S, "tensor")
                scp(ek_tm[:, c * 128:(c + 1) * 128], PTB[c % 2])
            for (src, dstrow) in [(ek_fm, lnk), (eq_fm[0], lnq[0]),
                                  (eq_fm[1], lnq[1])]:
                for n in range(4):
                    p.w(T, "scalar")
                    mm(PS[2][0:1, 0:512], ones_cb[:, 0:1],
                       src[:, n * 512:(n + 1) * 512])
                    p.w(S, "tensor")
                    act(dstrow[0:1, n * 512:(n + 1) * 512],
                        PS[2][0:1, 0:512], AF.Ln)
            p.w(T, "scalar", "dgpsimd")
            for c in range(16):
                mm(PS[2][:, c:c + 1], k_fm[:, c * 128:(c + 1) * 128],
                   wg_bf[:, 0:1])
            p.w(V, "tensor", "dsync")
            p.a(V, lambda v: v.tensor_scalar(g_sb[:], PS[2][:, 0:16], bgb[:],
                                             None, ALU.add))
            p.w(S, "vector")
            act(gt1[:], g_sb[:], AF.Abs)
            p.w(S, "self")
            act(gt2[:], gt1[:], AF.Exp, scale=-1.0)
            p.w(S, "self")
            act(gt3[:], gt2[:], AF.Ln, bias=one_t[:])
            p.w(V, "scalar")
            p.a(V, lambda v: v.tensor_scalar_min(gt1[:], g_sb[:], 0.0))
            p.w(V, "self")
            p.a(V, lambda v: v.tensor_sub(gt2[:], gt1[:], gt3[:]))
            p.w(V, "self")
            p.a(V, lambda v: v.tensor_scalar_mul(gs_sb[:], gt2[:], 1.0 / 16.0))

            for dh in range(2):
                hl = 2 * j + dh
                p.w(V, "dgpsimd")
                for qb in range(4):
                    nkc = (qb + 1) * 4
                    for kc in range(nkc):
                        sb = 4 + (kc % 2)
                        p.w(T, "scalar", "vector")
                        mm(PS[sb][:, 0:512], k_fm[:, kc * 128:(kc + 1) * 128],
                           q_fm[dh][:, qb * 512:(qb + 1) * 512],
                           start=True, stop=(kc < qb * 4))
                        if kc >= qb * 4:
                            mm(PS[sb][:, 0:512], idn_bf[:],
                               bmask[kc - qb * 4][:], start=False, stop=True)
                        p.w(S, "tensor")
                        act(ptile[kc % 2][:], PS[sb][:, 0:512], AF.Exp,
                            bias=neg40[:], scale=DSCALE)
                        p.w(T, "scalar")
                        mm(PS[2][0:1, 0:512], ones_cb[:, 0:1],
                           ptile[kc % 2][:], start=(kc == 0),
                           stop=(kc == nkc - 1))
                        mm(PS[3][:, 0:512], v_tm[:, kc * 128:(kc + 1) * 128],
                           ptile[kc % 2][:], start=(kc == 0),
                           stop=(kc == nkc - 1))
                    p.w(V, "tensor")
                    p.a(V, lambda v: v.reciprocal(rrow[0:1, 0:512],
                                                  PS[2][0:1, 0:512]))
                    p.w(T, "vector")
                    mm(PS[1][:, 0:512], ones_r[0:1, :], rrow[0:1, 0:512])
                    p.w(S, "tensor")
                    scp(rbc[:], PS[1][:, 0:512])
                    p.w(V, "scalar")
                    p.a(V, lambda v, qb=qb: v.tensor_tensor(
                        obase[:, qb * 512:(qb + 1) * 512], PS[3][:, 0:512],
                        rbc[:], ALU.mult))
                p.w(G, "vector")
                p.admas(G, bass.AP(baseloc, hl * 128 * SEQ,
                                   [[SEQ, 128], [1, SEQ]]), obase[:])

            p.w(V, "dgpsimd")
            for c in range(16):
                cs = slice(c * 128, (c + 1) * 128)
                p.w(T, "vector", "scalar")
                mm(PS[5][0:1, 128:256], gs_sb[:, c:c + 1], uincl[:])
                p.w(S, "tensor")
                scp(arow[0:1, :], PS[5][0:1, 128:256])
                p.w(V, "scalar")
                p.a(V, lambda v, cs=cs: v.tensor_sub(bk_r[0:1, :],
                                                     arow[0:1, :],
                                                     lnk[0:1, cs]))
                p.a(V, lambda v, cs=cs: v.scalar_tensor_tensor(
                    nck_r[0:1, :], arow[0:1, :], -1.0, lnk[0:1, cs],
                    ALU.mult, ALU.subtract))
                p.a(V, lambda v, cs=cs: v.tensor_sub(bq_r[0][0:1, :],
                                                     arow[0:1, :],
                                                     lnq[0][0:1, cs]))
                p.a(V, lambda v, cs=cs: v.tensor_sub(bq_r[1][0:1, :],
                                                     arow[0:1, :],
                                                     lnq[1][0:1, cs]))
                p.w(T, "vector")
                mm(PS[5][:, 0:1], bk_r[0:1, :], ones_r[0:1, 0:1])
                mm(PS[5][:, 1:2], nck_r[0:1, :], ones_r[0:1, 0:1])
                mm(PS[5][:, 2:3], bq_r[0][0:1, :], ones_r[0:1, 0:1])
                mm(PS[5][:, 3:4], bq_r[1][0:1, :], ones_r[0:1, 0:1])
                mm(PS[5][:, 4:5], ones_r[0:1, :], arow[0:1, 127:128])
                p.w(S, "tensor")
                scp(cols_sb[:, 0:5], PS[5][:, 0:5])
                p.w(S, "self")
                act(negb[:], cols_sb[:, 0:1], AF.Exp)
                p.w(V, "scalar")
                p.a(V, lambda v: v.tensor_scalar_mul(negb[:], negb[:], -1.0))
                p.a(V, lambda v: v.tensor_add(cols_sb[:, 5:6],
                                              cols_sb[:, 4:5],
                                              cols_sb[:, 1:2]))
                p.w(S, "vector")
                act(kscale[:], cols_sb[:, 5:6], AF.Exp)
                act(bqe[0][:], cols_sb[:, 2:3], AF.Exp)
                act(bqe[1][:], cols_sb[:, 3:4], AF.Exp)
                act(ace[:], cols_sb[:, 4:5], AF.Exp)
                p.w(T, "scalar")
                mm(PS[0][:, 0:128], bk_r[0:1, :], ones_r[0:1, :], True, False)
                mm(PS[0][:, 0:128], ones_r[0:1, :], nck_r[0:1, :], False, False)
                mm(PS[0][:, 0:128], idn_f[:], m_sl[:], False, True)
                mm(PS[0][:, 128:256], nck_r[0:1, :], ones_r[0:1, :], True, False)
                mm(PS[0][:, 128:256], ones_r[0:1, :], bq_r[0][0:1, :], False, False)
                mm(PS[0][:, 128:256], idn_f[:], m_iu[:], False, True)
                mm(PS[0][:, 256:384], nck_r[0:1, :], ones_r[0:1, :], True, False)
                mm(PS[0][:, 256:384], ones_r[0:1, :], bq_r[1][0:1, :], False, False)
                mm(PS[0][:, 256:384], idn_f[:], m_iu[:], False, True)
                mm(PS[0][:, 384:512], nck_r[0:1, :], ones_r[0:1, :], True, False)
                mm(PS[0][:, 384:512], ones_r[0:1, :], bk_r[0:1, :], False, False)
                mm(PS[0][:, 384:512], idn_f[:], m_su[:], False, True)
                p.w(S, "tensor")
                act(e_sb[:], PS[0][:, 0:512], AF.Exp)
                p.w(T, "scalar")
                mm(PS[1][:, 0:128], ek_fm[:, cs], ek_fm[:, cs])
                mm(PS[1][:, 128:256], ek_fm[:, cs], eq_fm[0][:, cs])
                mm(PS[1][:, 256:384], ek_fm[:, cs], eq_fm[1][:, cs])
                p.w(V, "tensor", "scalar")
                p.a(V, lambda v: v.tensor_tensor(t_bf[:], e_sb[:, 0:128],
                                                 PS[1][:, 0:128], ALU.mult))
                p.a(V, lambda v: v.tensor_tensor(pt_bf[0][:],
                                                 e_sb[:, 128:256],
                                                 PS[1][:, 128:256], ALU.mult))
                p.a(V, lambda v: v.tensor_tensor(pt_bf[1][:],
                                                 e_sb[:, 256:384],
                                                 PS[1][:, 256:384], ALU.mult))
                p.a(V, lambda v: v.tensor_tensor(tt_bf[:], e_sb[:, 384:512],
                                                 PS[1][:, 0:128], ALU.mult))
                p.w(T, "vector")
                prevX, prevXt = t_bf[:], tt_bf[:]
                for lev in range(1, 6):
                    mm(PS[2][:, 0:128], prevXt, prevX)
                    mm(PS[2][:, 128:256], prevX, prevXt)
                    p.w(S, "tensor")
                    scp(pow_sb[:, (lev - 1) * 128:lev * 128], PS[2][:, 0:128])
                    scp(powt_sb[:, (lev - 1) * 128:lev * 128],
                        PS[2][:, 128:256])
                    p.w(T, "scalar")
                    prevX = pow_sb[:, (lev - 1) * 128:lev * 128]
                    prevXt = powt_sb[:, (lev - 1) * 128:lev * 128]
                mm(PS[2][:, 0:128], pow_sb[:, 4 * 128:5 * 128],
                   powt_sb[:, 4 * 128:5 * 128])
                p.w(S, "tensor")
                scp(powt_sb[:, 5 * 128:6 * 128], PS[2][:, 0:128])
                if c > 0:
                    p.w(T, "scalar", "vector")
                    mm(PS[3][:, 0:128], ek_fm[:, cs], s_bf[:])
                    p.w(V, "tensor")
                    p.a(V, lambda v, cs=cs: v.scalar_tensor_tensor(
                        r_bf[:], PS[3][:, 0:128], negb[:], v_tm[:, cs],
                        ALU.mult, ALU.add))
                cur = (v_tm[:, cs] if c == 0 else r_bf[:])
                xb = 0
                for lev in [5, 4, 3, 2, 1]:
                    p.w(T, "scalar", "vector")
                    mm(PS[3][:, 128:256],
                       powt_sb[:, lev * 128:(lev + 1) * 128], cur)
                    p.w(V, "tensor")
                    p.a(V, lambda v, xb=xb, cur=cur: v.tensor_tensor(
                        xbuf[xb][:], cur, PS[3][:, 128:256], ALU.add))
                    cur = xbuf[xb][:]
                    xb = 1 - xb
                p.w(T, "vector")
                mm(PS[3][:, 128:256], powt_sb[:, 0:128], cur)
                p.w(V, "tensor")
                p.a(V, lambda v, cur=cur: v.tensor_tensor(
                    xbuf[xb][:], cur, PS[3][:, 128:256], ALU.add))
                cur = xbuf[xb][:]
                p.w(T, "vector")
                mm(PS[3][:, 128:256], tt_bf[:], cur)
                p.w(V, "tensor")
                p.a(V, lambda v, cur=cur: v.tensor_sub(u_bf[:], cur,
                                                       PS[3][:, 128:256]))
                p.w(T, "vector")
                if c > 0:
                    mm(PS[3][:, 256:384], eq_fm[0][:, cs], s_bf[:])
                    mm(PS[3][:, 384:512], eq_fm[1][:, cs], s_bf[:])
                mm(PS[4][:, 0:128], pt_bf[0][:], u_bf[:])
                mm(PS[4][:, 128:256], pt_bf[1][:], u_bf[:])
                p.w(V, "tensor")
                for dh in range(2):
                    dst = fla_j[:, c * 256 + dh * 128: c * 256 + dh * 128 + 128]
                    if c > 0:
                        p.a(V, lambda v, dh=dh: v.tensor_scalar(
                            zt_sb[:], PS[3][:, 256 + dh * 128:384 + dh * 128],
                            bqe[dh][:], DSCALE, ALU.mult, ALU.mult))
                        p.w(V, "self")
                        p.a(V, lambda v, dh=dh, dst=dst: v.scalar_tensor_tensor(
                            dst, PS[4][:, dh * 128:(dh + 1) * 128], DSCALE,
                            zt_sb[:], ALU.mult, ALU.add))
                    else:
                        p.a(V, lambda v, dh=dh, dst=dst: v.tensor_scalar(
                            dst, PS[4][:, dh * 128:(dh + 1) * 128], DSCALE,
                            None, ALU.mult))
                if c < 15:
                    p.w(V, "scalar")
                    p.a(V, lambda v, cs=cs: v.tensor_scalar(
                        kt_bf[:], ek_tm[:, cs], kscale[:], None, ALU.mult))
                    p.w(T, "vector")
                    mm(PS[5][:, 128:256], kt_bf[:], u_bf[:])
                    p.w(V, "tensor")
                    cu = c % 2
                    if c == 0:
                        p.a(V, lambda v, cu=cu: v.tensor_copy(
                            s_f32[cu][:], PS[5][:, 128:256]))
                    else:
                        p.a(V, lambda v, cu=cu: v.scalar_tensor_tensor(
                            s_f32[cu][:], s_f32[1 - cu][:], ace[:],
                            PS[5][:, 128:256], ALU.mult, ALU.add))
                    p.w(S, "vector")
                    scp(s_bf[:], s_f32[cu][:])
            p.w(G, "vector")
            p.admas(G, bass.AP(flaloc, j * 256,
                               [[NT, 128], [128 * NT, 16], [1, 256]]),
                    fla_j.rearrange("p (c t) -> p c t", c=16))
        p.w(G, "dgpsimd")
        p.admas(G, bass.AP(sha, FLA0 + half * SHX_REG + 0,
                           [[32768, 64], [1, 32768]]),
                bass.AP(flaloc, 0, [[32768, 64], [1, 32768]]))
        p.admas(G, bass.AP(sha, BASE0 + half * (NT * SEQ) + 0,
                           [[32768, 64], [1, 32768]]),
                bass.AP(baseloc, 0, [[32768, 64], [1, 32768]]))
        exchange(2)

    def ph_mix(mode, xsrc, xdst, wo_name):
        A1.reset(); A2.reset(); A4.reset()
        wo_sb = A1.take([128, 16 * HID], BF16)
        bs_sb = A2.take([128, 16 * NT], BF16)
        fla_t = [A4.take([128, HID], BF16) for _ in range(2)]
        mixed = A4.take([128, HID], F32)
        lnout = A4.take([128, HID], F32)
        xo = A4.take([128, HID], F32)
        fs = A4.take([128, 512], F32)
        bo_bc = A4.take([128, HID], F32) if mode == "l1" else None

        for c4 in range(4):
            p.admas(G, wo_sb[:].rearrange("p (c j) -> p c j", c=16)
                    [:, 4 * c4:4 * c4 + 4, :],
                    ins[wo_name][:, :].rearrange("(c p) j -> p c j", p=128)
                    [:, 4 * c4:4 * c4 + 4, :])
        for hh in range(2):
            p.admas(G, bs_sb[:, hh * 8 * NT:(hh + 1) * 8 * NT]
                    .rearrange("p (cb t) -> p cb t", cb=8),
                    bass.AP(sha, BASE0 + hh * (NT * SEQ) + half * NT,
                            [[SEQ, 128], [128 * SEQ, 8], [1, NT]]))
        if mode == "l0":
            for hh in range(2):
                p.admas(G, bass.AP(flaloc, hh * NT, [[SEQ, NT], [1, NT]]),
                        bass.AP(sha, FLA0 + hh * SHX_REG + half * (NT * NT),
                                [[NT, NT], [1, NT]]))
            p.w(G, "dgpsimd")
        if mode == "l1":
            p.admas(SY, bo_bc[:], ins["bo"][:, :].broadcast_to([128, HID]))
        wbs_load = p.dmark("gpsimd")
        ps_use = [0, 0]
        fla_use = [0, 0]

        for t in range(8):
            bi = t % 2
            p.w(SY, "vector", "scalar")
            p.admas(SY, xrt[bi][:], xsrc[t * 128:(t + 1) * 128, :])
            if mode == "l0":
                p.wait_at(G, "vector", fla_use[bi])
                p.admas(G, fla_t[bi][:],
                        bass.AP(flaloc, t * 128 * SEQ, [[SEQ, 128], [1, SEQ]]))
            fla_mark = p.dmark("gpsimd")
            p.w(V, "scalar")
            for n in range(4):
                bank = n % 2
                p.wait_at(T, "vector", ps_use[bank])
                if t == 0 and n == 0:
                    p.wait_at(T, "dgpsimd", wbs_load)
                for c in range(16):
                    mm_l = bs_sb[:, c * NT + t * 128: c * NT + (t + 1) * 128]
                    mm_r = wo_sb[:, c * HID + n * 512: c * HID + n * 512 + 512]
                    p.a(T, lambda tt, o=PS[bank][:, 0:512], a=mm_l, b=mm_r,
                        st=(c == 0), sp=(c == 15): tt.matmul(o, a, b,
                                                             start=st, stop=sp))
                p.w(V, "tensor", "dsync")
                p.wait_at(V, "dgpsimd", fla_mark)
                if mode == "l0":
                    p.a(V, lambda v, bi=bi, n=n: v.tensor_scalar(
                        fs[:], fla_t[bi][:, n * 512:(n + 1) * 512], 0.01,
                        None, ALU.mult))
                    p.w(V, "self")
                    p.a(V, lambda v, n=n, bank=bank: v.scalar_tensor_tensor(
                        mixed[:, n * 512:(n + 1) * 512], PS[bank][:, 0:512],
                        0.99, fs[:], ALU.mult, ALU.add))
                else:
                    p.a(V, lambda v, n=n, bank=bank: v.tensor_tensor(
                        mixed[:, n * 512:(n + 1) * 512], PS[bank][:, 0:512],
                        bo_bc[:, n * 512:(n + 1) * 512], ALU.add))
                ps_use[bank] = p.mark("vector")
            fla_use[bi] = p.mark("vector")
            p.w(V, "self")
            p.w(S, "vector")
            emit_ln(p, mixed[:], lnout[:], HID, tmp,
                    p.auto["vector"], p.auto["scalar"])
            p.w(V, "self", "dsync")
            p.a(V, lambda v, bi=bi: v.tensor_add(xo[:], xrt[bi][:],
                                                 lnout[:]))
            p.w(SY, "vector")
            p.admas(SY, xdst[t * 128:(t + 1) * 128, :], xo[:])

    def ph_moe():
        A1.reset(); A2.reset(); A4.reset(); AW.reset()
        acc = A1.take([128, 8 * HID], F32)
        xlnt = A2.take([128, 16 * NT], BF16)
        gh = A4.take([128, 44 * 512], BF16)
        xlnf = A4.take([128, HID], F32)
        stmp = A4.take([128, 512], BF16)
        ytmp = A4.take([128, 512], F32)
        comb = A4.take([128, 8 * 16], F32)
        probs = A4.take([128, 16], F32)
        wrk = A4.take([128, 16], F32)
        eqt = A4.take([128, 16], F32)
        mx = A4.take([128, 4], F32)
        sig_sb = A4.take([128, 8], F32)
        s4 = A4.take([128, 1], F32)
        rs4 = A4.take([128, 1], F32)
        negm = A4.take([128, 1], F32)
        lnout = xlnf
        wring = [AW.take([128, 16 * 128], BF16) for _ in range(2)]
        dring = [AW.take([128, 512], BF16) for _ in range(2)]
        gwt_sb = AW.take([128, 16 * 16], BF16)
        sgv_sb = AW.take([128, 16], BF16)

        p.admas(G, gwt_sb[:].rearrange("p (c j) -> p c j", c=16),
                ins["gw_t"][:, :].rearrange("(c p) j -> p c j", p=128))
        p.admas(G, sgv_sb[:], ins["sgv_l"][:, :])

        for t in range(8):
            bi = t % 2
            p.w(SY, "vector", "scalar")
            p.admas(SY, xrt[bi][:], xres1[t * 128:(t + 1) * 128, :])
            p.w(V, "dsync")
            p.w(S, "dsync")
            p.w(V, "tensor")
            emit_ln(p, xrt[bi][:], xlnf[:], HID, tmp,
                    p.auto["vector"], p.auto["scalar"])
            p.w(T, "vector", "dsync")
            for c in range(16):
                cb = c % 2
                p.w(T, "scalar")
                p.a(T, lambda tt, c=c, cb=cb: tt.transpose(
                    PS[2][:, cb * 128:cb * 128 + 128],
                    xlnf[:, c * 128:(c + 1) * 128], idn_f[:]))
                p.w(S, "tensor")
                scp(xlnt[:, c * NT + t * 128: c * NT + (t + 1) * 128],
                    PS[2][:, cb * 128:cb * 128 + 128])
            p.w(T, "scalar", "dgpsimd")
            for c in range(16):
                mm(PS[0][:, 0:16], xlnt[:, c * NT + t * 128: c * NT + (t + 1) * 128],
                   gwt_sb[:, c * 16:(c + 1) * 16], start=(c == 0),
                   stop=(c == 15))
            p.w(V, "tensor")
            p.a(V, lambda v: v.reduce_max(negm[:], PS[0][:, 0:16], axis=AX.X,
                                          negate=True))
            p.w(S, "vector")
            act(probs[:], PS[0][:, 0:16], AF.Exp, bias=negm[:])
            p.w(V, "scalar")
            p.a(V, lambda v: v.tensor_copy(wrk[:], probs[:]))
            for jj in range(4):
                p.w(V, "self")
                p.a(V, lambda v, jj=jj: v.reduce_max(mx[:, jj:jj + 1], wrk[:],
                                                     axis=AX.X))
                if jj < 3:
                    p.w(V, "self")
                    p.a(V, lambda v, jj=jj: v.tensor_scalar(
                        eqt[:], wrk[:], mx[:, jj:jj + 1], None, ALU.is_ge))
                    p.w(V, "self")
                    p.a(V, lambda v: v.scalar_tensor_tensor(
                        wrk[:], eqt[:], -1e30, wrk[:], ALU.mult, ALU.add))
            p.w(V, "self")
            p.a(V, lambda v: v.tensor_scalar(eqt[:], probs[:], mx[:, 3:4],
                                             None, ALU.is_ge))
            p.w(V, "self")
            p.a(V, lambda v: v.tensor_mul(wrk[:], probs[:], eqt[:]))
            p.w(V, "self")
            p.a(V, lambda v: v.reduce_sum(s4[:], wrk[:], axis=AX.X))
            p.w(V, "self")
            p.a(V, lambda v: v.reciprocal(rs4[:], s4[:]))
            p.w(V, "self")
            p.a(V, lambda v, t=t: v.tensor_scalar(
                comb[:, t * 16:(t + 1) * 16], wrk[:], rs4[:], None, ALU.mult))
            p.w(T, "vector")
            for c in range(16):
                mm(PS[1][:, 0:1], xlnt[:, c * NT + t * 128: c * NT + (t + 1) * 128],
                   sgv_sb[:, c:c + 1], start=(c == 0), stop=(c == 15))
            p.w(S, "tensor")
            act(sig_sb[:, t:t + 1], PS[1][:, 0:1], AF.Sigmoid)

        for blk in range(2):
            tok0 = blk * 512
            for m in range(44):
                wr = wring[m % 2]
                p.w(G, "tensor")
                p.admas(G, wr[:].rearrange("p (c j) -> p c j", c=16),
                        ins["sgu_t"][:, m * 128:(m + 1) * 128]
                        .rearrange("(c p) j -> p c j", p=128))
                p.w(T, "dgpsimd", "scalar")
                for c in range(16):
                    mm(PS[0][:, 0:512], wr[:, c * 128:(c + 1) * 128],
                       xlnt[:, c * NT + tok0: c * NT + tok0 + 512],
                       start=(c == 0), stop=(c == 15))
                p.w(S, "tensor")
                scp(gh[:, m * 512:(m + 1) * 512], PS[0][:, 0:512])
            for m in range(44):
                wr = wring[m % 2]
                p.w(G, "tensor")
                p.admas(G, wr[:].rearrange("p (c j) -> p c j", c=16),
                        ins["sgu_t"][:, (44 + m) * 128:(45 + m) * 128]
                        .rearrange("(c p) j -> p c j", p=128))
                p.w(T, "dgpsimd", "vector")
                for c in range(16):
                    mm(PS[0][:, 0:512], wr[:, c * 128:(c + 1) * 128],
                       xlnt[:, c * NT + tok0: c * NT + tok0 + 512],
                       start=(c == 0), stop=(c == 15))
                p.w(S, "tensor", "vector")
                act(stmp[:], gh[:, m * 512:(m + 1) * 512], AF.Silu)
                p.w(V, "scalar", "tensor")
                p.a(V, lambda v, m=m: v.tensor_tensor(
                    gh[:, m * 512:(m + 1) * 512], stmp[:], PS[0][:, 0:512],
                    ALU.mult))
            for n in range(4):
                p.w(T, "scalar", "vector")
                for k in range(44):
                    dr = dring[k % 2]
                    p.w(G, "tensor")
                    p.admas(G, dr[:], ins["sdn_t"][k * 128:(k + 1) * 128,
                                                   n * 512:(n + 1) * 512])
                    p.w(T, "dgpsimd")
                    for ts in range(4):
                        mm(PS[4 + ts][:, 0:512],
                           gh[:, k * 512 + ts * 128: k * 512 + (ts + 1) * 128],
                           dr[:], start=(k == 0), stop=(k == 43))
                p.w(S, "tensor")
                for ts in range(4):
                    gts = blk * 4 + ts
                    act(acc[:, gts * HID + n * 512: gts * HID + n * 512 + 512],
                        PS[4 + ts][:, 0:512], AF.Copy,
                        scale=sig_sb[:, gts:gts + 1])

        for e in range(16):
            for m in range(11):
                wr = wring[m % 2]
                p.w(G, "tensor")
                p.admas(G, wr[:].rearrange("p (c j) -> p c j", c=16),
                        ins["egu_t"][e, :, m * 128:(m + 1) * 128]
                        .rearrange("(c p) j -> p c j", p=128))
                p.w(T, "dgpsimd", "scalar")
                for blk in range(2):
                    for c in range(16):
                        mm(PS[blk][:, 0:512], wr[:, c * 128:(c + 1) * 128],
                           xlnt[:, c * NT + blk * 512: c * NT + blk * 512 + 512],
                           start=(c == 0), stop=(c == 15))
                p.w(S, "tensor")
                scp(gh[:, m * NT: m * NT + 512], PS[0][:, 0:512])
                scp(gh[:, m * NT + 512: (m + 1) * NT], PS[1][:, 0:512])
            for m in range(11):
                wr = wring[m % 2]
                p.w(G, "tensor")
                p.admas(G, wr[:].rearrange("p (c j) -> p c j", c=16),
                        ins["egu_t"][e, :, (11 + m) * 128:(12 + m) * 128]
                        .rearrange("(c p) j -> p c j", p=128))
                p.w(T, "dgpsimd", "vector")
                for blk in range(2):
                    for c in range(16):
                        mm(PS[blk][:, 0:512], wr[:, c * 128:(c + 1) * 128],
                           xlnt[:, c * NT + blk * 512: c * NT + blk * 512 + 512],
                           start=(c == 0), stop=(c == 15))
                for blk in range(2):
                    p.w(S, "tensor", "vector")
                    act(stmp[:], gh[:, m * NT + blk * 512: m * NT + blk * 512 + 512],
                        AF.Silu)
                    p.w(V, "scalar")
                    p.a(V, lambda v, m=m, blk=blk: v.tensor_tensor(
                        gh[:, m * NT + blk * 512: m * NT + blk * 512 + 512],
                        stmp[:], PS[blk][:, 0:512], ALU.mult))
            for n in range(4):
                p.w(T, "scalar", "vector")
                for k in range(11):
                    dr = dring[k % 2]
                    p.w(G, "tensor")
                    p.admas(G, dr[:], ins["edn_t"][e, k * 128:(k + 1) * 128,
                                                   n * 512:(n + 1) * 512])
                    p.w(T, "dgpsimd")
                    for ts in range(8):
                        mm(PS[ts][:, 0:512],
                           gh[:, k * NT + ts * 128: k * NT + (ts + 1) * 128],
                           dr[:], start=(k == 0), stop=(k == 10))
                p.w(S, "tensor", "vector")
                for ts in range(8):
                    act(ytmp[:], PS[ts][:, 0:512], AF.Copy,
                        scale=comb[:, ts * 16 + e: ts * 16 + e + 1])
                    p.w(V, "scalar")
                    dst = acc[:, ts * HID + n * 512: ts * HID + n * 512 + 512]
                    p.a(V, lambda v, dst=dst: v.tensor_tensor(dst, dst,
                                                              ytmp[:], ALU.add))
                    p.w(S, "vector")
                p.w(V, "self")

        p.join()
        A4.reset()
        lnout4 = A4.take([128, HID], F32)
        xo4 = A4.take([128, HID], F32)
        for t in range(8):
            bi = t % 2
            p.w(SY, "vector", "scalar")
            p.admas(SY, xrt[bi][:], xres1[t * 128:(t + 1) * 128, :])
            p.w(V, "dsync", "self")
            p.w(S, "dsync")
            emit_ln(p, acc[:, t * HID:(t + 1) * HID], lnout4[:], HID, tmp,
                    p.auto["vector"], p.auto["scalar"])
            p.w(V, "self", "dsync")
            p.a(V, lambda v, bi=bi: v.tensor_add(xo4[:], xrt[bi][:],
                                                 lnout4[:]))
            p.w(SY, "vector")
            p.admas(SY, xres2[t * 128:(t + 1) * 128, :], xo4[:])

    def ph_l1_attn():
        A1.reset(); A2.reset(); A4.reset(); AW.reset()
        xl = A1.take([128, 16 * SEQ], BF16)
        w_sb = [AW.take([128, 16 * 128], BF16) for _ in range(3)]
        q_fm = A4.take([128, SEQ], BF16)
        k_fm = A4.take([128, SEQ], BF16)
        v_fm = A4.take([128, SEQ], BF16)
        v_tm = A4.take([128, SEQ], BF16)
        qk_el = A4.take([128, SEQ], BF16)
        obase = A4.take([128, SEQ], BF16)
        r_bf = A4.take([1, SEQ], BF16)
        nbd = A4.take([1, SEQ], F32)
        rrow_f = A4.take([1, 512], F32)
        rr = A4.take([1, 512], F32)
        rbc = A4.take([128, 512], F32)
        ptile = [A4.take([128, 512], BF16) for _ in range(2)]
        alibi = A4.take([128, 8 * 16], F32)
        bqc = A4.take([128, 8], F32)
        bkc = A4.take([128, 8], F32)
        bvc = A4.take([128, 8], F32)

        load_xl(xl)
        for nm, dst in [("bq_l", bqc), ("bk_l", bkc), ("bv_l", bvc),
                        ("alibi_c", alibi)]:
            p.admas(SY, dst[:], ins[nm][:, :])

        wgt_use = [0]
        wload = [0]
        for hl in range(8):
            p.wait_at(G, "tensor", wgt_use[0])
            for wi, wn in enumerate(["wq1", "wk1", "wv1"]):
                p.admas(G, w_sb[wi][:].rearrange("p (c j) -> p c j", c=16),
                        ins[wn][:, hl * 128:(hl + 1) * 128]
                        .rearrange("(c p) j -> p c j", p=128))
            wload[0] = p.dmark("gpsimd")
            p.w(SY, "vector")
            p.admas(SY, nbd[0:1, :], ins["negbd"][hl:hl + 1, :])
            if hl == 0:
                p.w(T, "dsync")
            for wi, dst, bc in [(0, q_fm, bqc), (1, k_fm, bkc), (2, v_fm, bvc)]:
                for n in range(4):
                    p.wait_at(T, "dgpsimd", wload[0])
                    p.w(T, "scalar", "vector")
                    for c in range(16):
                        mm(PS[0][:, 0:512], w_sb[wi][:, c * 128:(c + 1) * 128],
                           xl[:, c * SEQ + n * 512: c * SEQ + n * 512 + 512],
                           start=(c == 0), stop=(c == 15))
                    p.w(S, "tensor")
                    p.a(S, lambda s, dsl=dst[:, n * 512:(n + 1) * 512],
                        bcc=bc[:, hl:hl + 1]: s.add(dsl, PS[0][:, 0:512], bcc))
            wgt_use[0] = p.mark("tensor")
            for c in range(16):
                p.w(T, "scalar")
                p.a(T, lambda t, c=c: t.transpose(
                    PTB[c % 2], v_fm[:, c * 128:(c + 1) * 128], idn_bf[:]))
                p.w(S, "tensor")
                p.a(S, lambda s, c=c: s.copy(v_tm[:, c * 128:(c + 1) * 128],
                                             PTB[c % 2]))
            p.w(V, "scalar")
            p.a(V, lambda v: v.tensor_mul(qk_el[:], q_fm[:], k_fm[:]))
            for n in range(4):
                p.w(T, "vector", "scalar")
                mm(PS[2][0:1, 0:512], ones_cb[:, 0:1],
                   qk_el[:, n * 512:(n + 1) * 512])
                p.w(V, "tensor", "dsync")
                p.a(V, lambda v, n=n: v.scalar_tensor_tensor(
                    rrow_f[0:1, :], PS[2][0:1, 0:512], -1.0,
                    nbd[0:1, n * 512:(n + 1) * 512], ALU.mult, ALU.add))
                p.w(S, "vector")
                p.a(S, lambda s, n=n: s.copy(r_bf[0:1, n * 512:(n + 1) * 512],
                                             rrow_f[0:1, :]))
            p.w(V, "dgpsimd")
            for qb in range(4):
                nkc = (qb + 1) * 4
                for kc in range(nkc):
                    sb = 4 + (kc % 2)
                    p.w(T, "scalar", "vector")
                    mm(PS[sb][:, 0:512], k_fm[:, kc * 128:(kc + 1) * 128],
                       q_fm[:, qb * 512:(qb + 1) * 512], start=True, stop=False)
                    mm(PS[sb][:, 0:512], ones_rb[0:1, :],
                       r_bf[0:1, qb * 512:(qb + 1) * 512],
                       start=False, stop=(kc < qb * 4))
                    if kc >= qb * 4:
                        mm(PS[sb][:, 0:512], idn_bf[:], bmask[kc - qb * 4][:],
                           start=False, stop=True)
                    p.w(S, "tensor")
                    act(ptile[kc % 2][:], PS[sb][:, 0:512], AF.Exp,
                        bias=alibi[:, hl * 16 + kc: hl * 16 + kc + 1],
                        scale=DSCALE)
                    p.w(T, "scalar")
                    mm(PS[2][0:1, 0:512], ones_cb[:, 0:1], ptile[kc % 2][:],
                       start=(kc == 0), stop=(kc == nkc - 1))
                    mm(PS[3][:, 0:512], v_tm[:, kc * 128:(kc + 1) * 128],
                       ptile[kc % 2][:], start=(kc == 0), stop=(kc == nkc - 1))
                p.w(V, "tensor")
                p.a(V, lambda v: v.reciprocal(rr[0:1, 0:512],
                                              PS[2][0:1, 0:512]))
                p.w(T, "vector")
                mm(PS[1][:, 0:512], ones_r[0:1, :], rr[0:1, 0:512])
                p.w(S, "tensor")
                p.a(S, lambda s: s.copy(rbc[:], PS[1][:, 0:512]))
                p.w(V, "scalar")
                p.a(V, lambda v, qb=qb: v.tensor_tensor(
                    obase[:, qb * 512:(qb + 1) * 512], PS[3][:, 0:512],
                    rbc[:], ALU.mult))
            p.w(G, "vector")
            p.admas(G, bass.AP(baseloc, hl * 128 * SEQ,
                               [[SEQ, 128], [1, SEQ]]), obase[:])
        p.w(G, "dgpsimd")
        p.admas(G, bass.AP(sha, BASE0 + half * (NT * SEQ) + 0,
                           [[32768, 64], [1, 32768]]),
                bass.AP(baseloc, 0, [[32768, 64], [1, 32768]]))
        exchange(4)

    def ph_l1ffn():
        A1.reset(); A2.reset(); A4.reset()
        w_sb = A1.take([128, 16 * HID], BF16)
        sc_bc = A2.take([128, HID], F32)
        bi_bc = A2.take([128, HID], F32)
        xlnt = A2.take([128, 16 * 128], BF16)
        xlnf = A4.take([128, HID], F32)
        mixed = A4.take([128, HID], F32)
        lnout = A4.take([128, HID], F32)
        xo = A4.take([128, HID], F32)
        y1 = A4.take([128, 512], F32)

        p.admas(SY, sc_bc[:], ins["scale_r"][:, :].broadcast_to([128, HID]))
        p.admas(SY, bi_bc[:], ins["bias_r"][:, :].broadcast_to([128, HID]))
        for c4 in range(4):
            p.admas(G, w_sb[:].rearrange("p (c j) -> p c j", c=16)
                    [:, 4 * c4:4 * c4 + 4, :],
                    ins["w1t"][:, :].rearrange("(c p) j -> p c j", p=128)
                    [:, 4 * c4:4 * c4 + 4, :])
        w_load = p.dmark("gpsimd")
        ps_use = [0, 0]

        for t in range(8):
            bi = t % 2
            p.w(SY, "vector", "scalar")
            p.admas(SY, xrt[bi][:], xres3[t * 128:(t + 1) * 128, :])
            p.w(V, "dsync", "tensor")
            p.w(S, "dsync")
            emit_ln(p, xrt[bi][:], xlnf[:], HID, tmp,
                    p.auto["vector"], p.auto["scalar"])
            p.w(T, "vector")
            for c in range(16):
                cb = c % 2
                p.w(T, "scalar")
                p.a(T, lambda tt, c=c, cb=cb: tt.transpose(
                    PS[2][:, cb * 128:cb * 128 + 128],
                    xlnf[:, c * 128:(c + 1) * 128], idn_f[:]))
                p.w(S, "tensor")
                p.a(S, lambda s, c=c, cb=cb: s.copy(
                    xlnt[:, c * 128:(c + 1) * 128],
                    PS[2][:, cb * 128:cb * 128 + 128]))
            for n in range(4):
                bank = n % 2
                if n == 0:
                    p.w(T, "scalar")
                    if t == 0:
                        p.wait_at(T, "dgpsimd", w_load)
                p.wait_at(T, "vector", ps_use[bank])
                for c in range(16):
                    mm(PS[bank][:, 0:512], xlnt[:, c * 128:(c + 1) * 128],
                       w_sb[:, c * HID + n * 512: c * HID + n * 512 + 512],
                       start=(c == 0), stop=(c == 15))
                p.w(V, "tensor", "dsync")
                p.a(V, lambda v, n=n, bank=bank: v.tensor_tensor(
                    y1[:], PS[bank][:, 0:512], sc_bc[:, n * 512:(n + 1) * 512],
                    ALU.mult))
                p.w(V, "self")
                p.a(V, lambda v, n=n: v.tensor_tensor(
                    mixed[:, n * 512:(n + 1) * 512], y1[:],
                    bi_bc[:, n * 512:(n + 1) * 512], ALU.add))
                ps_use[bank] = p.mark("vector")
            p.w(V, "self")
            p.w(S, "vector")
            emit_ln(p, mixed[:], lnout[:], HID, tmp,
                    p.auto["vector"], p.auto["scalar"])
            p.w(V, "self", "dsync")
            p.a(V, lambda v, bi=bi: v.tensor_add(xo[:], xrt[bi][:],
                                                 lnout[:]))
            p.w(SY, "vector")
            p.admas(SY, xout[t * 128:(t + 1) * 128, :], xo[:])
        p.wait(SY, p.adma[SY], p.adma[SY].n)
        p.wait(G, p.adma[G], p.adma[G].n)

    ph_lntr(ins["xin"], 1)
    ph_l0_attn()
    p.join()
    ph_mix("l0", ins["xin"], xres1, "wo0")
    p.join()
    ph_moe()
    p.join()
    ph_lntr(xres2, 3)
    ph_l1_attn()
    p.join()
    ph_mix("l1", xres2, xres3, "wo1")
    p.join()
    ph_l1ffn()
    if DEBUG_TAPS:
        p.w(G, "dsync")
        for nm, src in [("dbg_x1", xres1), ("dbg_x2", xres2),
                        ("dbg_x3", xres3)]:
            d = nc.dram_tensor(nm, [NT, HID], F32, kind="ExternalOutput")
            p.admas(G, d[:, :], src[:, :])
        dshx = nc.dram_tensor("dbg_shx", [2 * SHX_REG], BF16,
                              kind="ExternalOutput")
        p.admas(G, bass.AP(dshx, 0, [[32768, 128], [1, 32768]]),
                bass.AP(shx, 0, [[32768, 128], [1, 32768]]))
        dsha = nc.dram_tensor("dbg_sha", [SHA_SIZE], BF16,
                              kind="ExternalOutput")
        p.admas(G, bass.AP(dsha, 0, [[32768, 256], [1, 32768]]),
                bass.AP(sha, 0, [[32768, 256], [1, 32768]]))
        p.wait(G, p.adma[G], p.adma[G].n)
    p.finalize()
    return nc



def make_maps(kw):
    C = host_constants()
    f32 = np.float32
    g = {k: np.ascontiguousarray(np.asarray(v, dtype=f32)) for k, v in kw.items()}
    X = g["L0_attn_hidden_states"]
    H = 16
    slopes = (2.0 ** (-8.0 / H)) ** np.arange(1, H + 1)
    dist = np.arange(2048, dtype=np.float64) - 2047.0
    egu_t = np.ascontiguousarray(g["L0_ffn_expert_gate_up"].transpose(0, 2, 1))
    edn_t = np.ascontiguousarray(g["L0_ffn_expert_down"].transpose(0, 2, 1))
    sgu_t = np.ascontiguousarray(g["L0_ffn_shared_gate_up"].T)
    sdn_t = np.ascontiguousarray(g["L0_ffn_shared_down"].T)
    sgv_l = np.ascontiguousarray(g["L0_ffn_shared_gate_vec"][0].reshape(16, 128).T)
    gw_t = np.ascontiguousarray(g["L0_ffn_gate_w"].T)
    w1t = np.ascontiguousarray(g["L1_ffn_weight"].T)
    nonce = np.array([[np.random.randint(1, 2 ** 28)]], dtype=np.int32)

    maps = []
    for c in range(8):
        b, h = c // 2, c % 2
        hsl = slice(h * 1024, (h + 1) * 1024)
        alibi_c = np.zeros((128, 8 * 16), np.float32)
        negbd = np.zeros((8, 2048), np.float32)
        for hh in range(8):
            gh_ = h * 8 + hh
            ac = slopes[gh_] * dist
            alibi_c[:, hh * 16:(hh + 1) * 16] = ac.reshape(16, 128).T
            negbd[hh] = -(slopes[gh_] * dist) / DSCALE
        m = {
            "xin": np.ascontiguousarray(X[b, hsl]),
            "nonce": nonce,
            "wq0": np.ascontiguousarray(g["L0_attn_Wq"][:, hsl]),
            "wk0": np.ascontiguousarray(g["L0_attn_Wk"][:, h * 512:(h + 1) * 512]),
            "wv0": np.ascontiguousarray(g["L0_attn_Wv"][:, h * 512:(h + 1) * 512]),
            "wg": g["L0_attn_Wg"],
            "bg": g["L0_attn_bg"].reshape(1, 1),
            "wo0": g["L0_attn_Wo"],
            "gw_t": gw_t, "egu_t": egu_t, "edn_t": edn_t, "sgu_t": sgu_t,
            "sdn_t": sdn_t, "sgv_l": sgv_l,
            "wq1": np.ascontiguousarray(g["L1_attn_Wq"][:, hsl]),
            "wk1": np.ascontiguousarray(g["L1_attn_Wk"][:, hsl]),
            "wv1": np.ascontiguousarray(g["L1_attn_Wv"][:, hsl]),
            "bq_l": np.ascontiguousarray(g["L1_attn_bq"][hsl].reshape(8, 128).T),
            "bk_l": np.ascontiguousarray(g["L1_attn_bk"][hsl].reshape(8, 128).T),
            "bv_l": np.ascontiguousarray(g["L1_attn_bv"][hsl].reshape(8, 128).T),
            "alibi_c": alibi_c, "negbd": negbd,
            "wo1": g["L1_attn_Wo"],
            "bo": g["L1_attn_bo"].reshape(1, -1),
            "w1t": w1t,
            "scale_r": g["L1_ffn_scale_weight"].reshape(1, -1),
            "bias_r": g["L1_ffn_bias"].reshape(1, -1),
        }
        for n in ["c_idn", "c_idn_f32", "c_uincl", "c_ones_row",
                  "c_ones_row_bf", "c_ones_col_bf", "c_m_sl", "c_m_su",
                  "c_m_iu", "c_bmask0", "c_bmask1", "c_bmask2", "c_bmask3"]:
            m[n] = C[n]
        maps.append(m)
    return maps


def kernel(**kw):
    maps = make_maps(kw)
    nc = bass.Bass("TRN2", target_bir_lowering=False)
    build_fused(nc)
    res = run_bass_kernel_spmd(nc, maps, core_ids=list(range(8)))
    out = np.stack([
        np.concatenate([res.results[2 * b]["xout"],
                        res.results[2 * b + 1]["xout"]], axis=0)
        for b in range(4)])
    return out.astype(np.float32)


# version 11
# speedup vs baseline: 1.2473x; 1.2473x over previous
import contextlib
import numpy as np
import ml_dtypes
import concourse.bass as bass
import concourse.mybir as mybir
from concourse.bass_utils import run_bass_kernel_spmd

F32 = mybir.dt.float32
BF16 = mybir.dt.bfloat16
I32 = mybir.dt.int32
AF = mybir.ActivationFunctionType
ALU = mybir.AluOpType
AX = mybir.AxisListType

ENGINES = ("tensor", "vector", "scalar", "gpsimd", "sync")

DEBUG_TAPS = False

LN_EPS = 1e-5
QK_SHIFT = 12.0
SC_SHIFT = 40.0
DSCALE = float(1.0 / np.sqrt(128.0))

SEQ, HID, NT = 2048, 2048, 1024
SHX_REG = 2048 * 1024
FLA0 = 0
BASE0 = 2 * SHX_REG
SHA_SIZE = 4 * SHX_REG


class Sem:
    def __init__(self, handle):
        self.h = handle
        self.n = 0

    def bump(self, k):
        self.n += k
        return self.n


class Prog:
    def __init__(self, nc):
        self.nc = nc
        self.ops = {e: [] for e in ENGINES}
        self.es = contextlib.ExitStack()
        self._semid = 0

    def sem(self, name=None):
        self._semid += 1
        h = self.es.enter_context(self.nc.semaphore(name or f"s{self._semid}"))
        return Sem(h)

    def sbuf(self, name, shape, dtype):
        return self.es.enter_context(self.nc.sbuf_tensor(name, shape, dtype))

    def psum(self, name, shape, dtype=F32):
        return self.es.enter_context(self.nc.psum_tensor(name, shape, dtype))

    def reg(self, name):
        return self.es.enter_context(self.nc.gpsimd.register(name))

    def wait(self, eng, sem, val=None):
        v = sem.n if val is None else val
        if v <= 0:
            return
        self.ops[eng].append(lambda e, h=sem.h, v=v: e.wait_ge(h, v))

    def emit(self, eng, fn, inc=None, inc_k=1):
        if inc is not None:
            inc.bump(inc_k)

            def wrapped(e, fn=fn, h=inc.h, k=inc_k):
                ins = fn(e)
                ins.then_inc(h, k)
            self.ops[eng].append(wrapped)
        else:
            self.ops[eng].append(fn)

    def mk_auto(self):
        self.auto = {e: self.sem(f"auto_{e}") for e in
                     ("tensor", "vector", "scalar", "gpsimd")}
        self.adma = {e: self.sem(f"adma_{e}") for e in ("sync", "gpsimd",
                                                        "scalar")}
        self.xch = self.sem("xch")

    def a(self, eng, fn):
        self.emit(eng, fn, inc=self.auto[eng])

    def admas(self, eng, out_ap, in_ap):
        self.emit(eng, lambda e, o=out_ap, i=in_ap: e.dma_start(o, i),
                  inc=self.adma[eng], inc_k=16)

    def w(self, eng, *srcs):
        for s in srcs:
            if s == "self":
                s = eng
            if s == "xch":
                self.wait(eng, self.xch)
                continue
            if s.startswith("d"):
                sem = self.adma[s[1:]]
            else:
                sem = self.auto[s]
            self.wait(eng, sem)

    def mark(self, eng):
        return self.auto[eng].n

    def dmark(self, eng):
        return self.adma[eng].n

    def wait_at(self, eng, kind, val):
        sem = self.adma[kind[1:]] if kind.startswith("d") else self.auto[kind]
        self.wait(eng, sem, val)

    def join(self):
        sems = list(self.auto.values()) + list(self.adma.values()) + [self.xch]
        for e in ENGINES:
            for s in sems:
                self.wait(e, s)

    def finalize(self):
        with self.nc.Block() as block:
            for eng in ENGINES:
                ops = self.ops[eng]
                if not ops:
                    continue
                getattr(block, eng)(lambda h, ops=ops: [f(h) for f in ops])
        self.es.close()


class Arena:

    def __init__(self, p, name, kb):
        self.t = p.sbuf(name, [128, kb * 256], F32)
        self.kb = kb
        self.off = 0

    def reset(self):
        self.off = 0

    def take(self, shape, dtype):
        esz = 4 if dtype == F32 else (2 if dtype == BF16 else 4)
        nb = shape[1] * esz
        off = (self.off + 63) & ~63
        self.off = off + nb
        assert self.off <= self.kb * 1024, \
            f"arena overflow: {self.off} > {self.kb * 1024}"
        sl = self.t[0:shape[0], off // 4: off // 4 + ((nb + 3) // 4)]
        if dtype == F32:
            return sl[:, 0:shape[1]]
        return sl.bitcast(dtype)[:, 0:shape[1]]


def host_constants():
    c = {}
    c["c_idn"] = np.eye(128, dtype=ml_dtypes.bfloat16)
    c["c_idn_f32"] = np.eye(128, dtype=np.float32)
    c["c_uincl"] = np.triu(np.ones((128, 128), np.float32))
    c["c_ones_row"] = np.ones((1, 128), np.float32)
    c["c_ones_row_bf"] = np.ones((1, 128), dtype=ml_dtypes.bfloat16)
    c["c_ones_col_bf"] = np.ones((128, 1), dtype=ml_dtypes.bfloat16)
    NEG = -1e30
    c["c_m_sl"] = np.where(np.tri(128, 128, -1, dtype=bool), 0., NEG).astype(np.float32)
    c["c_m_su"] = np.where(np.triu(np.ones((128, 128), bool), 1), 0., NEG).astype(np.float32)
    c["c_m_iu"] = np.where(np.triu(np.ones((128, 128), bool), 0), 0., NEG).astype(np.float32)
    for r in range(4):
        k_idx = np.arange(128)[:, None]
        q_idx = np.arange(512)[None, :]
        m = np.where(k_idx <= q_idx - 128 * r, 0., NEG)
        c[f"c_bmask{r}"] = m.astype(ml_dtypes.bfloat16)
    return c



def make_ln_tmp(p, W):
    return {
        "sq": p.sbuf("ln_sq", [128, W], F32),
        "s1": p.sbuf("ln_s1", [128, 1], F32),
        "ssq": p.sbuf("ln_ssq", [128, 1], F32),
        "m": p.sbuf("ln_m", [128, 1], F32),
        "m2": p.sbuf("ln_m2", [128, 1], F32),
        "var": p.sbuf("ln_var", [128, 1], F32),
        "rstd": p.sbuf("ln_rstd", [128, 1], F32),
        "weps": p.sbuf("ln_weps", [128, 1], F32),
        "lnwh": p.sbuf("ln_lnwh", [128, 1], F32),
        "W": W,
    }


def emit_ln(p, xt, out_tile, W, tmp, sv, ss, sums=None, out_slice=None):
    inv = 1.0 / W
    if sums is None:
        p.emit("vector", lambda v: v.reduce_sum(
            tmp["s1"][:], xt, axis=AX.X), inc=sv)
        sums = tmp["s1"]
    p.wait("scalar", sv)
    p.emit("scalar", lambda s: s.activation(
        tmp["sq"][:, 0:W], xt, AF.Square, accum_out=tmp["ssq"][:]), inc=ss)
    p.wait("vector", sv)
    p.wait("vector", ss)
    p.emit("vector", lambda v: v.tensor_scalar(
        tmp["m"][:], sums[:], inv, None, ALU.mult))
    p.emit("vector", lambda v: v.tensor_tensor(
        tmp["m2"][:], sums[:], sums[:], ALU.mult), inc=sv)
    p.wait("vector", sv)
    p.emit("vector", lambda v: v.scalar_tensor_tensor(
        tmp["var"][:], tmp["m2"][:], -inv, tmp["ssq"][:], ALU.mult, ALU.add),
        inc=sv)
    p.wait("scalar", sv)
    p.emit("scalar", lambda s: s.activation(
        tmp["var"][:], tmp["var"][:], AF.Ln, bias=tmp["weps"][:]), inc=ss)
    p.wait("scalar", ss)
    p.emit("scalar", lambda s: s.activation(
        tmp["rstd"][:], tmp["var"][:], AF.Exp, scale=-0.5,
        bias=tmp["lnwh"][:]), inc=ss)
    p.wait("vector", ss)
    dst = out_tile if out_slice is None else out_slice
    p.emit("vector", lambda v: v.tensor_scalar(
        dst, xt, tmp["m"][:], tmp["rstd"][:], ALU.subtract, ALU.mult), inc=sv)



def build_fused(nc):
    p = Prog(nc)
    p.mk_auto()
    T, V, S, G, SY = "tensor", "vector", "scalar", "gpsimd", "sync"

    E, I_, SI = 16, 1408, 5632

    ins = {n: nc.dram_tensor(n, list(s), d, kind="ExternalInput")
           for n, (s, d) in {
        "xin": ([NT, HID], F32),
        "nonce": ([1, 1], I32),
        "wq0": ([128, 8, 16, 128], BF16),
        "wk0": ([128, 4, 16, 128], BF16),
        "wv0": ([128, 4, 16, 128], BF16),
        "wg": ([128, 1], F32),
        "bg": ([1, 1], F32),
        "wo0": ([128, 16, HID], BF16),
        "gw_t": ([128, 16, 16], BF16),
        "egu_t": ([E, 128, 22, 16, 128], BF16),
        "edn_t": ([E, I_, HID], BF16),
        "sgu_t": ([128, 88, 16, 128], BF16),
        "sdn_t": ([SI, HID], BF16),
        "sgv_l": ([128, 16], F32),
        "wq1": ([128, 8, 16, 128], BF16),
        "wk1": ([128, 8, 16, 128], BF16),
        "wv1": ([128, 8, 16, 128], BF16),
        "bq_l": ([128, 8], F32),
        "bk_l": ([128, 8], F32),
        "bv_l": ([128, 8], F32),
        "alibi_c": ([128, 8 * 16], F32),
        "negbd": ([8, SEQ], F32),
        "wo1": ([128, 16, HID], BF16),
        "bo": ([1, HID], F32),
        "w1t": ([128, 16, HID], BF16),
        "scale_r": ([1, HID], F32),
        "bias_r": ([1, HID], F32),
        "c_idn": ([128, 128], BF16),
        "c_idn_f32": ([128, 128], F32),
        "c_uincl": ([128, 128], F32),
        "c_ones_row": ([1, 128], F32),
        "c_ones_row_bf": ([1, 128], BF16),
        "c_ones_col_bf": ([128, 1], BF16),
        "c_m_sl": ([128, 128], F32),
        "c_m_su": ([128, 128], F32),
        "c_m_iu": ([128, 128], F32),
        "c_bmask0": ([128, 512], BF16),
        "c_bmask1": ([128, 512], BF16),
        "c_bmask2": ([128, 512], BF16),
        "c_bmask3": ([128, 512], BF16),
    }.items()}

    xout = nc.dram_tensor("xout", [NT, HID], F32, kind="ExternalOutput")
    xres1 = nc.dram_tensor("xres1", [NT, HID], F32)
    xres2 = nc.dram_tensor("xres2", [NT, HID], F32)
    xres3 = nc.dram_tensor("xres3", [NT, HID], F32)
    flaloc = nc.dram_tensor("flaloc", [SEQ * NT], BF16)
    baseloc = nc.dram_tensor("baseloc", [NT * SEQ], BF16)
    shx = nc.dram_tensor("shx", [2 * SHX_REG], BF16, addr_space="Shared")
    sha = nc.dram_tensor("sha", [SHA_SIZE], BF16, addr_space="Shared")
    flags = nc.dram_tensor("flags", [2, 8], I32, addr_space="Shared")

    pid = nc.partition_id()
    half = pid % 2
    oth = 1 - half

    A1 = Arena(p, "A1", 64)
    A2 = Arena(p, "A2", 32)
    A4 = Arena(p, "A4", 60)
    AW = Arena(p, "AW", 16)
    xrt = [p.sbuf(f"xrt{i}", [128, HID], F32) for i in range(2)]
    tmp = make_ln_tmp(p, HID)
    nt_sb = p.sbuf("nt_sb", [1, 1], I32)
    idn_bf = p.sbuf("idn_bf", [128, 128], BF16)
    idn_f = p.sbuf("idn_f", [128, 128], F32)
    uincl = p.sbuf("uincl_s", [128, 128], F32)
    ones_r = p.sbuf("ones_r", [1, 128], F32)
    ones_rb = p.sbuf("ones_rb", [1, 128], BF16)
    ones_cb = p.sbuf("ones_cb", [128, 1], BF16)
    m_sl = p.sbuf("m_sl", [128, 128], F32)
    m_su = p.sbuf("m_su", [128, 128], F32)
    m_iu = p.sbuf("m_iu", [128, 128], F32)
    bmask = [p.sbuf(f"bmask{r}", [128, 512], BF16) for r in range(4)]
    PSB = [p.psum(f"PSB{i}", [128, 1024], F32) for i in range(4)]
    PS = [PSB[i // 2][:, (i % 2) * 512:(i % 2) * 512 + 512] for i in range(8)]
    PTB = [PS[6][:, 0:64].bitcast(BF16), PS[7][:, 0:64].bitcast(BF16)]

    nr = p.reg("nr")
    pr = p.reg("pr")
    pr2 = p.reg("pr2")
    fr = p.reg("fr")
    f2r = p.reg("f2r")
    cr = p.reg("cr")

    def mm(out_ap, l_ap, r_ap, start=True, stop=True):
        p.a(T, lambda t, o=out_ap, a=l_ap, b=r_ap, st=start, sp=stop:
            t.matmul(o, a, b, start=st, stop=sp))

    def scp(out_ap, in_ap):
        p.a(S, lambda s, o=out_ap, i=in_ap: s.copy(o, i))

    def act(out_ap, in_ap, f, bias=0.0, scale=1.0, accum=None):
        if accum is None:
            p.a(S, lambda s, o=out_ap, i=in_ap, ff=f, b=bias, sc=scale:
                s.activation(o, i, ff, bias=b, scale=sc))
        else:
            p.a(S, lambda s, o=out_ap, i=in_ap, ff=f, b=bias, sc=scale,
                ac=accum: s.activation(o, i, ff, bias=b, scale=sc,
                                       accum_out=ac))

    def dynap(handle, off_rv, rows, cols, row_stride):
        return bass.AP(handle, off_rv, [[row_stride, rows], [1, cols]])

    def exchange(phase):
        p.w(G, "dgpsimd")

        def do_x(g, ph=phase):
            g.reg_add(pr, nr, ph)
            g.store(dynap(flags, half * 8 + (ph - 1), 1, 1, 8), pr)
            g.reg_add(pr2, pr, pr)
            g.reg_mov(cr, 1)
            with g.While(cr):
                g.load(fr, flags[0:1, ph - 1:ph])
                g.load(f2r, flags[1:2, ph - 1:ph])
                g.reg_add(cr, fr, f2r)
                g.reg_sub(cr, cr, pr2)
        p.ops[G].append(do_x)
        p.emit(G, lambda g: g.nop(), inc=p.xch)
        p.join()

    for nm, dst in [("c_idn", idn_bf), ("c_idn_f32", idn_f),
                    ("c_uincl", uincl), ("c_ones_row", ones_r),
                    ("c_ones_row_bf", ones_rb), ("c_ones_col_bf", ones_cb),
                    ("c_m_sl", m_sl), ("c_m_su", m_su), ("c_m_iu", m_iu)]:
        p.admas(SY, dst[:], ins[nm][:, :])
    for r in range(4):
        p.admas(SY, bmask[r][:], ins[f"c_bmask{r}"][:, :])
    p.admas(G, nt_sb[:], ins["nonce"][:, :])
    p.w(G, "dgpsimd")
    p.ops[G].append(lambda g: g.reg_load(nr, nt_sb[0:1, 0:1]))
    p.a(V, lambda v: v.memset(tmp["weps"][:], HID * LN_EPS))
    p.a(V, lambda v: v.memset(tmp["lnwh"][:], 0.5 * float(np.log(HID))))

    def ph_lntr(src, phase):
        A4.reset(); A2.reset()
        xln = A4.take([128, HID], BF16)
        strips = A2.take([128, 16 * NT], BF16)
        sv, ss = p.auto["vector"], p.auto["scalar"]
        tr_done = [0]
        for t in range(8):
            buf = xrt[t % 2]
            p.w(SY, "vector", "scalar")
            p.admas(SY, buf[:], src[t * 128:(t + 1) * 128, :])
            p.w(V, "dsync")
            p.w(S, "dsync")
            if t > 0:
                p.w(V, "tensor")
            emit_ln(p, buf[:], xln[:], HID, tmp, sv, ss)
            p.w(T, "vector")
            if t == 0:
                p.w(T, "dsync")
            for hc in range(16):
                bk = tr_done[0] % 2
                p.w(T, "scalar")
                p.a(T, lambda tt, bk=bk, hc=hc: tt.transpose(
                    PTB[bk], xln[:, hc * 128:(hc + 1) * 128], idn_bf[:]))
                tr_done[0] += 1
                p.w(S, "tensor")
                scp(strips[:, hc * NT + t * 128: hc * NT + (t + 1) * 128],
                    PTB[bk])
        p.w(G, "scalar")
        p.admas(G, bass.AP(shx, half * SHX_REG,
                           [[NT, 128], [128 * NT, 16], [1, NT]]),
                strips.rearrange("p (hc t) -> p hc t", hc=16))
        exchange(phase)

    def load_xl(xl):
        for hh in range(2):
            for c in range(16):
                p.admas(G, xl[:, c * SEQ + hh * NT: c * SEQ + hh * NT + NT],
                        bass.AP(shx, hh * SHX_REG + c * 128 * NT,
                                [[NT, 128], [1, NT]]))

    def ph_l0_attn():
        A4.reset(); A2.reset(); AW.reset(); A1.reset()
        xl = A1.take([128, 16 * SEQ], BF16)
        wq_sb = [AW.take([128, 16 * 128], BF16) for _ in range(2)]
        wk_sb = AW.take([128, 16 * 128], BF16)
        wv_sb = AW.take([128, 16 * 128], BF16)
        k_fm = A2.take([128, SEQ], BF16)
        v_fm = A2.take([128, SEQ], BF16)
        v_tm = A2.take([128, SEQ], BF16)
        q_fm = [A2.take([128, SEQ], BF16) for _ in range(2)]
        eq_fm = [A2.take([128, SEQ], BF16) for _ in range(2)]
        ek_fm = A2.take([128, SEQ], BF16)
        ek_tm = A4.take([128, SEQ], BF16)
        fla_j = A4.take([128, 16 * 256], BF16)
        obase = A4.take([128, SEQ], BF16)
        lnk = A4.take([1, SEQ], F32)
        lnq = [A4.take([1, SEQ], F32) for _ in range(2)]
        ptile = [A4.take([128, 512], BF16) for _ in range(2)]
        e_sb = A4.take([128, 512], F32)
        rbc = A4.take([128, 512], F32)
        pow_sb = A4.take([128, 5 * 128], BF16)
        powt_sb = A4.take([128, 6 * 128], BF16)
        s_f32 = [A4.take([128, 128], F32) for _ in range(2)]
        xbuf = [A4.take([128, 128], BF16) for _ in range(2)]
        u_bf = A4.take([128, 128], BF16)
        r_bf = A4.take([128, 128], BF16)
        kt_bf = A4.take([128, 128], BF16)
        t_bf = A4.take([128, 128], BF16)
        tt_bf = A4.take([128, 128], BF16)
        pt_bf = [A4.take([128, 128], BF16) for _ in range(2)]
        s_bf = A4.take([128, 128], BF16)
        zt_sb = A4.take([128, 128], F32)
        rrow = A4.take([1, 512], F32)
        arow = A4.take([1, 128], F32)
        bk_r = A4.take([1, 128], F32)
        nck_r = A4.take([1, 128], F32)
        bq_r = [A4.take([1, 128], F32) for _ in range(2)]
        g_sb = A4.take([128, 16], F32)
        gt1 = A4.take([128, 16], F32)
        gt2 = A4.take([128, 16], F32)
        gt3 = A4.take([128, 16], F32)
        gs_sb = A4.take([128, 16], F32)
        cols_sb = A4.take([128, 6], F32)
        negb = A4.take([128, 1], F32)
        kscale = A4.take([128, 1], F32)
        bqe = [A4.take([128, 1], F32) for _ in range(2)]
        ace = A4.take([128, 1], F32)
        neg12 = A4.take([128, 1], F32)
        neg40 = A4.take([128, 1], F32)
        one_t = A4.take([128, 1], F32)
        wg_bf = A4.take([128, 1], BF16)
        bgb = A4.take([128, 1], F32)

        load_xl(xl)
        p.admas(G, wg_bf[:], ins["wg"][:, :])
        p.admas(SY, bgb[:], ins["bg"][:, :].broadcast_to([128, 1]))
        p.a(V, lambda v: v.memset(neg12[:], -QK_SHIFT))
        p.a(V, lambda v: v.memset(neg40[:], -SC_SHIFT))
        p.a(V, lambda v: v.memset(one_t[:], 1.0))

        wload = [0]

        def proj(dst_sb, w_sb, nblocks=2):
            for n in range(nblocks):
                p.wait_at(T, "dgpsimd", wload[0])
                p.w(T, "scalar")
                for c in range(16):
                    mm(PS[0][:, 0:512],
                       w_sb[:, c * 128:(c + 1) * 128],
                       xl[:, c * SEQ + n * 1024: c * SEQ + n * 1024 + 512],
                       start=(c == 0), stop=(c == 15))
                    mm(PS[1][:, 0:512],
                       w_sb[:, c * 128:(c + 1) * 128],
                       xl[:, c * SEQ + n * 1024 + 512: c * SEQ + n * 1024 + 1024],
                       start=(c == 0), stop=(c == 15))
                p.w(S, "tensor")
                scp(dst_sb[:, n * 1024:(n + 1) * 1024], PSB[0][:, 0:1024])

        wgt_use = [0]
        for j in range(4):
            p.wait_at(G, "tensor", wgt_use[0])
            p.admas(G, wk_sb[:].rearrange("p (c j) -> p c j", c=16),
                    ins["wk0"][:, j, :, :])
            p.admas(G, wv_sb[:].rearrange("p (c j) -> p c j", c=16),
                    ins["wv0"][:, j, :, :])
            for dh in range(2):
                p.admas(G, wq_sb[dh][:].rearrange("p (c j) -> p c j", c=16),
                        ins["wq0"][:, 2 * j + dh, :, :])
            wload[0] = p.dmark("gpsimd")
            if j == 0:
                p.w(T, "dsync")
            proj(k_fm, wk_sb)
            proj(v_fm, wv_sb)
            proj(q_fm[0], wq_sb[0])
            proj(q_fm[1], wq_sb[1])
            wgt_use[0] = p.mark("tensor")
            for c in range(16):
                p.w(T, "scalar")
                p.a(T, lambda t, c=c: t.transpose(
                    PTB[c % 2], v_fm[:, c * 128:(c + 1) * 128], idn_bf[:]))
                p.w(S, "tensor")
                scp(v_tm[:, c * 128:(c + 1) * 128], PTB[c % 2])
            p.w(S, "tensor", "vector")
            act(ek_fm[:], k_fm[:], AF.Exp, bias=neg12[:])
            act(eq_fm[0][:], q_fm[0][:], AF.Exp, bias=neg12[:])
            act(eq_fm[1][:], q_fm[1][:], AF.Exp, bias=neg12[:])
            for c in range(16):
                p.w(T, "scalar")
                p.a(T, lambda t, c=c: t.transpose(
                    PTB[c % 2], ek_fm[:, c * 128:(c + 1) * 128], idn_bf[:]))
                p.w(S, "tensor")
                scp(ek_tm[:, c * 128:(c + 1) * 128], PTB[c % 2])
            for (src, dstrow) in [(ek_fm, lnk), (eq_fm[0], lnq[0]),
                                  (eq_fm[1], lnq[1])]:
                for n in range(4):
                    p.w(T, "scalar")
                    mm(PS[2][0:1, 0:512], ones_cb[:, 0:1],
                       src[:, n * 512:(n + 1) * 512])
                    p.w(S, "tensor")
                    act(dstrow[0:1, n * 512:(n + 1) * 512],
                        PS[2][0:1, 0:512], AF.Ln)
            p.w(T, "scalar", "dgpsimd")
            for c in range(16):
                mm(PS[2][:, c:c + 1], k_fm[:, c * 128:(c + 1) * 128],
                   wg_bf[:, 0:1])
            p.w(V, "tensor", "dsync")
            p.a(V, lambda v: v.tensor_scalar(g_sb[:], PS[2][:, 0:16], bgb[:],
                                             None, ALU.add))
            p.w(S, "vector")
            act(gt1[:], g_sb[:], AF.Abs)
            p.w(S, "self")
            act(gt2[:], gt1[:], AF.Exp, scale=-1.0)
            p.w(S, "self")
            act(gt3[:], gt2[:], AF.Ln, bias=one_t[:])
            p.w(V, "scalar")
            p.a(V, lambda v: v.tensor_scalar_min(gt1[:], g_sb[:], 0.0))
            p.w(V, "self")
            p.a(V, lambda v: v.tensor_sub(gt2[:], gt1[:], gt3[:]))
            p.w(V, "self")
            p.a(V, lambda v: v.tensor_scalar_mul(gs_sb[:], gt2[:], 1.0 / 16.0))

            for dh in range(2):
                hl = 2 * j + dh
                p.w(V, "dgpsimd")
                for qb in range(4):
                    nkc = (qb + 1) * 4
                    for kc in range(nkc):
                        sb = 4 + (kc % 2)
                        p.w(T, "scalar", "vector")
                        mm(PS[sb][:, 0:512], k_fm[:, kc * 128:(kc + 1) * 128],
                           q_fm[dh][:, qb * 512:(qb + 1) * 512],
                           start=True, stop=(kc < qb * 4))
                        if kc >= qb * 4:
                            mm(PS[sb][:, 0:512], idn_bf[:],
                               bmask[kc - qb * 4][:], start=False, stop=True)
                        p.w(S, "tensor")
                        act(ptile[kc % 2][:], PS[sb][:, 0:512], AF.Exp,
                            bias=neg40[:], scale=DSCALE)
                        p.w(T, "scalar")
                        mm(PS[2][0:1, 0:512], ones_cb[:, 0:1],
                           ptile[kc % 2][:], start=(kc == 0),
                           stop=(kc == nkc - 1))
                        mm(PS[3][:, 0:512], v_tm[:, kc * 128:(kc + 1) * 128],
                           ptile[kc % 2][:], start=(kc == 0),
                           stop=(kc == nkc - 1))
                    p.w(V, "tensor")
                    p.a(V, lambda v: v.reciprocal(rrow[0:1, 0:512],
                                                  PS[2][0:1, 0:512]))
                    p.w(T, "vector")
                    mm(PS[1][:, 0:512], ones_r[0:1, :], rrow[0:1, 0:512])
                    p.w(S, "tensor")
                    scp(rbc[:], PS[1][:, 0:512])
                    p.w(V, "scalar")
                    p.a(V, lambda v, qb=qb: v.tensor_tensor(
                        obase[:, qb * 512:(qb + 1) * 512], PS[3][:, 0:512],
                        rbc[:], ALU.mult))
                p.w(G, "vector")
                p.admas(G, bass.AP(baseloc, hl * 128 * SEQ,
                                   [[SEQ, 128], [1, SEQ]]), obase[:])

            p.w(V, "dgpsimd")
            for c in range(16):
                cs = slice(c * 128, (c + 1) * 128)
                p.w(T, "vector", "scalar")
                mm(PS[5][0:1, 128:256], gs_sb[:, c:c + 1], uincl[:])
                p.w(S, "tensor")
                scp(arow[0:1, :], PS[5][0:1, 128:256])
                p.w(V, "scalar")
                p.a(V, lambda v, cs=cs: v.tensor_sub(bk_r[0:1, :],
                                                     arow[0:1, :],
                                                     lnk[0:1, cs]))
                p.a(V, lambda v, cs=cs: v.scalar_tensor_tensor(
                    nck_r[0:1, :], arow[0:1, :], -1.0, lnk[0:1, cs],
                    ALU.mult, ALU.subtract))
                p.a(V, lambda v, cs=cs: v.tensor_sub(bq_r[0][0:1, :],
                                                     arow[0:1, :],
                                                     lnq[0][0:1, cs]))
                p.a(V, lambda v, cs=cs: v.tensor_sub(bq_r[1][0:1, :],
                                                     arow[0:1, :],
                                                     lnq[1][0:1, cs]))
                p.w(T, "vector")
                mm(PS[5][:, 0:1], bk_r[0:1, :], ones_r[0:1, 0:1])
                mm(PS[5][:, 1:2], nck_r[0:1, :], ones_r[0:1, 0:1])
                mm(PS[5][:, 2:3], bq_r[0][0:1, :], ones_r[0:1, 0:1])
                mm(PS[5][:, 3:4], bq_r[1][0:1, :], ones_r[0:1, 0:1])
                mm(PS[5][:, 4:5], ones_r[0:1, :], arow[0:1, 127:128])
                p.w(S, "tensor")
                scp(cols_sb[:, 0:5], PS[5][:, 0:5])
                p.w(S, "self")
                act(negb[:], cols_sb[:, 0:1], AF.Exp)
                p.w(V, "scalar")
                p.a(V, lambda v: v.tensor_scalar_mul(negb[:], negb[:], -1.0))
                p.a(V, lambda v: v.tensor_add(cols_sb[:, 5:6],
                                              cols_sb[:, 4:5],
                                              cols_sb[:, 1:2]))
                p.w(S, "vector")
                act(kscale[:], cols_sb[:, 5:6], AF.Exp)
                act(bqe[0][:], cols_sb[:, 2:3], AF.Exp)
                act(bqe[1][:], cols_sb[:, 3:4], AF.Exp)
                act(ace[:], cols_sb[:, 4:5], AF.Exp)
                p.w(T, "scalar")
                mm(PS[0][:, 0:128], bk_r[0:1, :], ones_r[0:1, :], True, False)
                mm(PS[0][:, 0:128], ones_r[0:1, :], nck_r[0:1, :], False, False)
                mm(PS[0][:, 0:128], idn_f[:], m_sl[:], False, True)
                mm(PS[0][:, 128:256], nck_r[0:1, :], ones_r[0:1, :], True, False)
                mm(PS[0][:, 128:256], ones_r[0:1, :], bq_r[0][0:1, :], False, False)
                mm(PS[0][:, 128:256], idn_f[:], m_iu[:], False, True)
                mm(PS[0][:, 256:384], nck_r[0:1, :], ones_r[0:1, :], True, False)
                mm(PS[0][:, 256:384], ones_r[0:1, :], bq_r[1][0:1, :], False, False)
                mm(PS[0][:, 256:384], idn_f[:], m_iu[:], False, True)
                mm(PS[0][:, 384:512], nck_r[0:1, :], ones_r[0:1, :], True, False)
                mm(PS[0][:, 384:512], ones_r[0:1, :], bk_r[0:1, :], False, False)
                mm(PS[0][:, 384:512], idn_f[:], m_su[:], False, True)
                p.w(S, "tensor")
                act(e_sb[:], PS[0][:, 0:512], AF.Exp)
                p.w(T, "scalar")
                mm(PS[1][:, 0:128], ek_fm[:, cs], ek_fm[:, cs])
                mm(PS[1][:, 128:256], ek_fm[:, cs], eq_fm[0][:, cs])
                mm(PS[1][:, 256:384], ek_fm[:, cs], eq_fm[1][:, cs])
                p.w(V, "tensor", "scalar")
                p.a(V, lambda v: v.tensor_tensor(t_bf[:], e_sb[:, 0:128],
                                                 PS[1][:, 0:128], ALU.mult))
                p.a(V, lambda v: v.tensor_tensor(pt_bf[0][:],
                                                 e_sb[:, 128:256],
                                                 PS[1][:, 128:256], ALU.mult))
                p.a(V, lambda v: v.tensor_tensor(pt_bf[1][:],
                                                 e_sb[:, 256:384],
                                                 PS[1][:, 256:384], ALU.mult))
                p.a(V, lambda v: v.tensor_tensor(tt_bf[:], e_sb[:, 384:512],
                                                 PS[1][:, 0:128], ALU.mult))
                p.w(T, "vector")
                prevX, prevXt = t_bf[:], tt_bf[:]
                for lev in range(1, 6):
                    mm(PS[2][:, 0:128], prevXt, prevX)
                    mm(PS[2][:, 128:256], prevX, prevXt)
                    p.w(S, "tensor")
                    scp(pow_sb[:, (lev - 1) * 128:lev * 128], PS[2][:, 0:128])
                    scp(powt_sb[:, (lev - 1) * 128:lev * 128],
                        PS[2][:, 128:256])
                    p.w(T, "scalar")
                    prevX = pow_sb[:, (lev - 1) * 128:lev * 128]
                    prevXt = powt_sb[:, (lev - 1) * 128:lev * 128]
                mm(PS[2][:, 0:128], pow_sb[:, 4 * 128:5 * 128],
                   powt_sb[:, 4 * 128:5 * 128])
                p.w(S, "tensor")
                scp(powt_sb[:, 5 * 128:6 * 128], PS[2][:, 0:128])
                if c > 0:
                    p.w(T, "scalar", "vector")
                    mm(PS[3][:, 0:128], ek_fm[:, cs], s_bf[:])
                    p.w(V, "tensor")
                    p.a(V, lambda v, cs=cs: v.scalar_tensor_tensor(
                        r_bf[:], PS[3][:, 0:128], negb[:], v_tm[:, cs],
                        ALU.mult, ALU.add))
                cur = (v_tm[:, cs] if c == 0 else r_bf[:])
                xb = 0
                for lev in [5, 4, 3, 2, 1]:
                    p.w(T, "scalar", "vector")
                    mm(PS[3][:, 128:256],
                       powt_sb[:, lev * 128:(lev + 1) * 128], cur)
                    p.w(V, "tensor")
                    p.a(V, lambda v, xb=xb, cur=cur: v.tensor_tensor(
                        xbuf[xb][:], cur, PS[3][:, 128:256], ALU.add))
                    cur = xbuf[xb][:]
                    xb = 1 - xb
                p.w(T, "vector")
                mm(PS[3][:, 128:256], powt_sb[:, 0:128], cur)
                p.w(V, "tensor")
                p.a(V, lambda v, cur=cur: v.tensor_tensor(
                    xbuf[xb][:], cur, PS[3][:, 128:256], ALU.add))
                cur = xbuf[xb][:]
                p.w(T, "vector")
                mm(PS[3][:, 128:256], tt_bf[:], cur)
                p.w(V, "tensor")
                p.a(V, lambda v, cur=cur: v.tensor_sub(u_bf[:], cur,
                                                       PS[3][:, 128:256]))
                p.w(T, "vector")
                if c > 0:
                    mm(PS[3][:, 256:384], eq_fm[0][:, cs], s_bf[:])
                    mm(PS[3][:, 384:512], eq_fm[1][:, cs], s_bf[:])
                mm(PS[4][:, 0:128], pt_bf[0][:], u_bf[:])
                mm(PS[4][:, 128:256], pt_bf[1][:], u_bf[:])
                p.w(V, "tensor")
                for dh in range(2):
                    dst = fla_j[:, c * 256 + dh * 128: c * 256 + dh * 128 + 128]
                    if c > 0:
                        p.a(V, lambda v, dh=dh: v.tensor_scalar(
                            zt_sb[:], PS[3][:, 256 + dh * 128:384 + dh * 128],
                            bqe[dh][:], DSCALE, ALU.mult, ALU.mult))
                        p.w(V, "self")
                        p.a(V, lambda v, dh=dh, dst=dst: v.scalar_tensor_tensor(
                            dst, PS[4][:, dh * 128:(dh + 1) * 128], DSCALE,
                            zt_sb[:], ALU.mult, ALU.add))
                    else:
                        p.a(V, lambda v, dh=dh, dst=dst: v.tensor_scalar(
                            dst, PS[4][:, dh * 128:(dh + 1) * 128], DSCALE,
                            None, ALU.mult))
                if c < 15:
                    p.w(V, "scalar")
                    p.a(V, lambda v, cs=cs: v.tensor_scalar(
                        kt_bf[:], ek_tm[:, cs], kscale[:], None, ALU.mult))
                    p.w(T, "vector")
                    mm(PS[5][:, 128:256], kt_bf[:], u_bf[:])
                    p.w(V, "tensor")
                    cu = c % 2
                    if c == 0:
                        p.a(V, lambda v, cu=cu: v.tensor_copy(
                            s_f32[cu][:], PS[5][:, 128:256]))
                    else:
                        p.a(V, lambda v, cu=cu: v.scalar_tensor_tensor(
                            s_f32[cu][:], s_f32[1 - cu][:], ace[:],
                            PS[5][:, 128:256], ALU.mult, ALU.add))
                    p.w(S, "vector")
                    scp(s_bf[:], s_f32[cu][:])
            p.w(G, "vector")
            p.admas(G, bass.AP(flaloc, j * 256,
                               [[NT, 128], [128 * NT, 16], [1, 256]]),
                    fla_j.rearrange("p (c t) -> p c t", c=16))
        p.w(G, "dgpsimd")
        p.admas(G, bass.AP(sha, FLA0 + half * SHX_REG + 0,
                           [[32768, 64], [1, 32768]]),
                bass.AP(flaloc, 0, [[32768, 64], [1, 32768]]))
        p.admas(G, bass.AP(sha, BASE0 + half * (NT * SEQ) + 0,
                           [[32768, 64], [1, 32768]]),
                bass.AP(baseloc, 0, [[32768, 64], [1, 32768]]))
        exchange(2)

    def ph_mix(mode, xsrc, xdst, wo_name):
        A1.reset(); A2.reset(); A4.reset()
        wo_sb = A1.take([128, 16 * HID], BF16)
        bs_sb = A2.take([128, 16 * NT], BF16)
        fla_t = [A4.take([128, HID], BF16) for _ in range(2)]
        mixed = A4.take([128, HID], F32)
        lnout = A4.take([128, HID], F32)
        xo = A4.take([128, HID], F32)
        fs = A4.take([128, NT], F32)
        bo_bc = A4.take([128, HID], F32) if mode == "l1" else None

        for c4 in range(4):
            p.admas(G, wo_sb[:].rearrange("p (c j) -> p c j", c=16)
                    [:, 4 * c4:4 * c4 + 4, :],
                    ins[wo_name][:, 4 * c4:4 * c4 + 4, :])
        for hh in range(2):
            p.admas(G, bs_sb[:, hh * 8 * NT:(hh + 1) * 8 * NT]
                    .rearrange("p (cb t) -> p cb t", cb=8),
                    bass.AP(sha, BASE0 + hh * (NT * SEQ) + half * NT,
                            [[SEQ, 128], [128 * SEQ, 8], [1, NT]]))
        if mode == "l0":
            for hh in range(2):
                p.admas(G, bass.AP(flaloc, hh * NT, [[SEQ, NT], [1, NT]]),
                        bass.AP(sha, FLA0 + hh * SHX_REG + half * (NT * NT),
                                [[NT, NT], [1, NT]]))
            p.w(G, "dgpsimd")
        if mode == "l1":
            p.admas(SY, bo_bc[:], ins["bo"][:, :].broadcast_to([128, HID]))
        wbs_load = p.dmark("gpsimd")
        ps_use = [0, 0]
        fla_use = [0, 0]

        for t in range(8):
            bi = t % 2
            p.w(SY, "vector", "scalar")
            p.admas(SY, xrt[bi][:], xsrc[t * 128:(t + 1) * 128, :])
            if mode == "l0":
                p.wait_at(G, "vector", fla_use[bi])
                p.admas(G, fla_t[bi][:],
                        bass.AP(flaloc, t * 128 * SEQ, [[SEQ, 128], [1, SEQ]]))
            fla_mark = p.dmark("gpsimd")
            p.w(V, "scalar")
            for n in range(2):
                bank = n % 2
                p.wait_at(T, "vector", ps_use[bank])
                if t == 0 and n == 0:
                    p.wait_at(T, "dgpsimd", wbs_load)
                for c in range(16):
                    mm_l = bs_sb[:, c * NT + t * 128: c * NT + (t + 1) * 128]
                    mm_r = wo_sb[:, c * HID + n * 1024: c * HID + n * 1024 + 512]
                    mm_r2 = wo_sb[:, c * HID + n * 1024 + 512:
                                  c * HID + n * 1024 + 1024]
                    p.a(T, lambda tt, o=PS[2 * bank][:, 0:512], a=mm_l, b=mm_r,
                        st=(c == 0), sp=(c == 15): tt.matmul(o, a, b,
                                                             start=st, stop=sp))
                    p.a(T, lambda tt, o=PS[2 * bank + 1][:, 0:512], a=mm_l,
                        b=mm_r2, st=(c == 0), sp=(c == 15):
                        tt.matmul(o, a, b, start=st, stop=sp))
                p.w(V, "tensor", "dsync")
                p.wait_at(V, "dgpsimd", fla_mark)
                if mode == "l0":
                    p.a(V, lambda v, bi=bi, n=n: v.tensor_scalar(
                        fs[:], fla_t[bi][:, n * 1024:(n + 1) * 1024], 0.01,
                        None, ALU.mult))
                    p.w(V, "self")
                    p.a(V, lambda v, n=n, bank=bank: v.scalar_tensor_tensor(
                        mixed[:, n * 1024:(n + 1) * 1024], PSB[bank][:, 0:1024],
                        0.99, fs[:], ALU.mult, ALU.add))
                else:
                    p.a(V, lambda v, n=n, bank=bank: v.tensor_tensor(
                        mixed[:, n * 1024:(n + 1) * 1024], PSB[bank][:, 0:1024],
                        bo_bc[:, n * 1024:(n + 1) * 1024], ALU.add))
                ps_use[bank] = p.mark("vector")
            fla_use[bi] = p.mark("vector")
            p.w(V, "self")
            p.w(S, "vector")
            emit_ln(p, mixed[:], lnout[:], HID, tmp,
                    p.auto["vector"], p.auto["scalar"])
            p.w(V, "self", "dsync")
            p.a(V, lambda v, bi=bi: v.tensor_add(xo[:], xrt[bi][:],
                                                 lnout[:]))
            p.w(SY, "vector")
            p.admas(SY, xdst[t * 128:(t + 1) * 128, :], xo[:])

    def ph_moe():
        A1.reset(); A2.reset(); A4.reset(); AW.reset()
        acc = A1.take([128, 8 * HID], F32)
        xlnt = A2.take([128, 16 * NT], BF16)
        gh = A4.take([128, 44 * 512], BF16)
        xlnf = A4.take([128, HID], F32)
        stmp = A4.take([128, NT], BF16)
        ytmp = A4.take([128, 512], F32)
        comb = A4.take([128, 8 * 16], F32)
        probs = A4.take([128, 16], F32)
        wrk = A4.take([128, 16], F32)
        eqt = A4.take([128, 16], F32)
        mx = A4.take([128, 4], F32)
        sig_sb = A4.take([128, 8], F32)
        s4 = A4.take([128, 1], F32)
        rs4 = A4.take([128, 1], F32)
        negm = A4.take([128, 1], F32)
        lnout = xlnf
        wring = [AW.take([128, 16 * 128], BF16) for _ in range(2)]
        dring = [AW.take([128, 512], BF16) for _ in range(2)]
        gwt_sb = AW.take([128, 16 * 16], BF16)
        sgv_sb = AW.take([128, 16], BF16)

        p.admas(G, gwt_sb[:].rearrange("p (c j) -> p c j", c=16),
                ins["gw_t"][:, :, :])
        p.admas(G, sgv_sb[:], ins["sgv_l"][:, :])

        for t in range(8):
            bi = t % 2
            p.w(SY, "vector", "scalar")
            p.admas(SY, xrt[bi][:], xres1[t * 128:(t + 1) * 128, :])
            p.w(V, "dsync")
            p.w(S, "dsync")
            p.w(V, "tensor")
            emit_ln(p, xrt[bi][:], xlnf[:], HID, tmp,
                    p.auto["vector"], p.auto["scalar"])
            p.w(T, "vector", "dsync")
            for c in range(16):
                cb = c % 2
                p.w(T, "scalar")
                p.a(T, lambda tt, c=c, cb=cb: tt.transpose(
                    PS[2][:, cb * 128:cb * 128 + 128],
                    xlnf[:, c * 128:(c + 1) * 128], idn_f[:]))
                p.w(S, "tensor")
                scp(xlnt[:, c * NT + t * 128: c * NT + (t + 1) * 128],
                    PS[2][:, cb * 128:cb * 128 + 128])
            p.w(T, "scalar", "dgpsimd")
            for c in range(16):
                mm(PS[0][:, 0:16], xlnt[:, c * NT + t * 128: c * NT + (t + 1) * 128],
                   gwt_sb[:, c * 16:(c + 1) * 16], start=(c == 0),
                   stop=(c == 15))
            p.w(V, "tensor")
            p.a(V, lambda v: v.reduce_max(negm[:], PS[0][:, 0:16], axis=AX.X,
                                          negate=True))
            p.w(S, "vector")
            act(probs[:], PS[0][:, 0:16], AF.Exp, bias=negm[:])
            p.w(V, "scalar")
            p.a(V, lambda v: v.tensor_copy(wrk[:], probs[:]))
            for jj in range(4):
                p.w(V, "self")
                p.a(V, lambda v, jj=jj: v.reduce_max(mx[:, jj:jj + 1], wrk[:],
                                                     axis=AX.X))
                if jj < 3:
                    p.w(V, "self")
                    p.a(V, lambda v, jj=jj: v.tensor_scalar(
                        eqt[:], wrk[:], mx[:, jj:jj + 1], None, ALU.is_ge))
                    p.w(V, "self")
                    p.a(V, lambda v: v.scalar_tensor_tensor(
                        wrk[:], eqt[:], -1e30, wrk[:], ALU.mult, ALU.add))
            p.w(V, "self")
            p.a(V, lambda v: v.tensor_scalar(eqt[:], probs[:], mx[:, 3:4],
                                             None, ALU.is_ge))
            p.w(V, "self")
            p.a(V, lambda v: v.tensor_mul(wrk[:], probs[:], eqt[:]))
            p.w(V, "self")
            p.a(V, lambda v: v.reduce_sum(s4[:], wrk[:], axis=AX.X))
            p.w(V, "self")
            p.a(V, lambda v: v.reciprocal(rs4[:], s4[:]))
            p.w(V, "self")
            p.a(V, lambda v, t=t: v.tensor_scalar(
                comb[:, t * 16:(t + 1) * 16], wrk[:], rs4[:], None, ALU.mult))
            p.w(T, "vector")
            for c in range(16):
                mm(PS[1][:, 0:1], xlnt[:, c * NT + t * 128: c * NT + (t + 1) * 128],
                   sgv_sb[:, c:c + 1], start=(c == 0), stop=(c == 15))
            p.w(S, "tensor")
            act(sig_sb[:, t:t + 1], PS[1][:, 0:1], AF.Sigmoid)

        for blk in range(2):
            tok0 = blk * 512
            for m in range(44):
                wr = wring[m % 2]
                p.w(G, "tensor")
                p.admas(G, wr[:].rearrange("p (c j) -> p c j", c=16),
                        ins["sgu_t"][:, m, :, :])
                p.w(T, "dgpsimd", "scalar")
                for c in range(16):
                    mm(PS[0][:, 0:512], wr[:, c * 128:(c + 1) * 128],
                       xlnt[:, c * NT + tok0: c * NT + tok0 + 512],
                       start=(c == 0), stop=(c == 15))
                p.w(S, "tensor")
                scp(gh[:, m * 512:(m + 1) * 512], PS[0][:, 0:512])
            for m in range(44):
                wr = wring[m % 2]
                p.w(G, "tensor")
                p.admas(G, wr[:].rearrange("p (c j) -> p c j", c=16),
                        ins["sgu_t"][:, 44 + m, :, :])
                p.w(T, "dgpsimd", "vector")
                for c in range(16):
                    mm(PS[0][:, 0:512], wr[:, c * 128:(c + 1) * 128],
                       xlnt[:, c * NT + tok0: c * NT + tok0 + 512],
                       start=(c == 0), stop=(c == 15))
                p.w(S, "tensor", "vector")
                act(stmp[:, 0:512], gh[:, m * 512:(m + 1) * 512], AF.Silu)
                p.w(V, "scalar", "tensor")
                p.a(V, lambda v, m=m: v.tensor_tensor(
                    gh[:, m * 512:(m + 1) * 512], stmp[:, 0:512],
                    PS[0][:, 0:512], ALU.mult))
            for n in range(4):
                p.w(T, "scalar", "vector")
                for k in range(44):
                    dr = dring[k % 2]
                    p.w(G, "tensor")
                    p.admas(G, dr[:], ins["sdn_t"][k * 128:(k + 1) * 128,
                                                   n * 512:(n + 1) * 512])
                    p.w(T, "dgpsimd")
                    for ts in range(4):
                        mm(PS[4 + ts][:, 0:512],
                           gh[:, k * 512 + ts * 128: k * 512 + (ts + 1) * 128],
                           dr[:], start=(k == 0), stop=(k == 43))
                p.w(S, "tensor")
                for ts in range(4):
                    gts = blk * 4 + ts
                    act(acc[:, gts * HID + n * 512: gts * HID + n * 512 + 512],
                        PS[4 + ts][:, 0:512], AF.Copy,
                        scale=sig_sb[:, gts:gts + 1])

        for e in range(16):
            for m in range(11):
                wr = wring[m % 2]
                p.w(G, "tensor")
                p.admas(G, wr[:].rearrange("p (c j) -> p c j", c=16),
                        ins["egu_t"][e, :, m, :, :])
                p.w(T, "dgpsimd", "scalar")
                for c in range(16):
                    mm(PS[0][:, 0:512], wr[:, c * 128:(c + 1) * 128],
                       xlnt[:, c * NT: c * NT + 512],
                       start=(c == 0), stop=(c == 15))
                    mm(PS[1][:, 0:512], wr[:, c * 128:(c + 1) * 128],
                       xlnt[:, c * NT + 512: c * NT + NT],
                       start=(c == 0), stop=(c == 15))
                p.w(S, "tensor")
                scp(gh[:, m * NT: (m + 1) * NT], PSB[0][:, 0:1024])
            for m in range(11):
                wr = wring[m % 2]
                p.w(G, "tensor")
                p.admas(G, wr[:].rearrange("p (c j) -> p c j", c=16),
                        ins["egu_t"][e, :, 11 + m, :, :])
                p.w(T, "dgpsimd", "vector")
                for c in range(16):
                    mm(PS[0][:, 0:512], wr[:, c * 128:(c + 1) * 128],
                       xlnt[:, c * NT: c * NT + 512],
                       start=(c == 0), stop=(c == 15))
                    mm(PS[1][:, 0:512], wr[:, c * 128:(c + 1) * 128],
                       xlnt[:, c * NT + 512: c * NT + NT],
                       start=(c == 0), stop=(c == 15))
                p.w(S, "tensor", "vector")
                act(stmp[:], gh[:, m * NT: (m + 1) * NT], AF.Silu)
                p.w(V, "scalar")
                p.a(V, lambda v, m=m: v.tensor_tensor(
                    gh[:, m * NT: (m + 1) * NT],
                    stmp[:], PSB[0][:, 0:1024], ALU.mult))
            for n in range(4):
                p.w(T, "scalar", "vector")
                for k in range(11):
                    dr = dring[k % 2]
                    p.w(G, "tensor")
                    p.admas(G, dr[:], ins["edn_t"][e, k * 128:(k + 1) * 128,
                                                   n * 512:(n + 1) * 512])
                    p.w(T, "dgpsimd")
                    for ts in range(8):
                        mm(PS[ts][:, 0:512],
                           gh[:, k * NT + ts * 128: k * NT + (ts + 1) * 128],
                           dr[:], start=(k == 0), stop=(k == 10))
                p.w(S, "tensor", "vector")
                for ts in range(8):
                    act(ytmp[:], PS[ts][:, 0:512], AF.Copy,
                        scale=comb[:, ts * 16 + e: ts * 16 + e + 1])
                    p.w(V, "scalar")
                    dst = acc[:, ts * HID + n * 512: ts * HID + n * 512 + 512]
                    p.a(V, lambda v, dst=dst: v.tensor_tensor(dst, dst,
                                                              ytmp[:], ALU.add))
                    p.w(S, "vector")
                p.w(V, "self")

        p.join()
        A4.reset()
        lnout4 = A4.take([128, HID], F32)
        xo4 = A4.take([128, HID], F32)
        for t in range(8):
            bi = t % 2
            p.w(SY, "vector", "scalar")
            p.admas(SY, xrt[bi][:], xres1[t * 128:(t + 1) * 128, :])
            p.w(V, "dsync", "self")
            p.w(S, "dsync")
            emit_ln(p, acc[:, t * HID:(t + 1) * HID], lnout4[:], HID, tmp,
                    p.auto["vector"], p.auto["scalar"])
            p.w(V, "self", "dsync")
            p.a(V, lambda v, bi=bi: v.tensor_add(xo4[:], xrt[bi][:],
                                                 lnout4[:]))
            p.w(SY, "vector")
            p.admas(SY, xres2[t * 128:(t + 1) * 128, :], xo4[:])

    def ph_l1_attn():
        A1.reset(); A2.reset(); A4.reset(); AW.reset()
        xl = A1.take([128, 16 * SEQ], BF16)
        w_sb = [AW.take([128, 16 * 128], BF16) for _ in range(3)]
        q_fm = A4.take([128, SEQ], BF16)
        k_fm = A4.take([128, SEQ], BF16)
        v_fm = A4.take([128, SEQ], BF16)
        v_tm = A4.take([128, SEQ], BF16)
        qk_el = A4.take([128, SEQ], BF16)
        obase = A4.take([128, SEQ], BF16)
        r_bf = A4.take([1, SEQ], BF16)
        nbd = A4.take([1, SEQ], F32)
        rrow_f = A4.take([1, 512], F32)
        rr = A4.take([1, 512], F32)
        rbc = A4.take([128, 512], F32)
        ptile = [A4.take([128, 512], BF16) for _ in range(2)]
        alibi = A4.take([128, 8 * 16], F32)
        bqc = A4.take([128, 8], F32)
        bkc = A4.take([128, 8], F32)
        bvc = A4.take([128, 8], F32)

        load_xl(xl)
        for nm, dst in [("bq_l", bqc), ("bk_l", bkc), ("bv_l", bvc),
                        ("alibi_c", alibi)]:
            p.admas(SY, dst[:], ins[nm][:, :])

        wgt_use = [0]
        wload = [0]
        for hl in range(8):
            p.wait_at(G, "tensor", wgt_use[0])
            for wi, wn in enumerate(["wq1", "wk1", "wv1"]):
                p.admas(G, w_sb[wi][:].rearrange("p (c j) -> p c j", c=16),
                        ins[wn][:, hl, :, :])
            wload[0] = p.dmark("gpsimd")
            p.w(SY, "vector")
            p.admas(SY, nbd[0:1, :], ins["negbd"][hl:hl + 1, :])
            if hl == 0:
                p.w(T, "dsync")
            for wi, dst, bc in [(0, q_fm, bqc), (1, k_fm, bkc), (2, v_fm, bvc)]:
                for n in range(2):
                    p.wait_at(T, "dgpsimd", wload[0])
                    p.w(T, "scalar", "vector")
                    for c in range(16):
                        mm(PS[0][:, 0:512], w_sb[wi][:, c * 128:(c + 1) * 128],
                           xl[:, c * SEQ + n * 1024: c * SEQ + n * 1024 + 512],
                           start=(c == 0), stop=(c == 15))
                        mm(PS[1][:, 0:512], w_sb[wi][:, c * 128:(c + 1) * 128],
                           xl[:, c * SEQ + n * 1024 + 512:
                              c * SEQ + n * 1024 + 1024],
                           start=(c == 0), stop=(c == 15))
                    p.w(S, "tensor")
                    p.a(S, lambda s, dsl=dst[:, n * 1024:(n + 1) * 1024],
                        bcc=bc[:, hl:hl + 1]: s.add(dsl, PSB[0][:, 0:1024], bcc))
            wgt_use[0] = p.mark("tensor")
            for c in range(16):
                p.w(T, "scalar")
                p.a(T, lambda t, c=c: t.transpose(
                    PTB[c % 2], v_fm[:, c * 128:(c + 1) * 128], idn_bf[:]))
                p.w(S, "tensor")
                p.a(S, lambda s, c=c: s.copy(v_tm[:, c * 128:(c + 1) * 128],
                                             PTB[c % 2]))
            p.w(V, "scalar")
            p.a(V, lambda v: v.tensor_mul(qk_el[:], q_fm[:], k_fm[:]))
            for n in range(4):
                p.w(T, "vector", "scalar")
                mm(PS[2][0:1, 0:512], ones_cb[:, 0:1],
                   qk_el[:, n * 512:(n + 1) * 512])
                p.w(V, "tensor", "dsync")
                p.a(V, lambda v, n=n: v.scalar_tensor_tensor(
                    rrow_f[0:1, :], PS[2][0:1, 0:512], -1.0,
                    nbd[0:1, n * 512:(n + 1) * 512], ALU.mult, ALU.add))
                p.w(S, "vector")
                p.a(S, lambda s, n=n: s.copy(r_bf[0:1, n * 512:(n + 1) * 512],
                                             rrow_f[0:1, :]))
            p.w(V, "dgpsimd")
            for qb in range(4):
                nkc = (qb + 1) * 4
                for kc in range(nkc):
                    sb = 4 + (kc % 2)
                    p.w(T, "scalar", "vector")
                    mm(PS[sb][:, 0:512], k_fm[:, kc * 128:(kc + 1) * 128],
                       q_fm[:, qb * 512:(qb + 1) * 512], start=True, stop=False)
                    mm(PS[sb][:, 0:512], ones_rb[0:1, :],
                       r_bf[0:1, qb * 512:(qb + 1) * 512],
                       start=False, stop=(kc < qb * 4))
                    if kc >= qb * 4:
                        mm(PS[sb][:, 0:512], idn_bf[:], bmask[kc - qb * 4][:],
                           start=False, stop=True)
                    p.w(S, "tensor")
                    act(ptile[kc % 2][:], PS[sb][:, 0:512], AF.Exp,
                        bias=alibi[:, hl * 16 + kc: hl * 16 + kc + 1],
                        scale=DSCALE)
                    p.w(T, "scalar")
                    mm(PS[2][0:1, 0:512], ones_cb[:, 0:1], ptile[kc % 2][:],
                       start=(kc == 0), stop=(kc == nkc - 1))
                    mm(PS[3][:, 0:512], v_tm[:, kc * 128:(kc + 1) * 128],
                       ptile[kc % 2][:], start=(kc == 0), stop=(kc == nkc - 1))
                p.w(V, "tensor")
                p.a(V, lambda v: v.reciprocal(rr[0:1, 0:512],
                                              PS[2][0:1, 0:512]))
                p.w(T, "vector")
                mm(PS[1][:, 0:512], ones_r[0:1, :], rr[0:1, 0:512])
                p.w(S, "tensor")
                p.a(S, lambda s: s.copy(rbc[:], PS[1][:, 0:512]))
                p.w(V, "scalar")
                p.a(V, lambda v, qb=qb: v.tensor_tensor(
                    obase[:, qb * 512:(qb + 1) * 512], PS[3][:, 0:512],
                    rbc[:], ALU.mult))
            p.w(G, "vector")
            p.admas(G, bass.AP(baseloc, hl * 128 * SEQ,
                               [[SEQ, 128], [1, SEQ]]), obase[:])
        p.w(G, "dgpsimd")
        p.admas(G, bass.AP(sha, BASE0 + half * (NT * SEQ) + 0,
                           [[32768, 64], [1, 32768]]),
                bass.AP(baseloc, 0, [[32768, 64], [1, 32768]]))
        exchange(4)

    def ph_l1ffn():
        A1.reset(); A2.reset(); A4.reset()
        w_sb = A1.take([128, 16 * HID], BF16)
        sc_bc = A2.take([128, HID], F32)
        bi_bc = A2.take([128, HID], F32)
        xlnt = A2.take([128, 16 * 128], BF16)
        xlnf = A4.take([128, HID], F32)
        mixed = A4.take([128, HID], F32)
        lnout = A4.take([128, HID], F32)
        xo = A4.take([128, HID], F32)
        y1 = A4.take([128, NT], F32)

        p.admas(SY, sc_bc[:], ins["scale_r"][:, :].broadcast_to([128, HID]))
        p.admas(SY, bi_bc[:], ins["bias_r"][:, :].broadcast_to([128, HID]))
        for c4 in range(4):
            p.admas(G, w_sb[:].rearrange("p (c j) -> p c j", c=16)
                    [:, 4 * c4:4 * c4 + 4, :],
                    ins["w1t"][:, 4 * c4:4 * c4 + 4, :])
        w_load = p.dmark("gpsimd")
        ps_use = [0, 0]

        for t in range(8):
            bi = t % 2
            p.w(SY, "vector", "scalar")
            p.admas(SY, xrt[bi][:], xres3[t * 128:(t + 1) * 128, :])
            p.w(V, "dsync", "tensor")
            p.w(S, "dsync")
            emit_ln(p, xrt[bi][:], xlnf[:], HID, tmp,
                    p.auto["vector"], p.auto["scalar"])
            p.w(T, "vector")
            for c in range(16):
                cb = c % 2
                p.w(T, "scalar")
                p.a(T, lambda tt, c=c, cb=cb: tt.transpose(
                    PS[4][:, cb * 128:cb * 128 + 128],
                    xlnf[:, c * 128:(c + 1) * 128], idn_f[:]))
                p.w(S, "tensor")
                p.a(S, lambda s, c=c, cb=cb: s.copy(
                    xlnt[:, c * 128:(c + 1) * 128],
                    PS[4][:, cb * 128:cb * 128 + 128]))
            for n in range(2):
                bank = n % 2
                if n == 0:
                    p.w(T, "scalar")
                    if t == 0:
                        p.wait_at(T, "dgpsimd", w_load)
                p.wait_at(T, "vector", ps_use[bank])
                for c in range(16):
                    mm(PS[2 * bank][:, 0:512], xlnt[:, c * 128:(c + 1) * 128],
                       w_sb[:, c * HID + n * 1024: c * HID + n * 1024 + 512],
                       start=(c == 0), stop=(c == 15))
                    mm(PS[2 * bank + 1][:, 0:512], xlnt[:, c * 128:(c + 1) * 128],
                       w_sb[:, c * HID + n * 1024 + 512: c * HID + n * 1024 + 1024],
                       start=(c == 0), stop=(c == 15))
                p.w(V, "tensor", "dsync")
                p.a(V, lambda v, n=n, bank=bank: v.tensor_tensor(
                    y1[:], PSB[bank][:, 0:1024], sc_bc[:, n * 1024:(n + 1) * 1024],
                    ALU.mult))
                p.w(V, "self")
                p.a(V, lambda v, n=n: v.tensor_tensor(
                    mixed[:, n * 1024:(n + 1) * 1024], y1[:],
                    bi_bc[:, n * 1024:(n + 1) * 1024], ALU.add))
                ps_use[bank] = p.mark("vector")
            p.w(V, "self")
            p.w(S, "vector")
            emit_ln(p, mixed[:], lnout[:], HID, tmp,
                    p.auto["vector"], p.auto["scalar"])
            p.w(V, "self", "dsync")
            p.a(V, lambda v, bi=bi: v.tensor_add(xo[:], xrt[bi][:],
                                                 lnout[:]))
            p.w(SY, "vector")
            p.admas(SY, xout[t * 128:(t + 1) * 128, :], xo[:])
        p.wait(SY, p.adma[SY], p.adma[SY].n)
        p.wait(G, p.adma[G], p.adma[G].n)

    ph_lntr(ins["xin"], 1)
    ph_l0_attn()
    p.join()
    ph_mix("l0", ins["xin"], xres1, "wo0")
    p.join()
    ph_moe()
    p.join()
    ph_lntr(xres2, 3)
    ph_l1_attn()
    p.join()
    ph_mix("l1", xres2, xres3, "wo1")
    p.join()
    ph_l1ffn()
    if DEBUG_TAPS:
        p.w(G, "dsync")
        for nm, src in [("dbg_x1", xres1), ("dbg_x2", xres2),
                        ("dbg_x3", xres3)]:
            d = nc.dram_tensor(nm, [NT, HID], F32, kind="ExternalOutput")
            p.admas(G, d[:, :], src[:, :])
        dshx = nc.dram_tensor("dbg_shx", [2 * SHX_REG], BF16,
                              kind="ExternalOutput")
        p.admas(G, bass.AP(dshx, 0, [[32768, 128], [1, 32768]]),
                bass.AP(shx, 0, [[32768, 128], [1, 32768]]))
        dsha = nc.dram_tensor("dbg_sha", [SHA_SIZE], BF16,
                              kind="ExternalOutput")
        p.admas(G, bass.AP(dsha, 0, [[32768, 256], [1, 32768]]),
                bass.AP(sha, 0, [[32768, 256], [1, 32768]]))
        p.wait(G, p.adma[G], p.adma[G].n)
    p.finalize()
    return nc



BF = ml_dtypes.bfloat16


def make_maps(kw):
    C = host_constants()
    f32 = np.float32
    g = {k: np.ascontiguousarray(np.asarray(v, dtype=f32)) for k, v in kw.items()}
    X = g["L0_attn_hidden_states"]
    H = 16
    slopes = (2.0 ** (-8.0 / H)) ** np.arange(1, H + 1)
    dist = np.arange(2048, dtype=np.float64) - 2047.0
    egu_t = np.ascontiguousarray(
        g["L0_ffn_expert_gate_up"].transpose(0, 2, 1)
        .reshape(16, 16, 128, 22, 128).transpose(0, 2, 3, 1, 4).astype(BF))
    edn_t = np.ascontiguousarray(g["L0_ffn_expert_down"].transpose(0, 2, 1).astype(BF))
    sgu_t = np.ascontiguousarray(
        g["L0_ffn_shared_gate_up"].T
        .reshape(16, 128, 88, 128).transpose(1, 2, 0, 3).astype(BF))
    sdn_t = np.ascontiguousarray(g["L0_ffn_shared_down"].T.astype(BF))
    sgv_l = np.ascontiguousarray(g["L0_ffn_shared_gate_vec"][0].reshape(16, 128).T)
    gw_t = np.ascontiguousarray(
        g["L0_ffn_gate_w"].T.reshape(16, 128, 16).transpose(1, 0, 2).astype(BF))
    w1t = np.ascontiguousarray(
        g["L1_ffn_weight"].T.reshape(16, 128, 2048).transpose(1, 0, 2).astype(BF))
    nonce = np.array([[np.random.randint(1, 2 ** 28)]], dtype=np.int32)

    def _wl(w, nj):
        return np.ascontiguousarray(
            w.reshape(16, 128, nj, 128).transpose(1, 2, 0, 3).astype(BF))
    wo0_l = np.ascontiguousarray(
        g["L0_attn_Wo"].reshape(16, 128, 2048).transpose(1, 0, 2).astype(BF))
    wo1_l = np.ascontiguousarray(
        g["L1_attn_Wo"].reshape(16, 128, 2048).transpose(1, 0, 2).astype(BF))

    maps = []
    for c in range(8):
        b, h = c // 2, c % 2
        hsl = slice(h * 1024, (h + 1) * 1024)
        alibi_c = np.zeros((128, 8 * 16), np.float32)
        negbd = np.zeros((8, 2048), np.float32)
        for hh in range(8):
            gh_ = h * 8 + hh
            ac = slopes[gh_] * dist
            alibi_c[:, hh * 16:(hh + 1) * 16] = ac.reshape(16, 128).T
            negbd[hh] = -(slopes[gh_] * dist) / DSCALE
        m = {
            "xin": np.ascontiguousarray(X[b, hsl]),
            "nonce": nonce,
            "wq0": _wl(g["L0_attn_Wq"][:, hsl], 8),
            "wk0": _wl(g["L0_attn_Wk"][:, h * 512:(h + 1) * 512], 4),
            "wv0": _wl(g["L0_attn_Wv"][:, h * 512:(h + 1) * 512], 4),
            "wg": g["L0_attn_Wg"],
            "bg": g["L0_attn_bg"].reshape(1, 1),
            "wo0": wo0_l,
            "gw_t": gw_t, "egu_t": egu_t, "edn_t": edn_t, "sgu_t": sgu_t,
            "sdn_t": sdn_t, "sgv_l": sgv_l,
            "wq1": _wl(g["L1_attn_Wq"][:, hsl], 8),
            "wk1": _wl(g["L1_attn_Wk"][:, hsl], 8),
            "wv1": _wl(g["L1_attn_Wv"][:, hsl], 8),
            "bq_l": np.ascontiguousarray(g["L1_attn_bq"][hsl].reshape(8, 128).T),
            "bk_l": np.ascontiguousarray(g["L1_attn_bk"][hsl].reshape(8, 128).T),
            "bv_l": np.ascontiguousarray(g["L1_attn_bv"][hsl].reshape(8, 128).T),
            "alibi_c": alibi_c, "negbd": negbd,
            "wo1": wo1_l,
            "bo": g["L1_attn_bo"].reshape(1, -1),
            "w1t": w1t,
            "scale_r": g["L1_ffn_scale_weight"].reshape(1, -1),
            "bias_r": g["L1_ffn_bias"].reshape(1, -1),
        }
        for n in ["c_idn", "c_idn_f32", "c_uincl", "c_ones_row",
                  "c_ones_row_bf", "c_ones_col_bf", "c_m_sl", "c_m_su",
                  "c_m_iu", "c_bmask0", "c_bmask1", "c_bmask2", "c_bmask3"]:
            m[n] = C[n]
        maps.append(m)
    return maps


def kernel(**kw):
    maps = make_maps(kw)
    nc = bass.Bass("TRN2", target_bir_lowering=False)
    build_fused(nc)
    res = run_bass_kernel_spmd(nc, maps, core_ids=list(range(8)))
    out = np.stack([
        np.concatenate([res.results[2 * b]["xout"],
                        res.results[2 * b + 1]["xout"]], axis=0)
        for b in range(4)])
    return out.astype(np.float32)


# version 12
# speedup vs baseline: 1.4108x; 1.1311x over previous
import contextlib
import numpy as np
import ml_dtypes
import concourse.bass as bass
import concourse.mybir as mybir
from concourse.bass_utils import run_bass_kernel_spmd

F32 = mybir.dt.float32
BF16 = mybir.dt.bfloat16
I32 = mybir.dt.int32
AF = mybir.ActivationFunctionType
ALU = mybir.AluOpType
AX = mybir.AxisListType

ENGINES = ("tensor", "vector", "scalar", "gpsimd", "sync")

DEBUG_TAPS = False

LN_EPS = 1e-5
QK_SHIFT = 12.0
SC_SHIFT = 40.0
DSCALE = float(1.0 / np.sqrt(128.0))

SEQ, HID, NT = 2048, 2048, 1024
SHX_REG = 2048 * 1024
FLA0 = 0
BASE0 = 2 * SHX_REG
SHA_SIZE = 4 * SHX_REG


class Sem:
    def __init__(self, handle):
        self.h = handle
        self.n = 0

    def bump(self, k):
        self.n += k
        return self.n


class Prog:
    def __init__(self, nc):
        self.nc = nc
        self.ops = {e: [] for e in ENGINES}
        self.es = contextlib.ExitStack()
        self._semid = 0

    def sem(self, name=None):
        self._semid += 1
        h = self.es.enter_context(self.nc.semaphore(name or f"s{self._semid}"))
        return Sem(h)

    def sbuf(self, name, shape, dtype):
        return self.es.enter_context(self.nc.sbuf_tensor(name, shape, dtype))

    def psum(self, name, shape, dtype=F32):
        return self.es.enter_context(self.nc.psum_tensor(name, shape, dtype))

    def reg(self, name):
        return self.es.enter_context(self.nc.gpsimd.register(name))

    def wait(self, eng, sem, val=None):
        v = sem.n if val is None else val
        if v <= 0:
            return
        self.ops[eng].append(lambda e, h=sem.h, v=v: e.wait_ge(h, v))

    def emit(self, eng, fn, inc=None, inc_k=1):
        if inc is not None:
            inc.bump(inc_k)

            def wrapped(e, fn=fn, h=inc.h, k=inc_k):
                ins = fn(e)
                ins.then_inc(h, k)
            self.ops[eng].append(wrapped)
        else:
            self.ops[eng].append(fn)

    def mk_auto(self):
        self.auto = {e: self.sem(f"auto_{e}") for e in
                     ("tensor", "vector", "scalar", "gpsimd")}
        self.adma = {e: self.sem(f"adma_{e}") for e in ("sync", "gpsimd",
                                                        "scalar")}
        self.xch = self.sem("xch")

    def a(self, eng, fn):
        self.emit(eng, fn, inc=self.auto[eng])

    def admas(self, eng, out_ap, in_ap):
        self.emit(eng, lambda e, o=out_ap, i=in_ap: e.dma_start(o, i),
                  inc=self.adma[eng], inc_k=16)

    def w(self, eng, *srcs):
        for s in srcs:
            if s == "self":
                s = eng
            if s == "xch":
                self.wait(eng, self.xch)
                continue
            if s.startswith("d"):
                sem = self.adma[s[1:]]
            else:
                sem = self.auto[s]
            self.wait(eng, sem)

    def mark(self, eng):
        return self.auto[eng].n

    def dmark(self, eng):
        return self.adma[eng].n

    def wait_at(self, eng, kind, val):
        sem = self.adma[kind[1:]] if kind.startswith("d") else self.auto[kind]
        self.wait(eng, sem, val)

    def join(self):
        sems = list(self.auto.values()) + list(self.adma.values()) + [self.xch]
        for e in ENGINES:
            for s in sems:
                self.wait(e, s)

    def finalize(self):
        with self.nc.Block() as block:
            for eng in ENGINES:
                ops = self.ops[eng]
                if not ops:
                    continue
                getattr(block, eng)(lambda h, ops=ops: [f(h) for f in ops])
        self.es.close()


class Arena:

    def __init__(self, p, name, kb):
        self.t = p.sbuf(name, [128, kb * 256], F32)
        self.kb = kb
        self.off = 0

    def reset(self):
        self.off = 0

    def take(self, shape, dtype):
        esz = 4 if dtype == F32 else (2 if dtype == BF16 else 4)
        nb = shape[1] * esz
        off = (self.off + 63) & ~63
        self.off = off + nb
        assert self.off <= self.kb * 1024, \
            f"arena overflow: {self.off} > {self.kb * 1024}"
        sl = self.t[0:shape[0], off // 4: off // 4 + ((nb + 3) // 4)]
        if dtype == F32:
            return sl[:, 0:shape[1]]
        return sl.bitcast(dtype)[:, 0:shape[1]]


def host_constants():
    c = {}
    c["c_idn"] = np.eye(128, dtype=ml_dtypes.bfloat16)
    c["c_idn_f32"] = np.eye(128, dtype=np.float32)
    c["c_uincl"] = np.triu(np.ones((128, 128), np.float32))
    c["c_ones_row"] = np.ones((1, 128), np.float32)
    c["c_ones_row_bf"] = np.ones((1, 128), dtype=ml_dtypes.bfloat16)
    c["c_ones_col_bf"] = np.ones((128, 1), dtype=ml_dtypes.bfloat16)
    NEG = -1e30
    c["c_m_sl"] = np.where(np.tri(128, 128, -1, dtype=bool), 0., NEG).astype(np.float32)
    c["c_m_su"] = np.where(np.triu(np.ones((128, 128), bool), 1), 0., NEG).astype(np.float32)
    c["c_m_iu"] = np.where(np.triu(np.ones((128, 128), bool), 0), 0., NEG).astype(np.float32)
    for r in range(4):
        k_idx = np.arange(128)[:, None]
        q_idx = np.arange(512)[None, :]
        m = np.where(k_idx <= q_idx - 128 * r, 0., NEG)
        c[f"c_bmask{r}"] = m.astype(ml_dtypes.bfloat16)
    return c



def make_ln_tmp(p, W):
    return {
        "sq": p.sbuf("ln_sq", [128, W], F32),
        "s1": p.sbuf("ln_s1", [128, 1], F32),
        "ssq": p.sbuf("ln_ssq", [128, 1], F32),
        "m": p.sbuf("ln_m", [128, 1], F32),
        "m2": p.sbuf("ln_m2", [128, 1], F32),
        "var": p.sbuf("ln_var", [128, 1], F32),
        "rstd": p.sbuf("ln_rstd", [128, 1], F32),
        "weps": p.sbuf("ln_weps", [128, 1], F32),
        "lnwh": p.sbuf("ln_lnwh", [128, 1], F32),
        "W": W,
    }


def emit_ln(p, xt, out_tile, W, tmp, sv, ss, sums=None, out_slice=None):
    inv = 1.0 / W
    if sums is None:
        p.emit("vector", lambda v: v.reduce_sum(
            tmp["s1"][:], xt, axis=AX.X), inc=sv)
        sums = tmp["s1"]
    p.wait("scalar", sv)
    p.emit("scalar", lambda s: s.activation(
        tmp["sq"][:, 0:W], xt, AF.Square, accum_out=tmp["ssq"][:]), inc=ss)
    p.wait("vector", sv)
    p.wait("vector", ss)
    p.emit("vector", lambda v: v.tensor_scalar(
        tmp["m"][:], sums[:], inv, None, ALU.mult))
    p.emit("vector", lambda v: v.tensor_tensor(
        tmp["m2"][:], sums[:], sums[:], ALU.mult), inc=sv)
    p.wait("vector", sv)
    p.emit("vector", lambda v: v.scalar_tensor_tensor(
        tmp["var"][:], tmp["m2"][:], -inv, tmp["ssq"][:], ALU.mult, ALU.add),
        inc=sv)
    p.wait("scalar", sv)
    p.emit("scalar", lambda s: s.activation(
        tmp["var"][:], tmp["var"][:], AF.Ln, bias=tmp["weps"][:]), inc=ss)
    p.wait("scalar", ss)
    p.emit("scalar", lambda s: s.activation(
        tmp["rstd"][:], tmp["var"][:], AF.Exp, scale=-0.5,
        bias=tmp["lnwh"][:]), inc=ss)
    p.wait("vector", ss)
    dst = out_tile if out_slice is None else out_slice
    p.emit("vector", lambda v: v.tensor_scalar(
        dst, xt, tmp["m"][:], tmp["rstd"][:], ALU.subtract, ALU.mult), inc=sv)



def build_fused(nc):
    p = Prog(nc)
    p.mk_auto()
    T, V, S, G, SY = "tensor", "vector", "scalar", "gpsimd", "sync"

    E, I_, SI = 16, 1408, 5632

    ins = {n: nc.dram_tensor(n, list(s), d, kind="ExternalInput")
           for n, (s, d) in {
        "xin": ([NT, HID], F32),
        "nonce": ([1, 1], I32),
        "wq0": ([128, 8, 16, 128], BF16),
        "wk0": ([128, 4, 16, 128], BF16),
        "wv0": ([128, 4, 16, 128], BF16),
        "wg": ([128, 1], F32),
        "bg": ([1, 1], F32),
        "wo0": ([128, 16, HID], BF16),
        "gw_t": ([128, 16, 16], BF16),
        "egu_t": ([E, 128, 22, 16, 128], BF16),
        "edn_t": ([E, I_, HID], BF16),
        "sgu_t": ([128, 88, 16, 128], BF16),
        "sdn_t": ([SI, HID], BF16),
        "sgv_l": ([128, 16], F32),
        "wq1": ([128, 8, 16, 128], BF16),
        "wk1": ([128, 8, 16, 128], BF16),
        "wv1": ([128, 8, 16, 128], BF16),
        "bq_l": ([128, 8], F32),
        "bk_l": ([128, 8], F32),
        "bv_l": ([128, 8], F32),
        "alibi_c": ([128, 8 * 16], F32),
        "negbd": ([8, SEQ], F32),
        "wo1": ([128, 16, HID], BF16),
        "bo": ([1, HID], F32),
        "w1t": ([128, 16, HID], BF16),
        "scale_r": ([1, HID], F32),
        "bias_r": ([1, HID], F32),
        "c_idn": ([128, 128], BF16),
        "c_idn_f32": ([128, 128], F32),
        "c_uincl": ([128, 128], F32),
        "c_ones_row": ([1, 128], F32),
        "c_ones_row_bf": ([1, 128], BF16),
        "c_ones_col_bf": ([128, 1], BF16),
        "c_m_sl": ([128, 128], F32),
        "c_m_su": ([128, 128], F32),
        "c_m_iu": ([128, 128], F32),
        "c_bmask0": ([128, 512], BF16),
        "c_bmask1": ([128, 512], BF16),
        "c_bmask2": ([128, 512], BF16),
        "c_bmask3": ([128, 512], BF16),
    }.items()}

    xout = nc.dram_tensor("xout", [NT, HID], F32, kind="ExternalOutput")
    xres1 = nc.dram_tensor("xres1", [NT, HID], F32)
    xres2 = nc.dram_tensor("xres2", [NT, HID], F32)
    xres3 = nc.dram_tensor("xres3", [NT, HID], F32)
    flaloc = nc.dram_tensor("flaloc", [SEQ * NT], BF16)
    baseloc = nc.dram_tensor("baseloc", [NT * SEQ], BF16)
    shx = nc.dram_tensor("shx", [2 * SHX_REG], BF16, addr_space="Shared")
    sha = nc.dram_tensor("sha", [SHA_SIZE], BF16, addr_space="Shared")
    flags = nc.dram_tensor("flags", [2, 8], I32, addr_space="Shared")

    pid = nc.partition_id()
    half = pid % 2
    oth = 1 - half

    A1 = Arena(p, "A1", 64)
    A2 = Arena(p, "A2", 32)
    A4 = Arena(p, "A4", 60)
    AW = Arena(p, "AW", 16)
    xrt = [p.sbuf(f"xrt{i}", [128, HID], F32) for i in range(2)]
    tmp = make_ln_tmp(p, HID)
    nt_sb = p.sbuf("nt_sb", [1, 1], I32)
    idn_bf = p.sbuf("idn_bf", [128, 128], BF16)
    idn_f = p.sbuf("idn_f", [128, 128], F32)
    uincl = p.sbuf("uincl_s", [128, 128], F32)
    ones_r = p.sbuf("ones_r", [1, 128], F32)
    ones_rb = p.sbuf("ones_rb", [1, 128], BF16)
    ones_cb = p.sbuf("ones_cb", [128, 1], BF16)
    m_sl = p.sbuf("m_sl", [128, 128], F32)
    m_su = p.sbuf("m_su", [128, 128], F32)
    m_iu = p.sbuf("m_iu", [128, 128], F32)
    bmask = [p.sbuf(f"bmask{r}", [128, 512], BF16) for r in range(4)]
    PSB = [p.psum(f"PSB{i}", [128, 1024], F32) for i in range(4)]
    PS = [PSB[i // 2][:, (i % 2) * 512:(i % 2) * 512 + 512] for i in range(8)]
    PTB = [PS[6][:, 0:64].bitcast(BF16), PS[7][:, 0:64].bitcast(BF16)]

    nr = p.reg("nr")
    pr = p.reg("pr")
    pr2 = p.reg("pr2")
    fr = p.reg("fr")
    f2r = p.reg("f2r")
    cr = p.reg("cr")

    def mm(out_ap, l_ap, r_ap, start=True, stop=True):
        p.a(T, lambda t, o=out_ap, a=l_ap, b=r_ap, st=start, sp=stop:
            t.matmul(o, a, b, start=st, stop=sp))

    def scp(out_ap, in_ap):
        p.a(S, lambda s, o=out_ap, i=in_ap: s.copy(o, i))

    def act(out_ap, in_ap, f, bias=0.0, scale=1.0, accum=None):
        if accum is None:
            p.a(S, lambda s, o=out_ap, i=in_ap, ff=f, b=bias, sc=scale:
                s.activation(o, i, ff, bias=b, scale=sc))
        else:
            p.a(S, lambda s, o=out_ap, i=in_ap, ff=f, b=bias, sc=scale,
                ac=accum: s.activation(o, i, ff, bias=b, scale=sc,
                                       accum_out=ac))

    def dynap(handle, off_rv, rows, cols, row_stride):
        return bass.AP(handle, off_rv, [[row_stride, rows], [1, cols]])

    def exchange(phase):
        p.w(G, "dgpsimd")

        def do_x(g, ph=phase):
            g.reg_add(pr, nr, ph)
            g.store(dynap(flags, half * 8 + (ph - 1), 1, 1, 8), pr)
            g.reg_add(pr2, pr, pr)
            g.reg_mov(cr, 1)
            with g.While(cr):
                g.load(fr, flags[0:1, ph - 1:ph])
                g.load(f2r, flags[1:2, ph - 1:ph])
                g.reg_add(cr, fr, f2r)
                g.reg_sub(cr, cr, pr2)
        p.ops[G].append(do_x)
        p.emit(G, lambda g: g.nop(), inc=p.xch)
        p.join()

    for nm, dst in [("c_idn", idn_bf), ("c_idn_f32", idn_f),
                    ("c_uincl", uincl), ("c_ones_row", ones_r),
                    ("c_ones_row_bf", ones_rb), ("c_ones_col_bf", ones_cb),
                    ("c_m_sl", m_sl), ("c_m_su", m_su), ("c_m_iu", m_iu)]:
        p.admas(SY, dst[:], ins[nm][:, :])
    for r in range(4):
        p.admas(SY, bmask[r][:], ins[f"c_bmask{r}"][:, :])
    p.admas(G, nt_sb[:], ins["nonce"][:, :])
    p.w(G, "dgpsimd")
    p.ops[G].append(lambda g: g.reg_load(nr, nt_sb[0:1, 0:1]))
    p.a(V, lambda v: v.memset(tmp["weps"][:], HID * LN_EPS))
    p.a(V, lambda v: v.memset(tmp["lnwh"][:], 0.5 * float(np.log(HID))))

    def ph_lntr(src, phase):
        A4.reset(); A2.reset()
        xln = A4.take([128, HID], BF16)
        strips = A2.take([128, 16 * NT], BF16)
        sv, ss = p.auto["vector"], p.auto["scalar"]
        tr_done = [0]
        for t in range(8):
            buf = xrt[t % 2]
            p.w(SY, "vector", "scalar")
            p.admas(SY, buf[:], src[t * 128:(t + 1) * 128, :])
            p.w(V, "dsync")
            p.w(S, "dsync")
            if t > 0:
                p.w(V, "tensor")
            emit_ln(p, buf[:], xln[:], HID, tmp, sv, ss)
            p.w(T, "vector")
            if t == 0:
                p.w(T, "dsync")
            for hc in range(16):
                bk = tr_done[0] % 2
                p.w(T, "scalar")
                p.a(T, lambda tt, bk=bk, hc=hc: tt.transpose(
                    PTB[bk], xln[:, hc * 128:(hc + 1) * 128], idn_bf[:]))
                tr_done[0] += 1
                p.w(S, "tensor")
                scp(strips[:, hc * NT + t * 128: hc * NT + (t + 1) * 128],
                    PTB[bk])
        p.w(G, "scalar")
        p.admas(G, bass.AP(shx, half * SHX_REG,
                           [[NT, 128], [128 * NT, 16], [1, NT]]),
                strips.rearrange("p (hc t) -> p hc t", hc=16))
        exchange(phase)

    def load_xl(xl):
        for hh in range(2):
            for c in range(16):
                p.admas(G, xl[:, c * SEQ + hh * NT: c * SEQ + hh * NT + NT],
                        bass.AP(shx, hh * SHX_REG + c * 128 * NT,
                                [[NT, 128], [1, NT]]))

    def ph_l0_attn():
        A4.reset(); A2.reset(); AW.reset(); A1.reset()
        xl = A1.take([128, 16 * SEQ], BF16)
        wq_sb = [AW.take([128, 16 * 128], BF16) for _ in range(2)]
        wk_sb = AW.take([128, 16 * 128], BF16)
        wv_sb = AW.take([128, 16 * 128], BF16)
        k_fm = A2.take([128, SEQ], BF16)
        v_fm = A2.take([128, SEQ], BF16)
        v_tm = A2.take([128, SEQ], BF16)
        q_fm = [A2.take([128, SEQ], BF16) for _ in range(2)]
        eq_fm = [A2.take([128, SEQ], BF16) for _ in range(2)]
        ek_fm = A2.take([128, SEQ], BF16)
        ek_tm = A4.take([128, SEQ], BF16)
        fla_j = A4.take([128, 16 * 256], BF16)
        obase = A4.take([128, SEQ], BF16)
        lnk = A4.take([1, SEQ], F32)
        lnq = [A4.take([1, SEQ], F32) for _ in range(2)]
        ptile = [A4.take([128, 512], BF16) for _ in range(2)]
        e_sb = A4.take([128, 512], F32)
        rbc = A4.take([128, 512], F32)
        pow_sb = A4.take([128, 5 * 128], BF16)
        powt_sb = A4.take([128, 6 * 128], BF16)
        s_f32 = [A4.take([128, 128], F32) for _ in range(2)]
        xbuf = [A4.take([128, 128], BF16) for _ in range(2)]
        u_bf = A4.take([128, 128], BF16)
        r_bf = A4.take([128, 128], BF16)
        kt_bf = A4.take([128, 128], BF16)
        t_bf = A4.take([128, 128], BF16)
        tt_bf = A4.take([128, 128], BF16)
        pt_bf = [A4.take([128, 128], BF16) for _ in range(2)]
        s_bf = A4.take([128, 128], BF16)
        zt_sb = A4.take([128, 128], F32)
        rrow = A4.take([1, 512], F32)
        arow = A4.take([1, 128], F32)
        bk_r = A4.take([1, 128], F32)
        nck_r = A4.take([1, 128], F32)
        bq_r = [A4.take([1, 128], F32) for _ in range(2)]
        g_sb = A4.take([128, 16], F32)
        gt1 = A4.take([128, 16], F32)
        gt2 = A4.take([128, 16], F32)
        gt3 = A4.take([128, 16], F32)
        gs_sb = A4.take([128, 16], F32)
        cols_sb = A4.take([128, 6], F32)
        negb = A4.take([128, 1], F32)
        kscale = A4.take([128, 1], F32)
        bqe = [A4.take([128, 1], F32) for _ in range(2)]
        ace = A4.take([128, 1], F32)
        neg12 = A4.take([128, 1], F32)
        neg40 = A4.take([128, 1], F32)
        one_t = A4.take([128, 1], F32)
        wg_bf = A4.take([128, 1], BF16)
        bgb = A4.take([128, 1], F32)

        load_xl(xl)
        p.admas(G, wg_bf[:], ins["wg"][:, :])
        p.admas(SY, bgb[:], ins["bg"][:, :].broadcast_to([128, 1]))
        p.a(V, lambda v: v.memset(neg12[:], -QK_SHIFT))
        p.a(V, lambda v: v.memset(neg40[:], -SC_SHIFT))
        p.a(V, lambda v: v.memset(one_t[:], 1.0))

        wload = [0]

        def proj(dst_sb, w_sb, nblocks=2):
            for n in range(nblocks):
                p.wait_at(T, "dgpsimd", wload[0])
                p.w(T, "scalar")
                for c in range(16):
                    mm(PS[0][:, 0:512],
                       w_sb[:, c * 128:(c + 1) * 128],
                       xl[:, c * SEQ + n * 1024: c * SEQ + n * 1024 + 512],
                       start=(c == 0), stop=(c == 15))
                    mm(PS[1][:, 0:512],
                       w_sb[:, c * 128:(c + 1) * 128],
                       xl[:, c * SEQ + n * 1024 + 512: c * SEQ + n * 1024 + 1024],
                       start=(c == 0), stop=(c == 15))
                p.w(S, "tensor")
                scp(dst_sb[:, n * 1024:(n + 1) * 1024], PSB[0][:, 0:1024])

        wgt_use = [0]
        for j in range(4):
            p.wait_at(G, "tensor", wgt_use[0])
            p.admas(G, wk_sb[:].rearrange("p (c j) -> p c j", c=16),
                    ins["wk0"][:, j, :, :])
            p.admas(G, wv_sb[:].rearrange("p (c j) -> p c j", c=16),
                    ins["wv0"][:, j, :, :])
            for dh in range(2):
                p.admas(G, wq_sb[dh][:].rearrange("p (c j) -> p c j", c=16),
                        ins["wq0"][:, 2 * j + dh, :, :])
            wload[0] = p.dmark("gpsimd")
            if j == 0:
                p.w(T, "dsync")
            proj(k_fm, wk_sb)
            proj(v_fm, wv_sb)
            proj(q_fm[0], wq_sb[0])
            proj(q_fm[1], wq_sb[1])
            wgt_use[0] = p.mark("tensor")
            for c in range(16):
                p.w(T, "scalar")
                p.a(T, lambda t, c=c: t.transpose(
                    PTB[c % 2], v_fm[:, c * 128:(c + 1) * 128], idn_bf[:]))
                p.w(S, "tensor")
                scp(v_tm[:, c * 128:(c + 1) * 128], PTB[c % 2])
            p.w(S, "tensor", "vector")
            act(ek_fm[:], k_fm[:], AF.Exp, bias=neg12[:])
            act(eq_fm[0][:], q_fm[0][:], AF.Exp, bias=neg12[:])
            act(eq_fm[1][:], q_fm[1][:], AF.Exp, bias=neg12[:])
            for c in range(16):
                p.w(T, "scalar")
                p.a(T, lambda t, c=c: t.transpose(
                    PTB[c % 2], ek_fm[:, c * 128:(c + 1) * 128], idn_bf[:]))
                p.w(S, "tensor")
                scp(ek_tm[:, c * 128:(c + 1) * 128], PTB[c % 2])
            for (src, dstrow) in [(ek_fm, lnk), (eq_fm[0], lnq[0]),
                                  (eq_fm[1], lnq[1])]:
                for n in range(4):
                    p.w(T, "scalar")
                    mm(PS[2][0:1, 0:512], ones_cb[:, 0:1],
                       src[:, n * 512:(n + 1) * 512])
                    p.w(S, "tensor")
                    act(dstrow[0:1, n * 512:(n + 1) * 512],
                        PS[2][0:1, 0:512], AF.Ln)
            p.w(T, "scalar", "dgpsimd")
            for c in range(16):
                mm(PS[2][:, c:c + 1], k_fm[:, c * 128:(c + 1) * 128],
                   wg_bf[:, 0:1])
            p.w(V, "tensor", "dsync")
            p.a(V, lambda v: v.tensor_scalar(g_sb[:], PS[2][:, 0:16], bgb[:],
                                             None, ALU.add))
            p.w(S, "vector")
            act(gt1[:], g_sb[:], AF.Abs)
            p.w(S, "self")
            act(gt2[:], gt1[:], AF.Exp, scale=-1.0)
            p.w(S, "self")
            act(gt3[:], gt2[:], AF.Ln, bias=one_t[:])
            p.w(V, "scalar")
            p.a(V, lambda v: v.tensor_scalar_min(gt1[:], g_sb[:], 0.0))
            p.w(V, "self")
            p.a(V, lambda v: v.tensor_sub(gt2[:], gt1[:], gt3[:]))
            p.w(V, "self")
            p.a(V, lambda v: v.tensor_scalar_mul(gs_sb[:], gt2[:], 1.0 / 16.0))

            for dh in range(2):
                hl = 2 * j + dh
                p.w(V, "dgpsimd")
                for qb in range(4):
                    nkc = (qb + 1) * 4
                    for kc in range(nkc):
                        sb = 4 + (kc % 2)
                        p.w(T, "scalar", "vector")
                        mm(PS[sb][:, 0:512], k_fm[:, kc * 128:(kc + 1) * 128],
                           q_fm[dh][:, qb * 512:(qb + 1) * 512],
                           start=True, stop=(kc < qb * 4))
                        if kc >= qb * 4:
                            mm(PS[sb][:, 0:512], idn_bf[:],
                               bmask[kc - qb * 4][:], start=False, stop=True)
                        p.w(S, "tensor")
                        act(ptile[kc % 2][:], PS[sb][:, 0:512], AF.Exp,
                            bias=neg40[:], scale=DSCALE)
                        p.w(T, "scalar")
                        mm(PS[2][0:1, 0:512], ones_cb[:, 0:1],
                           ptile[kc % 2][:], start=(kc == 0),
                           stop=(kc == nkc - 1))
                        mm(PS[3][:, 0:512], v_tm[:, kc * 128:(kc + 1) * 128],
                           ptile[kc % 2][:], start=(kc == 0),
                           stop=(kc == nkc - 1))
                    p.w(V, "tensor")
                    p.a(V, lambda v: v.reciprocal(rrow[0:1, 0:512],
                                                  PS[2][0:1, 0:512]))
                    p.w(T, "vector")
                    mm(PS[1][:, 0:512], ones_r[0:1, :], rrow[0:1, 0:512])
                    p.w(S, "tensor")
                    scp(rbc[:], PS[1][:, 0:512])
                    p.w(V, "scalar")
                    p.a(V, lambda v, qb=qb: v.tensor_tensor(
                        obase[:, qb * 512:(qb + 1) * 512], PS[3][:, 0:512],
                        rbc[:], ALU.mult))
                p.w(G, "vector")
                p.admas(G, bass.AP(baseloc, hl * 128 * SEQ,
                                   [[SEQ, 128], [1, SEQ]]), obase[:])

            p.w(V, "dgpsimd")
            for c in range(16):
                cs = slice(c * 128, (c + 1) * 128)
                p.w(T, "vector", "scalar")
                mm(PS[5][0:1, 128:256], gs_sb[:, c:c + 1], uincl[:])
                p.w(S, "tensor")
                scp(arow[0:1, :], PS[5][0:1, 128:256])
                p.w(V, "scalar")
                p.a(V, lambda v, cs=cs: v.tensor_sub(bk_r[0:1, :],
                                                     arow[0:1, :],
                                                     lnk[0:1, cs]))
                p.a(V, lambda v, cs=cs: v.scalar_tensor_tensor(
                    nck_r[0:1, :], arow[0:1, :], -1.0, lnk[0:1, cs],
                    ALU.mult, ALU.subtract))
                p.a(V, lambda v, cs=cs: v.tensor_sub(bq_r[0][0:1, :],
                                                     arow[0:1, :],
                                                     lnq[0][0:1, cs]))
                p.a(V, lambda v, cs=cs: v.tensor_sub(bq_r[1][0:1, :],
                                                     arow[0:1, :],
                                                     lnq[1][0:1, cs]))
                p.w(T, "vector")
                mm(PS[5][:, 0:1], bk_r[0:1, :], ones_r[0:1, 0:1])
                mm(PS[5][:, 1:2], nck_r[0:1, :], ones_r[0:1, 0:1])
                mm(PS[5][:, 2:3], bq_r[0][0:1, :], ones_r[0:1, 0:1])
                mm(PS[5][:, 3:4], bq_r[1][0:1, :], ones_r[0:1, 0:1])
                mm(PS[5][:, 4:5], ones_r[0:1, :], arow[0:1, 127:128])
                p.w(S, "tensor")
                scp(cols_sb[:, 0:5], PS[5][:, 0:5])
                p.w(S, "self")
                act(negb[:], cols_sb[:, 0:1], AF.Exp)
                p.w(V, "scalar")
                p.a(V, lambda v: v.tensor_scalar_mul(negb[:], negb[:], -1.0))
                p.a(V, lambda v: v.tensor_add(cols_sb[:, 5:6],
                                              cols_sb[:, 4:5],
                                              cols_sb[:, 1:2]))
                p.w(S, "vector")
                act(kscale[:], cols_sb[:, 5:6], AF.Exp)
                act(bqe[0][:], cols_sb[:, 2:3], AF.Exp)
                act(bqe[1][:], cols_sb[:, 3:4], AF.Exp)
                act(ace[:], cols_sb[:, 4:5], AF.Exp)
                p.w(T, "scalar")
                mm(PS[0][:, 0:128], bk_r[0:1, :], ones_r[0:1, :], True, False)
                mm(PS[0][:, 0:128], ones_r[0:1, :], nck_r[0:1, :], False, False)
                mm(PS[0][:, 0:128], idn_f[:], m_sl[:], False, True)
                mm(PS[0][:, 128:256], nck_r[0:1, :], ones_r[0:1, :], True, False)
                mm(PS[0][:, 128:256], ones_r[0:1, :], bq_r[0][0:1, :], False, False)
                mm(PS[0][:, 128:256], idn_f[:], m_iu[:], False, True)
                mm(PS[0][:, 256:384], nck_r[0:1, :], ones_r[0:1, :], True, False)
                mm(PS[0][:, 256:384], ones_r[0:1, :], bq_r[1][0:1, :], False, False)
                mm(PS[0][:, 256:384], idn_f[:], m_iu[:], False, True)
                mm(PS[0][:, 384:512], nck_r[0:1, :], ones_r[0:1, :], True, False)
                mm(PS[0][:, 384:512], ones_r[0:1, :], bk_r[0:1, :], False, False)
                mm(PS[0][:, 384:512], idn_f[:], m_su[:], False, True)
                p.w(S, "tensor")
                act(e_sb[:], PS[0][:, 0:512], AF.Exp)
                p.w(T, "scalar")
                mm(PS[1][:, 0:128], ek_fm[:, cs], ek_fm[:, cs])
                mm(PS[1][:, 128:256], ek_fm[:, cs], eq_fm[0][:, cs])
                mm(PS[1][:, 256:384], ek_fm[:, cs], eq_fm[1][:, cs])
                p.w(V, "tensor", "scalar")
                p.a(V, lambda v: v.tensor_tensor(t_bf[:], e_sb[:, 0:128],
                                                 PS[1][:, 0:128], ALU.mult))
                p.a(V, lambda v: v.tensor_tensor(pt_bf[0][:],
                                                 e_sb[:, 128:256],
                                                 PS[1][:, 128:256], ALU.mult))
                p.a(V, lambda v: v.tensor_tensor(pt_bf[1][:],
                                                 e_sb[:, 256:384],
                                                 PS[1][:, 256:384], ALU.mult))
                p.a(V, lambda v: v.tensor_tensor(tt_bf[:], e_sb[:, 384:512],
                                                 PS[1][:, 0:128], ALU.mult))
                p.w(T, "vector")
                prevX, prevXt = t_bf[:], tt_bf[:]
                for lev in range(1, 6):
                    mm(PS[2][:, 0:128], prevXt, prevX)
                    mm(PS[2][:, 128:256], prevX, prevXt)
                    p.w(S, "tensor")
                    scp(pow_sb[:, (lev - 1) * 128:lev * 128], PS[2][:, 0:128])
                    scp(powt_sb[:, (lev - 1) * 128:lev * 128],
                        PS[2][:, 128:256])
                    p.w(T, "scalar")
                    prevX = pow_sb[:, (lev - 1) * 128:lev * 128]
                    prevXt = powt_sb[:, (lev - 1) * 128:lev * 128]
                mm(PS[2][:, 0:128], pow_sb[:, 4 * 128:5 * 128],
                   powt_sb[:, 4 * 128:5 * 128])
                p.w(S, "tensor")
                scp(powt_sb[:, 5 * 128:6 * 128], PS[2][:, 0:128])
                if c > 0:
                    p.w(T, "scalar", "vector")
                    mm(PS[3][:, 0:128], ek_fm[:, cs], s_bf[:])
                    p.w(V, "tensor")
                    p.a(V, lambda v, cs=cs: v.scalar_tensor_tensor(
                        r_bf[:], PS[3][:, 0:128], negb[:], v_tm[:, cs],
                        ALU.mult, ALU.add))
                cur = (v_tm[:, cs] if c == 0 else r_bf[:])
                xb = 0
                for lev in [5, 4, 3, 2, 1]:
                    p.w(T, "scalar", "vector")
                    mm(PS[3][:, 128:256],
                       powt_sb[:, lev * 128:(lev + 1) * 128], cur)
                    p.w(V, "tensor")
                    p.a(V, lambda v, xb=xb, cur=cur: v.tensor_tensor(
                        xbuf[xb][:], cur, PS[3][:, 128:256], ALU.add))
                    cur = xbuf[xb][:]
                    xb = 1 - xb
                p.w(T, "vector")
                mm(PS[3][:, 128:256], powt_sb[:, 0:128], cur)
                p.w(V, "tensor")
                p.a(V, lambda v, cur=cur: v.tensor_tensor(
                    xbuf[xb][:], cur, PS[3][:, 128:256], ALU.add))
                cur = xbuf[xb][:]
                p.w(T, "vector")
                mm(PS[3][:, 128:256], tt_bf[:], cur)
                p.w(V, "tensor")
                p.a(V, lambda v, cur=cur: v.tensor_sub(u_bf[:], cur,
                                                       PS[3][:, 128:256]))
                p.w(T, "vector")
                if c > 0:
                    mm(PS[3][:, 256:384], eq_fm[0][:, cs], s_bf[:])
                    mm(PS[3][:, 384:512], eq_fm[1][:, cs], s_bf[:])
                mm(PS[4][:, 0:128], pt_bf[0][:], u_bf[:])
                mm(PS[4][:, 128:256], pt_bf[1][:], u_bf[:])
                p.w(V, "tensor")
                for dh in range(2):
                    dst = fla_j[:, c * 256 + dh * 128: c * 256 + dh * 128 + 128]
                    if c > 0:
                        p.a(V, lambda v, dh=dh: v.tensor_scalar(
                            zt_sb[:], PS[3][:, 256 + dh * 128:384 + dh * 128],
                            bqe[dh][:], DSCALE, ALU.mult, ALU.mult))
                        p.w(V, "self")
                        p.a(V, lambda v, dh=dh, dst=dst: v.scalar_tensor_tensor(
                            dst, PS[4][:, dh * 128:(dh + 1) * 128], DSCALE,
                            zt_sb[:], ALU.mult, ALU.add))
                    else:
                        p.a(V, lambda v, dh=dh, dst=dst: v.tensor_scalar(
                            dst, PS[4][:, dh * 128:(dh + 1) * 128], DSCALE,
                            None, ALU.mult))
                if c < 15:
                    p.w(V, "scalar")
                    p.a(V, lambda v, cs=cs: v.tensor_scalar(
                        kt_bf[:], ek_tm[:, cs], kscale[:], None, ALU.mult))
                    p.w(T, "vector")
                    mm(PS[5][:, 128:256], kt_bf[:], u_bf[:])
                    p.w(V, "tensor")
                    cu = c % 2
                    if c == 0:
                        p.a(V, lambda v, cu=cu: v.tensor_copy(
                            s_f32[cu][:], PS[5][:, 128:256]))
                    else:
                        p.a(V, lambda v, cu=cu: v.scalar_tensor_tensor(
                            s_f32[cu][:], s_f32[1 - cu][:], ace[:],
                            PS[5][:, 128:256], ALU.mult, ALU.add))
                    p.w(S, "vector")
                    scp(s_bf[:], s_f32[cu][:])
            p.w(G, "vector")
            p.admas(G, bass.AP(flaloc, j * 256,
                               [[NT, 128], [128 * NT, 16], [1, 256]]),
                    fla_j.rearrange("p (c t) -> p c t", c=16))
        p.w(G, "dgpsimd")
        p.admas(G, bass.AP(sha, FLA0 + half * SHX_REG + 0,
                           [[32768, 64], [1, 32768]]),
                bass.AP(flaloc, 0, [[32768, 64], [1, 32768]]))
        p.admas(G, bass.AP(sha, BASE0 + half * (NT * SEQ) + 0,
                           [[32768, 64], [1, 32768]]),
                bass.AP(baseloc, 0, [[32768, 64], [1, 32768]]))
        exchange(2)

    def ph_mix(mode, xsrc, xdst, wo_name):
        A1.reset(); A2.reset(); A4.reset()
        wo_sb = A1.take([128, 16 * HID], BF16)
        bs_sb = A2.take([128, 16 * NT], BF16)
        fla_t = [A4.take([128, HID], BF16) for _ in range(2)]
        mixed = A4.take([128, HID], F32)
        lnout = A4.take([128, HID], F32)
        xo = A4.take([128, HID], F32)
        fs = A4.take([128, NT], F32)
        bo_bc = A4.take([128, HID], F32) if mode == "l1" else None

        for c4 in range(4):
            p.admas(G, wo_sb[:].rearrange("p (c j) -> p c j", c=16)
                    [:, 4 * c4:4 * c4 + 4, :],
                    ins[wo_name][:, 4 * c4:4 * c4 + 4, :])
        for hh in range(2):
            p.admas(G, bs_sb[:, hh * 8 * NT:(hh + 1) * 8 * NT]
                    .rearrange("p (cb t) -> p cb t", cb=8),
                    bass.AP(sha, BASE0 + hh * (NT * SEQ) + half * NT,
                            [[SEQ, 128], [128 * SEQ, 8], [1, NT]]))
        if mode == "l0":
            for hh in range(2):
                p.admas(G, bass.AP(flaloc, hh * NT, [[SEQ, NT], [1, NT]]),
                        bass.AP(sha, FLA0 + hh * SHX_REG + half * (NT * NT),
                                [[NT, NT], [1, NT]]))
            p.w(G, "dgpsimd")
        if mode == "l1":
            p.admas(SY, bo_bc[:], ins["bo"][:, :].broadcast_to([128, HID]))
        wbs_load = p.dmark("gpsimd")
        ps_use = [0, 0]
        fla_use = [0, 0]

        for t in range(8):
            bi = t % 2
            p.w(SY, "vector", "scalar")
            p.admas(SY, xrt[bi][:], xsrc[t * 128:(t + 1) * 128, :])
            if mode == "l0":
                p.wait_at(G, "vector", fla_use[bi])
                p.admas(G, fla_t[bi][:],
                        bass.AP(flaloc, t * 128 * SEQ, [[SEQ, 128], [1, SEQ]]))
            fla_mark = p.dmark("gpsimd")
            p.w(V, "scalar")
            for n in range(2):
                bank = n % 2
                p.wait_at(T, "vector", ps_use[bank])
                if t == 0 and n == 0:
                    p.wait_at(T, "dgpsimd", wbs_load)
                for c in range(16):
                    mm_l = bs_sb[:, c * NT + t * 128: c * NT + (t + 1) * 128]
                    mm_r = wo_sb[:, c * HID + n * 1024: c * HID + n * 1024 + 512]
                    mm_r2 = wo_sb[:, c * HID + n * 1024 + 512:
                                  c * HID + n * 1024 + 1024]
                    p.a(T, lambda tt, o=PS[2 * bank][:, 0:512], a=mm_l, b=mm_r,
                        st=(c == 0), sp=(c == 15): tt.matmul(o, a, b,
                                                             start=st, stop=sp))
                    p.a(T, lambda tt, o=PS[2 * bank + 1][:, 0:512], a=mm_l,
                        b=mm_r2, st=(c == 0), sp=(c == 15):
                        tt.matmul(o, a, b, start=st, stop=sp))
                p.w(V, "tensor", "dsync")
                p.wait_at(V, "dgpsimd", fla_mark)
                if mode == "l0":
                    p.a(V, lambda v, bi=bi, n=n: v.tensor_scalar(
                        fs[:], fla_t[bi][:, n * 1024:(n + 1) * 1024], 0.01,
                        None, ALU.mult))
                    p.w(V, "self")
                    p.a(V, lambda v, n=n, bank=bank: v.scalar_tensor_tensor(
                        mixed[:, n * 1024:(n + 1) * 1024], PSB[bank][:, 0:1024],
                        0.99, fs[:], ALU.mult, ALU.add))
                else:
                    p.a(V, lambda v, n=n, bank=bank: v.tensor_tensor(
                        mixed[:, n * 1024:(n + 1) * 1024], PSB[bank][:, 0:1024],
                        bo_bc[:, n * 1024:(n + 1) * 1024], ALU.add))
                ps_use[bank] = p.mark("vector")
            fla_use[bi] = p.mark("vector")
            p.w(V, "self")
            p.w(S, "vector")
            emit_ln(p, mixed[:], lnout[:], HID, tmp,
                    p.auto["vector"], p.auto["scalar"])
            p.w(V, "self", "dsync")
            p.a(V, lambda v, bi=bi: v.tensor_add(xo[:], xrt[bi][:],
                                                 lnout[:]))
            p.w(SY, "vector")
            p.admas(SY, xdst[t * 128:(t + 1) * 128, :], xo[:])

    def ph_moe():
        A1.reset(); A2.reset(); A4.reset(); AW.reset()
        acc = A1.take([128, 8 * HID], F32)
        xlnt = A2.take([128, 16 * NT], BF16)
        gh = A4.take([128, 44 * 512], BF16)
        xlnf = A4.take([128, HID], F32)
        stmp = A4.take([128, NT], BF16)
        ytmp = A4.take([128, 512], F32)
        comb = A4.take([128, 8 * 16], F32)
        probs = A4.take([128, 16], F32)
        wrk = A4.take([128, 16], F32)
        eqt = A4.take([128, 16], F32)
        mx = A4.take([128, 4], F32)
        sig_sb = A4.take([128, 8], F32)
        s4 = A4.take([128, 1], F32)
        rs4 = A4.take([128, 1], F32)
        negm = A4.take([128, 1], F32)
        lnout = xlnf
        wring = [AW.take([128, 16 * 128], BF16) for _ in range(2)]
        dring = [AW.take([128, 1024], BF16) for _ in range(2)]
        gwt_sb = AW.take([128, 16 * 16], BF16)
        sgv_sb = AW.take([128, 16], BF16)

        p.admas(G, gwt_sb[:].rearrange("p (c j) -> p c j", c=16),
                ins["gw_t"][:, :, :])
        p.admas(G, sgv_sb[:], ins["sgv_l"][:, :])

        for t in range(8):
            bi = t % 2
            p.w(SY, "vector", "scalar")
            p.admas(SY, xrt[bi][:], xres1[t * 128:(t + 1) * 128, :])
            p.w(V, "dsync")
            p.w(S, "dsync")
            p.w(V, "tensor")
            emit_ln(p, xrt[bi][:], xlnf[:], HID, tmp,
                    p.auto["vector"], p.auto["scalar"])
            p.w(T, "vector", "dsync")
            for c in range(16):
                cb = c % 2
                p.w(T, "scalar")
                p.a(T, lambda tt, c=c, cb=cb: tt.transpose(
                    PS[2][:, cb * 128:cb * 128 + 128],
                    xlnf[:, c * 128:(c + 1) * 128], idn_f[:]))
                p.w(S, "tensor")
                scp(xlnt[:, c * NT + t * 128: c * NT + (t + 1) * 128],
                    PS[2][:, cb * 128:cb * 128 + 128])
            p.w(T, "scalar", "dgpsimd")
            for c in range(16):
                mm(PS[0][:, 0:16], xlnt[:, c * NT + t * 128: c * NT + (t + 1) * 128],
                   gwt_sb[:, c * 16:(c + 1) * 16], start=(c == 0),
                   stop=(c == 15))
            p.w(V, "tensor")
            p.a(V, lambda v: v.reduce_max(negm[:], PS[0][:, 0:16], axis=AX.X,
                                          negate=True))
            p.w(S, "vector")
            act(probs[:], PS[0][:, 0:16], AF.Exp, bias=negm[:])
            p.w(V, "scalar")
            p.a(V, lambda v: v.tensor_copy(wrk[:], probs[:]))
            for jj in range(4):
                p.w(V, "self")
                p.a(V, lambda v, jj=jj: v.reduce_max(mx[:, jj:jj + 1], wrk[:],
                                                     axis=AX.X))
                if jj < 3:
                    p.w(V, "self")
                    p.a(V, lambda v, jj=jj: v.tensor_scalar(
                        eqt[:], wrk[:], mx[:, jj:jj + 1], None, ALU.is_ge))
                    p.w(V, "self")
                    p.a(V, lambda v: v.scalar_tensor_tensor(
                        wrk[:], eqt[:], -1e30, wrk[:], ALU.mult, ALU.add))
            p.w(V, "self")
            p.a(V, lambda v: v.tensor_scalar(eqt[:], probs[:], mx[:, 3:4],
                                             None, ALU.is_ge))
            p.w(V, "self")
            p.a(V, lambda v: v.tensor_mul(wrk[:], probs[:], eqt[:]))
            p.w(V, "self")
            p.a(V, lambda v: v.reduce_sum(s4[:], wrk[:], axis=AX.X))
            p.w(V, "self")
            p.a(V, lambda v: v.reciprocal(rs4[:], s4[:]))
            p.w(V, "self")
            p.a(V, lambda v, t=t: v.tensor_scalar(
                comb[:, t * 16:(t + 1) * 16], wrk[:], rs4[:], None, ALU.mult))
            p.w(T, "vector")
            for c in range(16):
                mm(PS[1][:, 0:1], xlnt[:, c * NT + t * 128: c * NT + (t + 1) * 128],
                   sgv_sb[:, c:c + 1], start=(c == 0), stop=(c == 15))
            p.w(S, "tensor")
            act(sig_sb[:, t:t + 1], PS[1][:, 0:1], AF.Sigmoid)

        for blk in range(2):
            tok0 = blk * 512
            for m in range(44):
                wr = wring[m % 2]
                p.w(G, "tensor")
                p.admas(G, wr[:].rearrange("p (c j) -> p c j", c=16),
                        ins["sgu_t"][:, m, :, :])
                p.w(T, "dgpsimd", "scalar")
                for c in range(16):
                    mm(PS[0][:, 0:512], wr[:, c * 128:(c + 1) * 128],
                       xlnt[:, c * NT + tok0: c * NT + tok0 + 512],
                       start=(c == 0), stop=(c == 15))
                p.w(S, "tensor")
                scp(gh[:, m * 512:(m + 1) * 512], PS[0][:, 0:512])
            for m in range(44):
                wr = wring[m % 2]
                p.w(G, "tensor")
                p.admas(G, wr[:].rearrange("p (c j) -> p c j", c=16),
                        ins["sgu_t"][:, 44 + m, :, :])
                p.w(T, "dgpsimd", "vector")
                for c in range(16):
                    mm(PS[0][:, 0:512], wr[:, c * 128:(c + 1) * 128],
                       xlnt[:, c * NT + tok0: c * NT + tok0 + 512],
                       start=(c == 0), stop=(c == 15))
                p.w(S, "tensor", "vector")
                act(stmp[:, 0:512], gh[:, m * 512:(m + 1) * 512], AF.Silu)
                p.w(V, "scalar", "tensor")
                p.a(V, lambda v, m=m: v.tensor_tensor(
                    gh[:, m * 512:(m + 1) * 512], stmp[:, 0:512],
                    PS[0][:, 0:512], ALU.mult))
            for n in range(4):
                p.w(T, "scalar", "vector")
                for k2 in range(22):
                    dr = dring[k2 % 2]
                    p.w(G, "tensor")
                    p.admas(G, dr[:].rearrange("p (kk j) -> p kk j", kk=2),
                            ins["sdn_t"][k2 * 256:(k2 + 1) * 256,
                                         n * 512:(n + 1) * 512]
                            .rearrange("(kk p) j -> p kk j", p=128))
                    p.w(T, "dgpsimd")
                    for kk in range(2):
                        k = 2 * k2 + kk
                        for ts in range(4):
                            mm(PS[4 + ts][:, 0:512],
                               gh[:, k * 512 + ts * 128: k * 512 + (ts + 1) * 128],
                               dr[:, kk * 512:(kk + 1) * 512],
                               start=(k == 0), stop=(k == 43))
                p.w(S, "tensor")
                for ts in range(4):
                    gts = blk * 4 + ts
                    act(acc[:, gts * HID + n * 512: gts * HID + n * 512 + 512],
                        PS[4 + ts][:, 0:512], AF.Copy,
                        scale=sig_sb[:, gts:gts + 1])

        eg_s = [0, 0]
        eu_v = [0, 0]
        for e in range(16):
            for m in range(11):
                wr = wring[m % 2]
                bk = m % 2
                if m == 0:
                    eg_s[0] = eg_s[1] = p.mark("scalar")
                p.w(G, "tensor")
                p.admas(G, wr[:].rearrange("p (c j) -> p c j", c=16),
                        ins["egu_t"][e, :, m, :, :])
                p.w(T, "dgpsimd")
                p.wait_at(T, "scalar", eg_s[bk])
                for c in range(16):
                    mm(PS[2 * bk][:, 0:512], wr[:, c * 128:(c + 1) * 128],
                       xlnt[:, c * NT: c * NT + 512],
                       start=(c == 0), stop=(c == 15))
                    mm(PS[2 * bk + 1][:, 0:512], wr[:, c * 128:(c + 1) * 128],
                       xlnt[:, c * NT + 512: c * NT + NT],
                       start=(c == 0), stop=(c == 15))
                p.w(S, "tensor")
                scp(gh[:, m * NT: (m + 1) * NT], PSB[bk][:, 0:1024])
                eg_s[bk] = p.mark("scalar")
            for m in range(11):
                wr = wring[m % 2]
                bk = m % 2
                p.w(G, "tensor")
                p.admas(G, wr[:].rearrange("p (c j) -> p c j", c=16),
                        ins["egu_t"][e, :, 11 + m, :, :])
                p.w(T, "dgpsimd")
                p.wait_at(T, "vector", eu_v[bk])
                p.wait_at(T, "scalar", eg_s[bk])
                for c in range(16):
                    mm(PS[2 * bk][:, 0:512], wr[:, c * 128:(c + 1) * 128],
                       xlnt[:, c * NT: c * NT + 512],
                       start=(c == 0), stop=(c == 15))
                    mm(PS[2 * bk + 1][:, 0:512], wr[:, c * 128:(c + 1) * 128],
                       xlnt[:, c * NT + 512: c * NT + NT],
                       start=(c == 0), stop=(c == 15))
                p.w(S, "tensor", "vector")
                act(stmp[:], gh[:, m * NT: (m + 1) * NT], AF.Silu)
                p.w(V, "scalar")
                p.a(V, lambda v, m=m, bk=bk: v.tensor_tensor(
                    gh[:, m * NT: (m + 1) * NT],
                    stmp[:], PSB[bk][:, 0:1024], ALU.mult))
                eu_v[bk] = p.mark("vector")
            for n in range(4):
                p.w(T, "scalar", "vector")
                for k2 in range(6):
                    dr = dring[k2 % 2]
                    nk = 2 if k2 < 5 else 1
                    p.w(G, "tensor")
                    p.admas(G, dr[:, 0:nk * 512]
                            .rearrange("p (kk j) -> p kk j", kk=nk),
                            ins["edn_t"][e, k2 * 256: k2 * 256 + nk * 128,
                                         n * 512:(n + 1) * 512]
                            .rearrange("(kk p) j -> p kk j", p=128))
                    p.w(T, "dgpsimd")
                    for kk in range(nk):
                        k = 2 * k2 + kk
                        for ts in range(8):
                            mm(PS[ts][:, 0:512],
                               gh[:, k * NT + ts * 128: k * NT + (ts + 1) * 128],
                               dr[:, kk * 512:(kk + 1) * 512],
                               start=(k == 0), stop=(k == 10))
                p.w(S, "tensor", "vector")
                for ts in range(8):
                    act(ytmp[:], PS[ts][:, 0:512], AF.Copy,
                        scale=comb[:, ts * 16 + e: ts * 16 + e + 1])
                    p.w(V, "scalar")
                    dst = acc[:, ts * HID + n * 512: ts * HID + n * 512 + 512]
                    p.a(V, lambda v, dst=dst: v.tensor_tensor(dst, dst,
                                                              ytmp[:], ALU.add))
                    p.w(S, "vector")
                p.w(V, "self")

        p.join()
        A4.reset()
        lnout4 = A4.take([128, HID], F32)
        xo4 = A4.take([128, HID], F32)
        for t in range(8):
            bi = t % 2
            p.w(SY, "vector", "scalar")
            p.admas(SY, xrt[bi][:], xres1[t * 128:(t + 1) * 128, :])
            p.w(V, "dsync", "self")
            p.w(S, "dsync")
            emit_ln(p, acc[:, t * HID:(t + 1) * HID], lnout4[:], HID, tmp,
                    p.auto["vector"], p.auto["scalar"])
            p.w(V, "self", "dsync")
            p.a(V, lambda v, bi=bi: v.tensor_add(xo4[:], xrt[bi][:],
                                                 lnout4[:]))
            p.w(SY, "vector")
            p.admas(SY, xres2[t * 128:(t + 1) * 128, :], xo4[:])

    def ph_l1_attn():
        A1.reset(); A2.reset(); A4.reset(); AW.reset()
        xl = A1.take([128, 16 * SEQ], BF16)
        w_sb = [AW.take([128, 16 * 128], BF16) for _ in range(3)]
        q_fm = A4.take([128, SEQ], BF16)
        k_fm = A4.take([128, SEQ], BF16)
        v_fm = A4.take([128, SEQ], BF16)
        v_tm = A4.take([128, SEQ], BF16)
        qk_el = A4.take([128, SEQ], BF16)
        obase = A4.take([128, SEQ], BF16)
        r_bf = A4.take([1, SEQ], BF16)
        nbd = A4.take([1, SEQ], F32)
        rrow_f = A4.take([1, 512], F32)
        rr = A4.take([1, 512], F32)
        rbc = A4.take([128, 512], F32)
        ptile = [A4.take([128, 512], BF16) for _ in range(2)]
        alibi = A4.take([128, 8 * 16], F32)
        bqc = A4.take([128, 8], F32)
        bkc = A4.take([128, 8], F32)
        bvc = A4.take([128, 8], F32)

        load_xl(xl)
        for nm, dst in [("bq_l", bqc), ("bk_l", bkc), ("bv_l", bvc),
                        ("alibi_c", alibi)]:
            p.admas(SY, dst[:], ins[nm][:, :])

        wgt_use = [0]
        wload = [0]
        for hl in range(8):
            p.wait_at(G, "tensor", wgt_use[0])
            for wi, wn in enumerate(["wq1", "wk1", "wv1"]):
                p.admas(G, w_sb[wi][:].rearrange("p (c j) -> p c j", c=16),
                        ins[wn][:, hl, :, :])
            wload[0] = p.dmark("gpsimd")
            p.w(SY, "vector")
            p.admas(SY, nbd[0:1, :], ins["negbd"][hl:hl + 1, :])
            if hl == 0:
                p.w(T, "dsync")
            for wi, dst, bc in [(0, q_fm, bqc), (1, k_fm, bkc), (2, v_fm, bvc)]:
                for n in range(2):
                    p.wait_at(T, "dgpsimd", wload[0])
                    p.w(T, "scalar", "vector")
                    for c in range(16):
                        mm(PS[0][:, 0:512], w_sb[wi][:, c * 128:(c + 1) * 128],
                           xl[:, c * SEQ + n * 1024: c * SEQ + n * 1024 + 512],
                           start=(c == 0), stop=(c == 15))
                        mm(PS[1][:, 0:512], w_sb[wi][:, c * 128:(c + 1) * 128],
                           xl[:, c * SEQ + n * 1024 + 512:
                              c * SEQ + n * 1024 + 1024],
                           start=(c == 0), stop=(c == 15))
                    p.w(S, "tensor")
                    p.a(S, lambda s, dsl=dst[:, n * 1024:(n + 1) * 1024],
                        bcc=bc[:, hl:hl + 1]: s.add(dsl, PSB[0][:, 0:1024], bcc))
            wgt_use[0] = p.mark("tensor")
            for c in range(16):
                p.w(T, "scalar")
                p.a(T, lambda t, c=c: t.transpose(
                    PTB[c % 2], v_fm[:, c * 128:(c + 1) * 128], idn_bf[:]))
                p.w(S, "tensor")
                p.a(S, lambda s, c=c: s.copy(v_tm[:, c * 128:(c + 1) * 128],
                                             PTB[c % 2]))
            p.w(V, "scalar")
            p.a(V, lambda v: v.tensor_mul(qk_el[:], q_fm[:], k_fm[:]))
            for n in range(4):
                p.w(T, "vector", "scalar")
                mm(PS[2][0:1, 0:512], ones_cb[:, 0:1],
                   qk_el[:, n * 512:(n + 1) * 512])
                p.w(V, "tensor", "dsync")
                p.a(V, lambda v, n=n: v.scalar_tensor_tensor(
                    rrow_f[0:1, :], PS[2][0:1, 0:512], -1.0,
                    nbd[0:1, n * 512:(n + 1) * 512], ALU.mult, ALU.add))
                p.w(S, "vector")
                p.a(S, lambda s, n=n: s.copy(r_bf[0:1, n * 512:(n + 1) * 512],
                                             rrow_f[0:1, :]))
            p.w(V, "dgpsimd")
            for qb in range(4):
                nkc = (qb + 1) * 4
                for kc in range(nkc):
                    sb = 4 + (kc % 2)
                    p.w(T, "scalar", "vector")
                    mm(PS[sb][:, 0:512], k_fm[:, kc * 128:(kc + 1) * 128],
                       q_fm[:, qb * 512:(qb + 1) * 512], start=True, stop=False)
                    mm(PS[sb][:, 0:512], ones_rb[0:1, :],
                       r_bf[0:1, qb * 512:(qb + 1) * 512],
                       start=False, stop=(kc < qb * 4))
                    if kc >= qb * 4:
                        mm(PS[sb][:, 0:512], idn_bf[:], bmask[kc - qb * 4][:],
                           start=False, stop=True)
                    p.w(S, "tensor")
                    act(ptile[kc % 2][:], PS[sb][:, 0:512], AF.Exp,
                        bias=alibi[:, hl * 16 + kc: hl * 16 + kc + 1],
                        scale=DSCALE)
                    p.w(T, "scalar")
                    mm(PS[2][0:1, 0:512], ones_cb[:, 0:1], ptile[kc % 2][:],
                       start=(kc == 0), stop=(kc == nkc - 1))
                    mm(PS[3][:, 0:512], v_tm[:, kc * 128:(kc + 1) * 128],
                       ptile[kc % 2][:], start=(kc == 0), stop=(kc == nkc - 1))
                p.w(V, "tensor")
                p.a(V, lambda v: v.reciprocal(rr[0:1, 0:512],
                                              PS[2][0:1, 0:512]))
                p.w(T, "vector")
                mm(PS[1][:, 0:512], ones_r[0:1, :], rr[0:1, 0:512])
                p.w(S, "tensor")
                p.a(S, lambda s: s.copy(rbc[:], PS[1][:, 0:512]))
                p.w(V, "scalar")
                p.a(V, lambda v, qb=qb: v.tensor_tensor(
                    obase[:, qb * 512:(qb + 1) * 512], PS[3][:, 0:512],
                    rbc[:], ALU.mult))
            p.w(G, "vector")
            p.admas(G, bass.AP(baseloc, hl * 128 * SEQ,
                               [[SEQ, 128], [1, SEQ]]), obase[:])
        p.w(G, "dgpsimd")
        p.admas(G, bass.AP(sha, BASE0 + half * (NT * SEQ) + 0,
                           [[32768, 64], [1, 32768]]),
                bass.AP(baseloc, 0, [[32768, 64], [1, 32768]]))
        exchange(4)

    def ph_l1ffn():
        A1.reset(); A2.reset(); A4.reset()
        w_sb = A1.take([128, 16 * HID], BF16)
        sc_bc = A2.take([128, HID], F32)
        bi_bc = A2.take([128, HID], F32)
        xlnt = A2.take([128, 16 * 128], BF16)
        xlnf = A4.take([128, HID], F32)
        mixed = A4.take([128, HID], F32)
        lnout = A4.take([128, HID], F32)
        xo = A4.take([128, HID], F32)
        y1 = A4.take([128, NT], F32)

        p.admas(SY, sc_bc[:], ins["scale_r"][:, :].broadcast_to([128, HID]))
        p.admas(SY, bi_bc[:], ins["bias_r"][:, :].broadcast_to([128, HID]))
        for c4 in range(4):
            p.admas(G, w_sb[:].rearrange("p (c j) -> p c j", c=16)
                    [:, 4 * c4:4 * c4 + 4, :],
                    ins["w1t"][:, 4 * c4:4 * c4 + 4, :])
        w_load = p.dmark("gpsimd")
        ps_use = [0, 0]

        for t in range(8):
            bi = t % 2
            p.w(SY, "vector", "scalar")
            p.admas(SY, xrt[bi][:], xres3[t * 128:(t + 1) * 128, :])
            p.w(V, "dsync", "tensor")
            p.w(S, "dsync")
            emit_ln(p, xrt[bi][:], xlnf[:], HID, tmp,
                    p.auto["vector"], p.auto["scalar"])
            p.w(T, "vector")
            for c in range(16):
                cb = c % 2
                p.w(T, "scalar")
                p.a(T, lambda tt, c=c, cb=cb: tt.transpose(
                    PS[4][:, cb * 128:cb * 128 + 128],
                    xlnf[:, c * 128:(c + 1) * 128], idn_f[:]))
                p.w(S, "tensor")
                p.a(S, lambda s, c=c, cb=cb: s.copy(
                    xlnt[:, c * 128:(c + 1) * 128],
                    PS[4][:, cb * 128:cb * 128 + 128]))
            for n in range(2):
                bank = n % 2
                if n == 0:
                    p.w(T, "scalar")
                    if t == 0:
                        p.wait_at(T, "dgpsimd", w_load)
                p.wait_at(T, "vector", ps_use[bank])
                for c in range(16):
                    mm(PS[2 * bank][:, 0:512], xlnt[:, c * 128:(c + 1) * 128],
                       w_sb[:, c * HID + n * 1024: c * HID + n * 1024 + 512],
                       start=(c == 0), stop=(c == 15))
                    mm(PS[2 * bank + 1][:, 0:512], xlnt[:, c * 128:(c + 1) * 128],
                       w_sb[:, c * HID + n * 1024 + 512: c * HID + n * 1024 + 1024],
                       start=(c == 0), stop=(c == 15))
                p.w(V, "tensor", "dsync")
                p.a(V, lambda v, n=n, bank=bank: v.tensor_tensor(
                    y1[:], PSB[bank][:, 0:1024], sc_bc[:, n * 1024:(n + 1) * 1024],
                    ALU.mult))
                p.w(V, "self")
                p.a(V, lambda v, n=n: v.tensor_tensor(
                    mixed[:, n * 1024:(n + 1) * 1024], y1[:],
                    bi_bc[:, n * 1024:(n + 1) * 1024], ALU.add))
                ps_use[bank] = p.mark("vector")
            p.w(V, "self")
            p.w(S, "vector")
            emit_ln(p, mixed[:], lnout[:], HID, tmp,
                    p.auto["vector"], p.auto["scalar"])
            p.w(V, "self", "dsync")
            p.a(V, lambda v, bi=bi: v.tensor_add(xo[:], xrt[bi][:],
                                                 lnout[:]))
            p.w(SY, "vector")
            p.admas(SY, xout[t * 128:(t + 1) * 128, :], xo[:])
        p.wait(SY, p.adma[SY], p.adma[SY].n)
        p.wait(G, p.adma[G], p.adma[G].n)

    ph_lntr(ins["xin"], 1)
    ph_l0_attn()
    p.join()
    ph_mix("l0", ins["xin"], xres1, "wo0")
    p.join()
    ph_moe()
    p.join()
    ph_lntr(xres2, 3)
    ph_l1_attn()
    p.join()
    ph_mix("l1", xres2, xres3, "wo1")
    p.join()
    ph_l1ffn()
    if DEBUG_TAPS:
        p.w(G, "dsync")
        for nm, src in [("dbg_x1", xres1), ("dbg_x2", xres2),
                        ("dbg_x3", xres3)]:
            d = nc.dram_tensor(nm, [NT, HID], F32, kind="ExternalOutput")
            p.admas(G, d[:, :], src[:, :])
        dshx = nc.dram_tensor("dbg_shx", [2 * SHX_REG], BF16,
                              kind="ExternalOutput")
        p.admas(G, bass.AP(dshx, 0, [[32768, 128], [1, 32768]]),
                bass.AP(shx, 0, [[32768, 128], [1, 32768]]))
        dsha = nc.dram_tensor("dbg_sha", [SHA_SIZE], BF16,
                              kind="ExternalOutput")
        p.admas(G, bass.AP(dsha, 0, [[32768, 256], [1, 32768]]),
                bass.AP(sha, 0, [[32768, 256], [1, 32768]]))
        p.wait(G, p.adma[G], p.adma[G].n)
    p.finalize()
    return nc



BF = ml_dtypes.bfloat16


def make_maps(kw):
    C = host_constants()
    f32 = np.float32
    g = {k: np.ascontiguousarray(np.asarray(v, dtype=f32)) for k, v in kw.items()}
    X = g["L0_attn_hidden_states"]
    H = 16
    slopes = (2.0 ** (-8.0 / H)) ** np.arange(1, H + 1)
    dist = np.arange(2048, dtype=np.float64) - 2047.0
    egu_t = np.ascontiguousarray(
        g["L0_ffn_expert_gate_up"].transpose(0, 2, 1)
        .reshape(16, 16, 128, 22, 128).transpose(0, 2, 3, 1, 4).astype(BF))
    edn_t = np.ascontiguousarray(g["L0_ffn_expert_down"].transpose(0, 2, 1).astype(BF))
    sgu_t = np.ascontiguousarray(
        g["L0_ffn_shared_gate_up"].T
        .reshape(16, 128, 88, 128).transpose(1, 2, 0, 3).astype(BF))
    sdn_t = np.ascontiguousarray(g["L0_ffn_shared_down"].T.astype(BF))
    sgv_l = np.ascontiguousarray(g["L0_ffn_shared_gate_vec"][0].reshape(16, 128).T)
    gw_t = np.ascontiguousarray(
        g["L0_ffn_gate_w"].T.reshape(16, 128, 16).transpose(1, 0, 2).astype(BF))
    w1t = np.ascontiguousarray(
        g["L1_ffn_weight"].T.reshape(16, 128, 2048).transpose(1, 0, 2).astype(BF))
    nonce = np.array([[np.random.randint(1, 2 ** 28)]], dtype=np.int32)

    def _wl(w, nj):
        return np.ascontiguousarray(
            w.reshape(16, 128, nj, 128).transpose(1, 2, 0, 3).astype(BF))
    wo0_l = np.ascontiguousarray(
        g["L0_attn_Wo"].reshape(16, 128, 2048).transpose(1, 0, 2).astype(BF))
    wo1_l = np.ascontiguousarray(
        g["L1_attn_Wo"].reshape(16, 128, 2048).transpose(1, 0, 2).astype(BF))

    maps = []
    for c in range(8):
        b, h = c // 2, c % 2
        hsl = slice(h * 1024, (h + 1) * 1024)
        alibi_c = np.zeros((128, 8 * 16), np.float32)
        negbd = np.zeros((8, 2048), np.float32)
        for hh in range(8):
            gh_ = h * 8 + hh
            ac = slopes[gh_] * dist
            alibi_c[:, hh * 16:(hh + 1) * 16] = ac.reshape(16, 128).T
            negbd[hh] = -(slopes[gh_] * dist) / DSCALE
        m = {
            "xin": np.ascontiguousarray(X[b, hsl]),
            "nonce": nonce,
            "wq0": _wl(g["L0_attn_Wq"][:, hsl], 8),
            "wk0": _wl(g["L0_attn_Wk"][:, h * 512:(h + 1) * 512], 4),
            "wv0": _wl(g["L0_attn_Wv"][:, h * 512:(h + 1) * 512], 4),
            "wg": g["L0_attn_Wg"],
            "bg": g["L0_attn_bg"].reshape(1, 1),
            "wo0": wo0_l,
            "gw_t": gw_t, "egu_t": egu_t, "edn_t": edn_t, "sgu_t": sgu_t,
            "sdn_t": sdn_t, "sgv_l": sgv_l,
            "wq1": _wl(g["L1_attn_Wq"][:, hsl], 8),
            "wk1": _wl(g["L1_attn_Wk"][:, hsl], 8),
            "wv1": _wl(g["L1_attn_Wv"][:, hsl], 8),
            "bq_l": np.ascontiguousarray(g["L1_attn_bq"][hsl].reshape(8, 128).T),
            "bk_l": np.ascontiguousarray(g["L1_attn_bk"][hsl].reshape(8, 128).T),
            "bv_l": np.ascontiguousarray(g["L1_attn_bv"][hsl].reshape(8, 128).T),
            "alibi_c": alibi_c, "negbd": negbd,
            "wo1": wo1_l,
            "bo": g["L1_attn_bo"].reshape(1, -1),
            "w1t": w1t,
            "scale_r": g["L1_ffn_scale_weight"].reshape(1, -1),
            "bias_r": g["L1_ffn_bias"].reshape(1, -1),
        }
        for n in ["c_idn", "c_idn_f32", "c_uincl", "c_ones_row",
                  "c_ones_row_bf", "c_ones_col_bf", "c_m_sl", "c_m_su",
                  "c_m_iu", "c_bmask0", "c_bmask1", "c_bmask2", "c_bmask3"]:
            m[n] = C[n]
        maps.append(m)
    return maps


def kernel(**kw):
    maps = make_maps(kw)
    nc = bass.Bass("TRN2", target_bir_lowering=False)
    build_fused(nc)
    res = run_bass_kernel_spmd(nc, maps, core_ids=list(range(8)))
    out = np.stack([
        np.concatenate([res.results[2 * b]["xout"],
                        res.results[2 * b + 1]["xout"]], axis=0)
        for b in range(4)])
    return out.astype(np.float32)


# version 13 (final)
# speedup vs baseline: 1.8944x; 1.3428x over previous
import contextlib
import numpy as np
import ml_dtypes
import concourse.bass as bass
import concourse.mybir as mybir
from concourse.bass_utils import run_bass_kernel_spmd

F32 = mybir.dt.float32
BF16 = mybir.dt.bfloat16
I32 = mybir.dt.int32
AF = mybir.ActivationFunctionType
ALU = mybir.AluOpType
AX = mybir.AxisListType

ENGINES = ("tensor", "vector", "scalar", "gpsimd", "sync")

DEBUG_TAPS = False

LN_EPS = 1e-5
QK_SHIFT = 12.0
SC_SHIFT = 40.0
DSCALE = float(1.0 / np.sqrt(128.0))

SEQ, HID, NT = 2048, 2048, 1024
SHX_REG = 2048 * 1024
FLA0 = 0
BASE0 = 2 * SHX_REG
SHA_SIZE = 4 * SHX_REG


class Sem:
    def __init__(self, handle):
        self.h = handle
        self.n = 0

    def bump(self, k):
        self.n += k
        return self.n


class Prog:
    def __init__(self, nc):
        self.nc = nc
        self.ops = {e: [] for e in ENGINES}
        self.es = contextlib.ExitStack()
        self._semid = 0

    def sem(self, name=None):
        self._semid += 1
        h = self.es.enter_context(self.nc.semaphore(name or f"s{self._semid}"))
        return Sem(h)

    def sbuf(self, name, shape, dtype):
        return self.es.enter_context(self.nc.sbuf_tensor(name, shape, dtype))

    def psum(self, name, shape, dtype=F32):
        return self.es.enter_context(self.nc.psum_tensor(name, shape, dtype))

    def reg(self, name):
        return self.es.enter_context(self.nc.gpsimd.register(name))

    def wait(self, eng, sem, val=None):
        v = sem.n if val is None else val
        if v <= 0:
            return
        self.ops[eng].append(lambda e, h=sem.h, v=v: e.wait_ge(h, v))

    def emit(self, eng, fn, inc=None, inc_k=1):
        if inc is not None:
            inc.bump(inc_k)

            def wrapped(e, fn=fn, h=inc.h, k=inc_k):
                ins = fn(e)
                ins.then_inc(h, k)
            self.ops[eng].append(wrapped)
        else:
            self.ops[eng].append(fn)

    def mk_auto(self):
        self.auto = {e: self.sem(f"auto_{e}") for e in
                     ("tensor", "vector", "scalar", "gpsimd")}
        self.adma = {e: self.sem(f"adma_{e}") for e in ("sync", "gpsimd",
                                                        "scalar")}
        self.xch = self.sem("xch")

    def a(self, eng, fn):
        self.emit(eng, fn, inc=self.auto[eng])

    def admas(self, eng, out_ap, in_ap):
        self.emit(eng, lambda e, o=out_ap, i=in_ap: e.dma_start(o, i),
                  inc=self.adma[eng], inc_k=16)

    def w(self, eng, *srcs):
        for s in srcs:
            if s == "self":
                s = eng
            if s == "xch":
                self.wait(eng, self.xch)
                continue
            if s.startswith("d"):
                sem = self.adma[s[1:]]
            else:
                sem = self.auto[s]
            self.wait(eng, sem)

    def mark(self, eng):
        return self.auto[eng].n

    def dmark(self, eng):
        return self.adma[eng].n

    def wait_at(self, eng, kind, val):
        sem = self.adma[kind[1:]] if kind.startswith("d") else self.auto[kind]
        self.wait(eng, sem, val)

    def join(self):
        sems = list(self.auto.values()) + list(self.adma.values()) + [self.xch]
        for e in ENGINES:
            for s in sems:
                self.wait(e, s)

    def finalize(self):
        with self.nc.Block() as block:
            for eng in ENGINES:
                ops = self.ops[eng]
                if not ops:
                    continue
                getattr(block, eng)(lambda h, ops=ops: [f(h) for f in ops])
        self.es.close()


class Arena:

    def __init__(self, p, name, kb):
        self.t = p.sbuf(name, [128, kb * 256], F32)
        self.kb = kb
        self.off = 0

    def reset(self):
        self.off = 0

    def take(self, shape, dtype):
        esz = 4 if dtype == F32 else (2 if dtype == BF16 else 4)
        nb = shape[1] * esz
        off = (self.off + 63) & ~63
        self.off = off + nb
        assert self.off <= self.kb * 1024, \
            f"arena overflow: {self.off} > {self.kb * 1024}"
        sl = self.t[0:shape[0], off // 4: off // 4 + ((nb + 3) // 4)]
        if dtype == F32:
            return sl[:, 0:shape[1]]
        return sl.bitcast(dtype)[:, 0:shape[1]]


def host_constants():
    c = {}
    c["c_idn"] = np.eye(128, dtype=ml_dtypes.bfloat16)
    c["c_idn_f32"] = np.eye(128, dtype=np.float32)
    c["c_uincl"] = np.triu(np.ones((128, 128), np.float32))
    c["c_ones_row"] = np.ones((1, 128), np.float32)
    c["c_ones_row_bf"] = np.ones((1, 128), dtype=ml_dtypes.bfloat16)
    c["c_ones_col_bf"] = np.ones((128, 1), dtype=ml_dtypes.bfloat16)
    NEG = -1e30
    c["c_m_sl"] = np.where(np.tri(128, 128, -1, dtype=bool), 0., NEG).astype(np.float32)
    c["c_m_su"] = np.where(np.triu(np.ones((128, 128), bool), 1), 0., NEG).astype(np.float32)
    c["c_m_iu"] = np.where(np.triu(np.ones((128, 128), bool), 0), 0., NEG).astype(np.float32)
    for r in range(4):
        k_idx = np.arange(128)[:, None]
        q_idx = np.arange(512)[None, :]
        m = np.where(k_idx <= q_idx - 128 * r, 0., NEG)
        c[f"c_bmask{r}"] = m.astype(ml_dtypes.bfloat16)
    return c



def make_ln_tmp(p, W):
    return {
        "sq": p.sbuf("ln_sq", [128, W], F32),
        "s1": p.sbuf("ln_s1", [128, 1], F32),
        "ssq": p.sbuf("ln_ssq", [128, 1], F32),
        "m": p.sbuf("ln_m", [128, 1], F32),
        "m2": p.sbuf("ln_m2", [128, 1], F32),
        "var": p.sbuf("ln_var", [128, 1], F32),
        "rstd": p.sbuf("ln_rstd", [128, 1], F32),
        "weps": p.sbuf("ln_weps", [128, 1], F32),
        "lnwh": p.sbuf("ln_lnwh", [128, 1], F32),
        "W": W,
    }


def emit_ln(p, xt, out_tile, W, tmp, sv, ss, sums=None, out_slice=None):
    inv = 1.0 / W
    if sums is None:
        p.emit("vector", lambda v: v.reduce_sum(
            tmp["s1"][:], xt, axis=AX.X), inc=sv)
        sums = tmp["s1"]
    p.wait("scalar", sv)
    p.emit("scalar", lambda s: s.activation(
        tmp["sq"][:, 0:W], xt, AF.Square, accum_out=tmp["ssq"][:]), inc=ss)
    p.wait("vector", sv)
    p.wait("vector", ss)
    p.emit("vector", lambda v: v.tensor_scalar(
        tmp["m"][:], sums[:], inv, None, ALU.mult))
    p.emit("vector", lambda v: v.tensor_tensor(
        tmp["m2"][:], sums[:], sums[:], ALU.mult), inc=sv)
    p.wait("vector", sv)
    p.emit("vector", lambda v: v.scalar_tensor_tensor(
        tmp["var"][:], tmp["m2"][:], -inv, tmp["ssq"][:], ALU.mult, ALU.add),
        inc=sv)
    p.wait("scalar", sv)
    p.emit("scalar", lambda s: s.activation(
        tmp["var"][:], tmp["var"][:], AF.Ln, bias=tmp["weps"][:]), inc=ss)
    p.wait("scalar", ss)
    p.emit("scalar", lambda s: s.activation(
        tmp["rstd"][:], tmp["var"][:], AF.Exp, scale=-0.5,
        bias=tmp["lnwh"][:]), inc=ss)
    p.wait("vector", ss)
    dst = out_tile if out_slice is None else out_slice
    p.emit("vector", lambda v: v.tensor_scalar(
        dst, xt, tmp["m"][:], tmp["rstd"][:], ALU.subtract, ALU.mult), inc=sv)



def build_fused(nc):
    p = Prog(nc)
    p.mk_auto()
    T, V, S, G, SY = "tensor", "vector", "scalar", "gpsimd", "sync"

    E, I_, SI = 16, 1408, 5632

    ins = {n: nc.dram_tensor(n, list(s), d, kind="ExternalInput")
           for n, (s, d) in {
        "xin": ([NT, HID], F32),
        "nonce": ([1, 1], I32),
        "wq0": ([128, 8, 16, 128], BF16),
        "wk0": ([128, 4, 16, 128], BF16),
        "wv0": ([128, 4, 16, 128], BF16),
        "wg": ([128, 1], F32),
        "bg": ([1, 1], F32),
        "wo0": ([128, 16, HID], BF16),
        "gw_t": ([128, 16, 16], BF16),
        "egu_t": ([E, 128, 22, 16, 128], BF16),
        "edn_t": ([E, I_, HID], BF16),
        "sgu_t": ([128, 88, 16, 128], BF16),
        "sdn_t": ([SI, HID], BF16),
        "sgv_l": ([128, 16], F32),
        "wq1": ([128, 8, 16, 128], BF16),
        "wk1": ([128, 8, 16, 128], BF16),
        "wv1": ([128, 8, 16, 128], BF16),
        "bq_l": ([128, 8], F32),
        "bk_l": ([128, 8], F32),
        "bv_l": ([128, 8], F32),
        "alibi_c": ([128, 8 * 16], F32),
        "negbd": ([8, SEQ], F32),
        "wo1": ([128, 16, HID], BF16),
        "bo": ([1, HID], F32),
        "w1t": ([128, 16, HID], BF16),
        "scale_r": ([1, HID], F32),
        "bias_r": ([1, HID], F32),
        "c_idn": ([128, 128], BF16),
        "c_idn_f32": ([128, 128], F32),
        "c_uincl": ([128, 128], F32),
        "c_ones_row": ([1, 128], F32),
        "c_ones_row_bf": ([1, 128], BF16),
        "c_ones_col_bf": ([128, 1], BF16),
        "c_m_sl": ([128, 128], F32),
        "c_m_su": ([128, 128], F32),
        "c_m_iu": ([128, 128], F32),
        "c_bmask0": ([128, 512], BF16),
        "c_bmask1": ([128, 512], BF16),
        "c_bmask2": ([128, 512], BF16),
        "c_bmask3": ([128, 512], BF16),
    }.items()}

    xout = nc.dram_tensor("xout", [NT, HID], F32, kind="ExternalOutput")
    xres1 = nc.dram_tensor("xres1", [NT, HID], F32)
    xres2 = nc.dram_tensor("xres2", [NT, HID], F32)
    xres3 = nc.dram_tensor("xres3", [NT, HID], F32)
    flaloc = nc.dram_tensor("flaloc", [SEQ * NT], BF16)
    baseloc = nc.dram_tensor("baseloc", [NT * SEQ], BF16)
    shx = nc.dram_tensor("shx", [2 * SHX_REG], BF16, addr_space="Shared")
    sha = nc.dram_tensor("sha", [SHA_SIZE], BF16, addr_space="Shared")
    flags = nc.dram_tensor("flags", [2, 8], I32, addr_space="Shared")

    pid = nc.partition_id()
    half = pid % 2
    oth = 1 - half

    A1 = Arena(p, "A1", 64)
    A2 = Arena(p, "A2", 32)
    A4 = Arena(p, "A4", 60)
    AW = Arena(p, "AW", 16)
    xrt = [p.sbuf(f"xrt{i}", [128, HID], F32) for i in range(2)]
    tmp = make_ln_tmp(p, HID)
    nt_sb = p.sbuf("nt_sb", [1, 1], I32)
    idn_bf = p.sbuf("idn_bf", [128, 128], BF16)
    idn_f = p.sbuf("idn_f", [128, 128], F32)
    uincl = p.sbuf("uincl_s", [128, 128], F32)
    ones_r = p.sbuf("ones_r", [1, 128], F32)
    ones_rb = p.sbuf("ones_rb", [1, 128], BF16)
    ones_cb = p.sbuf("ones_cb", [128, 1], BF16)
    m_sl = p.sbuf("m_sl", [128, 128], F32)
    m_su = p.sbuf("m_su", [128, 128], F32)
    m_iu = p.sbuf("m_iu", [128, 128], F32)
    bmask = [p.sbuf(f"bmask{r}", [128, 512], BF16) for r in range(4)]
    PSB = [p.psum(f"PSB{i}", [128, 1024], F32) for i in range(4)]
    PS = [PSB[i // 2][:, (i % 2) * 512:(i % 2) * 512 + 512] for i in range(8)]
    PTB = [PS[6][:, 0:64].bitcast(BF16), PS[7][:, 0:64].bitcast(BF16)]

    nr = p.reg("nr")
    pr = p.reg("pr")
    pr2 = p.reg("pr2")
    fr = p.reg("fr")
    f2r = p.reg("f2r")
    cr = p.reg("cr")

    def mm(out_ap, l_ap, r_ap, start=True, stop=True):
        p.a(T, lambda t, o=out_ap, a=l_ap, b=r_ap, st=start, sp=stop:
            t.matmul(o, a, b, start=st, stop=sp))

    def scp(out_ap, in_ap):
        p.a(S, lambda s, o=out_ap, i=in_ap: s.copy(o, i))

    def ring(n_iter, load_fn, comp_fn, nbuf=2):
        tm = [p.mark("tensor")] * nbuf
        dm = [0] * nbuf
        p.w(G, "tensor")
        load_fn(0, 0)
        dm[0] = p.dmark("gpsimd")
        for m in range(n_iter):
            bk = m % nbuf
            nb = (m + 1) % nbuf
            if m + 1 < n_iter:
                p.wait_at(G, "tensor", tm[nb])
                load_fn(m + 1, nb)
                dm[nb] = p.dmark("gpsimd")
            p.wait_at(T, "dgpsimd", dm[bk])
            comp_fn(m, bk)
            tm[bk] = p.mark("tensor")

    def act(out_ap, in_ap, f, bias=0.0, scale=1.0, accum=None):
        if accum is None:
            p.a(S, lambda s, o=out_ap, i=in_ap, ff=f, b=bias, sc=scale:
                s.activation(o, i, ff, bias=b, scale=sc))
        else:
            p.a(S, lambda s, o=out_ap, i=in_ap, ff=f, b=bias, sc=scale,
                ac=accum: s.activation(o, i, ff, bias=b, scale=sc,
                                       accum_out=ac))

    def dynap(handle, off_rv, rows, cols, row_stride):
        return bass.AP(handle, off_rv, [[row_stride, rows], [1, cols]])

    def exchange(phase):
        p.w(G, "dgpsimd")

        def do_x(g, ph=phase):
            g.reg_add(pr, nr, ph)
            g.store(dynap(flags, half * 8 + (ph - 1), 1, 1, 8), pr)
            g.reg_add(pr2, pr, pr)
            g.reg_mov(cr, 1)
            with g.While(cr):
                g.load(fr, flags[0:1, ph - 1:ph])
                g.load(f2r, flags[1:2, ph - 1:ph])
                g.reg_add(cr, fr, f2r)
                g.reg_sub(cr, cr, pr2)
        p.ops[G].append(do_x)
        p.emit(G, lambda g: g.nop(), inc=p.xch)
        p.join()

    for nm, dst in [("c_idn", idn_bf), ("c_idn_f32", idn_f),
                    ("c_uincl", uincl), ("c_ones_row", ones_r),
                    ("c_ones_row_bf", ones_rb), ("c_ones_col_bf", ones_cb),
                    ("c_m_sl", m_sl), ("c_m_su", m_su), ("c_m_iu", m_iu)]:
        p.admas(SY, dst[:], ins[nm][:, :])
    for r in range(4):
        p.admas(SY, bmask[r][:], ins[f"c_bmask{r}"][:, :])
    p.admas(G, nt_sb[:], ins["nonce"][:, :])
    p.w(G, "dgpsimd")
    p.ops[G].append(lambda g: g.reg_load(nr, nt_sb[0:1, 0:1]))
    p.a(V, lambda v: v.memset(tmp["weps"][:], HID * LN_EPS))
    p.a(V, lambda v: v.memset(tmp["lnwh"][:], 0.5 * float(np.log(HID))))

    def ph_lntr(src, phase):
        A4.reset(); A2.reset()
        xln = A4.take([128, HID], BF16)
        strips = A2.take([128, 16 * NT], BF16)
        sv, ss = p.auto["vector"], p.auto["scalar"]
        tr_done = [0]
        for t in range(8):
            buf = xrt[t % 2]
            p.w(SY, "vector", "scalar")
            p.admas(SY, buf[:], src[t * 128:(t + 1) * 128, :])
            p.w(V, "dsync")
            p.w(S, "dsync")
            if t > 0:
                p.w(V, "tensor")
            emit_ln(p, buf[:], xln[:], HID, tmp, sv, ss)
            p.w(T, "vector")
            if t == 0:
                p.w(T, "dsync")
            for hc in range(16):
                bk = tr_done[0] % 2
                p.w(T, "scalar")
                p.a(T, lambda tt, bk=bk, hc=hc: tt.transpose(
                    PTB[bk], xln[:, hc * 128:(hc + 1) * 128], idn_bf[:]))
                tr_done[0] += 1
                p.w(S, "tensor")
                scp(strips[:, hc * NT + t * 128: hc * NT + (t + 1) * 128],
                    PTB[bk])
        p.w(G, "scalar")
        p.admas(G, bass.AP(shx, half * SHX_REG,
                           [[NT, 128], [128 * NT, 16], [1, NT]]),
                strips.rearrange("p (hc t) -> p hc t", hc=16))
        exchange(phase)

    def load_xl(xl):
        for hh in range(2):
            for c in range(16):
                p.admas(G, xl[:, c * SEQ + hh * NT: c * SEQ + hh * NT + NT],
                        bass.AP(shx, hh * SHX_REG + c * 128 * NT,
                                [[NT, 128], [1, NT]]))

    def ph_l0_attn():
        A4.reset(); A2.reset(); AW.reset(); A1.reset()
        xl = A1.take([128, 16 * SEQ], BF16)
        wq_sb = [AW.take([128, 16 * 128], BF16) for _ in range(2)]
        wk_sb = AW.take([128, 16 * 128], BF16)
        wv_sb = AW.take([128, 16 * 128], BF16)
        k_fm = A2.take([128, SEQ], BF16)
        v_fm = A2.take([128, SEQ], BF16)
        v_tm = A2.take([128, SEQ], BF16)
        q_fm = [A2.take([128, SEQ], BF16) for _ in range(2)]
        eq_fm = [A2.take([128, SEQ], BF16) for _ in range(2)]
        ek_fm = A2.take([128, SEQ], BF16)
        ek_tm = A4.take([128, SEQ], BF16)
        fla_j = A4.take([128, 16 * 256], BF16)
        obase = A4.take([128, SEQ], BF16)
        lnk = A4.take([1, SEQ], F32)
        lnq = [A4.take([1, SEQ], F32) for _ in range(2)]
        ptile = [A4.take([128, 512], BF16) for _ in range(2)]
        e_sb = A4.take([128, 512], F32)
        rbc = A4.take([128, 512], F32)
        pow_sb = A4.take([128, 5 * 128], BF16)
        powt_sb = A4.take([128, 6 * 128], BF16)
        s_f32 = [A4.take([128, 128], F32) for _ in range(2)]
        xbuf = [A4.take([128, 128], BF16) for _ in range(2)]
        u_bf = A4.take([128, 128], BF16)
        r_bf = A4.take([128, 128], BF16)
        kt_bf = A4.take([128, 128], BF16)
        t_bf = A4.take([128, 128], BF16)
        tt_bf = A4.take([128, 128], BF16)
        pt_bf = [A4.take([128, 128], BF16) for _ in range(2)]
        s_bf = A4.take([128, 128], BF16)
        zt_sb = A4.take([128, 128], F32)
        rrow = A4.take([1, 512], F32)
        arow = A4.take([1, 128], F32)
        bk_r = A4.take([1, 128], F32)
        nck_r = A4.take([1, 128], F32)
        bq_r = [A4.take([1, 128], F32) for _ in range(2)]
        g_sb = A4.take([128, 16], F32)
        gt1 = A4.take([128, 16], F32)
        gt2 = A4.take([128, 16], F32)
        gt3 = A4.take([128, 16], F32)
        gs_sb = A4.take([128, 16], F32)
        cols_sb = A4.take([128, 6], F32)
        negb = A4.take([128, 1], F32)
        kscale = A4.take([128, 1], F32)
        bqe = [A4.take([128, 1], F32) for _ in range(2)]
        ace = A4.take([128, 1], F32)
        neg12 = A4.take([128, 1], F32)
        neg40 = A4.take([128, 1], F32)
        one_t = A4.take([128, 1], F32)
        wg_bf = A4.take([128, 1], BF16)
        bgb = A4.take([128, 1], F32)

        load_xl(xl)
        p.admas(G, wg_bf[:], ins["wg"][:, :])
        p.admas(SY, bgb[:], ins["bg"][:, :].broadcast_to([128, 1]))
        p.a(V, lambda v: v.memset(neg12[:], -QK_SHIFT))
        p.a(V, lambda v: v.memset(neg40[:], -SC_SHIFT))
        p.a(V, lambda v: v.memset(one_t[:], 1.0))

        wload = [0]

        def proj(dst_sb, w_sb, nblocks=2):
            for n in range(nblocks):
                p.wait_at(T, "dgpsimd", wload[0])
                p.w(T, "scalar")
                for c in range(16):
                    mm(PS[0][:, 0:512],
                       w_sb[:, c * 128:(c + 1) * 128],
                       xl[:, c * SEQ + n * 1024: c * SEQ + n * 1024 + 512],
                       start=(c == 0), stop=(c == 15))
                    mm(PS[1][:, 0:512],
                       w_sb[:, c * 128:(c + 1) * 128],
                       xl[:, c * SEQ + n * 1024 + 512: c * SEQ + n * 1024 + 1024],
                       start=(c == 0), stop=(c == 15))
                p.w(S, "tensor")
                scp(dst_sb[:, n * 1024:(n + 1) * 1024], PSB[0][:, 0:1024])

        wgt_use = [0]
        for j in range(4):
            p.wait_at(G, "tensor", wgt_use[0])
            p.admas(G, wk_sb[:].rearrange("p (c j) -> p c j", c=16),
                    ins["wk0"][:, j, :, :])
            p.admas(G, wv_sb[:].rearrange("p (c j) -> p c j", c=16),
                    ins["wv0"][:, j, :, :])
            for dh in range(2):
                p.admas(G, wq_sb[dh][:].rearrange("p (c j) -> p c j", c=16),
                        ins["wq0"][:, 2 * j + dh, :, :])
            wload[0] = p.dmark("gpsimd")
            if j == 0:
                p.w(T, "dsync")
            proj(k_fm, wk_sb)
            proj(v_fm, wv_sb)
            proj(q_fm[0], wq_sb[0])
            proj(q_fm[1], wq_sb[1])
            wgt_use[0] = p.mark("tensor")
            for c in range(16):
                p.w(T, "scalar")
                p.a(T, lambda t, c=c: t.transpose(
                    PTB[c % 2], v_fm[:, c * 128:(c + 1) * 128], idn_bf[:]))
                p.w(S, "tensor")
                scp(v_tm[:, c * 128:(c + 1) * 128], PTB[c % 2])
            p.w(S, "tensor", "vector")
            act(ek_fm[:], k_fm[:], AF.Exp, bias=neg12[:])
            act(eq_fm[0][:], q_fm[0][:], AF.Exp, bias=neg12[:])
            act(eq_fm[1][:], q_fm[1][:], AF.Exp, bias=neg12[:])
            for c in range(16):
                p.w(T, "scalar")
                p.a(T, lambda t, c=c: t.transpose(
                    PTB[c % 2], ek_fm[:, c * 128:(c + 1) * 128], idn_bf[:]))
                p.w(S, "tensor")
                scp(ek_tm[:, c * 128:(c + 1) * 128], PTB[c % 2])
            for (src, dstrow) in [(ek_fm, lnk), (eq_fm[0], lnq[0]),
                                  (eq_fm[1], lnq[1])]:
                for n in range(4):
                    p.w(T, "scalar")
                    mm(PS[2][0:1, 0:512], ones_cb[:, 0:1],
                       src[:, n * 512:(n + 1) * 512])
                    p.w(S, "tensor")
                    act(dstrow[0:1, n * 512:(n + 1) * 512],
                        PS[2][0:1, 0:512], AF.Ln)
            p.w(T, "scalar", "dgpsimd")
            for c in range(16):
                mm(PS[2][:, c:c + 1], k_fm[:, c * 128:(c + 1) * 128],
                   wg_bf[:, 0:1])
            p.w(V, "tensor", "dsync")
            p.a(V, lambda v: v.tensor_scalar(g_sb[:], PS[2][:, 0:16], bgb[:],
                                             None, ALU.add))
            p.w(S, "vector")
            act(gt1[:], g_sb[:], AF.Abs)
            p.w(S, "self")
            act(gt2[:], gt1[:], AF.Exp, scale=-1.0)
            p.w(S, "self")
            act(gt3[:], gt2[:], AF.Ln, bias=one_t[:])
            p.w(V, "scalar")
            p.a(V, lambda v: v.tensor_scalar_min(gt1[:], g_sb[:], 0.0))
            p.w(V, "self")
            p.a(V, lambda v: v.tensor_sub(gt2[:], gt1[:], gt3[:]))
            p.w(V, "self")
            p.a(V, lambda v: v.tensor_scalar_mul(gs_sb[:], gt2[:], 1.0 / 16.0))

            for dh in range(2):
                hl = 2 * j + dh
                p.w(V, "dgpsimd")
                for qb in range(4):
                    nkc = (qb + 1) * 4
                    for kc in range(nkc):
                        sb = 4 + (kc % 2)
                        p.w(T, "scalar", "vector")
                        mm(PS[sb][:, 0:512], k_fm[:, kc * 128:(kc + 1) * 128],
                           q_fm[dh][:, qb * 512:(qb + 1) * 512],
                           start=True, stop=(kc < qb * 4))
                        if kc >= qb * 4:
                            mm(PS[sb][:, 0:512], idn_bf[:],
                               bmask[kc - qb * 4][:], start=False, stop=True)
                        p.w(S, "tensor")
                        act(ptile[kc % 2][:], PS[sb][:, 0:512], AF.Exp,
                            bias=neg40[:], scale=DSCALE)
                        p.w(T, "scalar")
                        mm(PS[2][0:1, 0:512], ones_cb[:, 0:1],
                           ptile[kc % 2][:], start=(kc == 0),
                           stop=(kc == nkc - 1))
                        mm(PS[3][:, 0:512], v_tm[:, kc * 128:(kc + 1) * 128],
                           ptile[kc % 2][:], start=(kc == 0),
                           stop=(kc == nkc - 1))
                    p.w(V, "tensor")
                    p.a(V, lambda v: v.reciprocal(rrow[0:1, 0:512],
                                                  PS[2][0:1, 0:512]))
                    p.w(T, "vector")
                    mm(PS[1][:, 0:512], ones_r[0:1, :], rrow[0:1, 0:512])
                    p.w(S, "tensor")
                    scp(rbc[:], PS[1][:, 0:512])
                    p.w(V, "scalar")
                    p.a(V, lambda v, qb=qb: v.tensor_tensor(
                        obase[:, qb * 512:(qb + 1) * 512], PS[3][:, 0:512],
                        rbc[:], ALU.mult))
                p.w(G, "vector")
                p.admas(G, bass.AP(baseloc, hl * 128 * SEQ,
                                   [[SEQ, 128], [1, SEQ]]), obase[:])

            p.w(V, "dgpsimd")
            for c in range(16):
                cs = slice(c * 128, (c + 1) * 128)
                p.w(T, "vector", "scalar")
                mm(PS[5][0:1, 128:256], gs_sb[:, c:c + 1], uincl[:])
                p.w(S, "tensor")
                scp(arow[0:1, :], PS[5][0:1, 128:256])
                p.w(V, "scalar")
                p.a(V, lambda v, cs=cs: v.tensor_sub(bk_r[0:1, :],
                                                     arow[0:1, :],
                                                     lnk[0:1, cs]))
                p.a(V, lambda v, cs=cs: v.scalar_tensor_tensor(
                    nck_r[0:1, :], arow[0:1, :], -1.0, lnk[0:1, cs],
                    ALU.mult, ALU.subtract))
                p.a(V, lambda v, cs=cs: v.tensor_sub(bq_r[0][0:1, :],
                                                     arow[0:1, :],
                                                     lnq[0][0:1, cs]))
                p.a(V, lambda v, cs=cs: v.tensor_sub(bq_r[1][0:1, :],
                                                     arow[0:1, :],
                                                     lnq[1][0:1, cs]))
                p.w(T, "vector")
                mm(PS[5][:, 0:1], bk_r[0:1, :], ones_r[0:1, 0:1])
                mm(PS[5][:, 1:2], nck_r[0:1, :], ones_r[0:1, 0:1])
                mm(PS[5][:, 2:3], bq_r[0][0:1, :], ones_r[0:1, 0:1])
                mm(PS[5][:, 3:4], bq_r[1][0:1, :], ones_r[0:1, 0:1])
                mm(PS[5][:, 4:5], ones_r[0:1, :], arow[0:1, 127:128])
                p.w(S, "tensor")
                scp(cols_sb[:, 0:5], PS[5][:, 0:5])
                p.w(S, "self")
                act(negb[:], cols_sb[:, 0:1], AF.Exp)
                p.w(V, "scalar")
                p.a(V, lambda v: v.tensor_scalar_mul(negb[:], negb[:], -1.0))
                p.a(V, lambda v: v.tensor_add(cols_sb[:, 5:6],
                                              cols_sb[:, 4:5],
                                              cols_sb[:, 1:2]))
                p.w(S, "vector")
                act(kscale[:], cols_sb[:, 5:6], AF.Exp)
                act(bqe[0][:], cols_sb[:, 2:3], AF.Exp)
                act(bqe[1][:], cols_sb[:, 3:4], AF.Exp)
                act(ace[:], cols_sb[:, 4:5], AF.Exp)
                p.w(T, "scalar")
                mm(PS[0][:, 0:128], bk_r[0:1, :], ones_r[0:1, :], True, False)
                mm(PS[0][:, 0:128], ones_r[0:1, :], nck_r[0:1, :], False, False)
                mm(PS[0][:, 0:128], idn_f[:], m_sl[:], False, True)
                mm(PS[0][:, 128:256], nck_r[0:1, :], ones_r[0:1, :], True, False)
                mm(PS[0][:, 128:256], ones_r[0:1, :], bq_r[0][0:1, :], False, False)
                mm(PS[0][:, 128:256], idn_f[:], m_iu[:], False, True)
                mm(PS[0][:, 256:384], nck_r[0:1, :], ones_r[0:1, :], True, False)
                mm(PS[0][:, 256:384], ones_r[0:1, :], bq_r[1][0:1, :], False, False)
                mm(PS[0][:, 256:384], idn_f[:], m_iu[:], False, True)
                mm(PS[0][:, 384:512], nck_r[0:1, :], ones_r[0:1, :], True, False)
                mm(PS[0][:, 384:512], ones_r[0:1, :], bk_r[0:1, :], False, False)
                mm(PS[0][:, 384:512], idn_f[:], m_su[:], False, True)
                p.w(S, "tensor")
                act(e_sb[:], PS[0][:, 0:512], AF.Exp)
                p.w(T, "scalar")
                mm(PS[1][:, 0:128], ek_fm[:, cs], ek_fm[:, cs])
                mm(PS[1][:, 128:256], ek_fm[:, cs], eq_fm[0][:, cs])
                mm(PS[1][:, 256:384], ek_fm[:, cs], eq_fm[1][:, cs])
                p.w(V, "tensor", "scalar")
                p.a(V, lambda v: v.tensor_tensor(t_bf[:], e_sb[:, 0:128],
                                                 PS[1][:, 0:128], ALU.mult))
                p.a(V, lambda v: v.tensor_tensor(pt_bf[0][:],
                                                 e_sb[:, 128:256],
                                                 PS[1][:, 128:256], ALU.mult))
                p.a(V, lambda v: v.tensor_tensor(pt_bf[1][:],
                                                 e_sb[:, 256:384],
                                                 PS[1][:, 256:384], ALU.mult))
                p.a(V, lambda v: v.tensor_tensor(tt_bf[:], e_sb[:, 384:512],
                                                 PS[1][:, 0:128], ALU.mult))
                p.w(T, "vector")
                prevX, prevXt = t_bf[:], tt_bf[:]
                for lev in range(1, 6):
                    mm(PS[2][:, 0:128], prevXt, prevX)
                    mm(PS[2][:, 128:256], prevX, prevXt)
                    p.w(S, "tensor")
                    scp(pow_sb[:, (lev - 1) * 128:lev * 128], PS[2][:, 0:128])
                    scp(powt_sb[:, (lev - 1) * 128:lev * 128],
                        PS[2][:, 128:256])
                    p.w(T, "scalar")
                    prevX = pow_sb[:, (lev - 1) * 128:lev * 128]
                    prevXt = powt_sb[:, (lev - 1) * 128:lev * 128]
                mm(PS[2][:, 0:128], pow_sb[:, 4 * 128:5 * 128],
                   powt_sb[:, 4 * 128:5 * 128])
                p.w(S, "tensor")
                scp(powt_sb[:, 5 * 128:6 * 128], PS[2][:, 0:128])
                if c > 0:
                    p.w(T, "scalar", "vector")
                    mm(PS[3][:, 0:128], ek_fm[:, cs], s_bf[:])
                    p.w(V, "tensor")
                    p.a(V, lambda v, cs=cs: v.scalar_tensor_tensor(
                        r_bf[:], PS[3][:, 0:128], negb[:], v_tm[:, cs],
                        ALU.mult, ALU.add))
                cur = (v_tm[:, cs] if c == 0 else r_bf[:])
                xb = 0
                for lev in [5, 4, 3, 2, 1]:
                    p.w(T, "scalar", "vector")
                    mm(PS[3][:, 128:256],
                       powt_sb[:, lev * 128:(lev + 1) * 128], cur)
                    p.w(V, "tensor")
                    p.a(V, lambda v, xb=xb, cur=cur: v.tensor_tensor(
                        xbuf[xb][:], cur, PS[3][:, 128:256], ALU.add))
                    cur = xbuf[xb][:]
                    xb = 1 - xb
                p.w(T, "vector")
                mm(PS[3][:, 128:256], powt_sb[:, 0:128], cur)
                p.w(V, "tensor")
                p.a(V, lambda v, cur=cur: v.tensor_tensor(
                    xbuf[xb][:], cur, PS[3][:, 128:256], ALU.add))
                cur = xbuf[xb][:]
                p.w(T, "vector")
                mm(PS[3][:, 128:256], tt_bf[:], cur)
                p.w(V, "tensor")
                p.a(V, lambda v, cur=cur: v.tensor_sub(u_bf[:], cur,
                                                       PS[3][:, 128:256]))
                p.w(T, "vector")
                if c > 0:
                    mm(PS[3][:, 256:384], eq_fm[0][:, cs], s_bf[:])
                    mm(PS[3][:, 384:512], eq_fm[1][:, cs], s_bf[:])
                mm(PS[4][:, 0:128], pt_bf[0][:], u_bf[:])
                mm(PS[4][:, 128:256], pt_bf[1][:], u_bf[:])
                p.w(V, "tensor")
                for dh in range(2):
                    dst = fla_j[:, c * 256 + dh * 128: c * 256 + dh * 128 + 128]
                    if c > 0:
                        p.a(V, lambda v, dh=dh: v.tensor_scalar(
                            zt_sb[:], PS[3][:, 256 + dh * 128:384 + dh * 128],
                            bqe[dh][:], DSCALE, ALU.mult, ALU.mult))
                        p.w(V, "self")
                        p.a(V, lambda v, dh=dh, dst=dst: v.scalar_tensor_tensor(
                            dst, PS[4][:, dh * 128:(dh + 1) * 128], DSCALE,
                            zt_sb[:], ALU.mult, ALU.add))
                    else:
                        p.a(V, lambda v, dh=dh, dst=dst: v.tensor_scalar(
                            dst, PS[4][:, dh * 128:(dh + 1) * 128], DSCALE,
                            None, ALU.mult))
                if c < 15:
                    p.w(V, "scalar")
                    p.a(V, lambda v, cs=cs: v.tensor_scalar(
                        kt_bf[:], ek_tm[:, cs], kscale[:], None, ALU.mult))
                    p.w(T, "vector")
                    mm(PS[5][:, 128:256], kt_bf[:], u_bf[:])
                    p.w(V, "tensor")
                    cu = c % 2
                    if c == 0:
                        p.a(V, lambda v, cu=cu: v.tensor_copy(
                            s_f32[cu][:], PS[5][:, 128:256]))
                    else:
                        p.a(V, lambda v, cu=cu: v.scalar_tensor_tensor(
                            s_f32[cu][:], s_f32[1 - cu][:], ace[:],
                            PS[5][:, 128:256], ALU.mult, ALU.add))
                    p.w(S, "vector")
                    scp(s_bf[:], s_f32[cu][:])
            p.w(G, "vector")
            p.admas(G, bass.AP(flaloc, j * 256,
                               [[NT, 128], [128 * NT, 16], [1, 256]]),
                    fla_j.rearrange("p (c t) -> p c t", c=16))
        p.w(G, "dgpsimd")
        p.admas(G, bass.AP(sha, FLA0 + half * SHX_REG + 0,
                           [[32768, 64], [1, 32768]]),
                bass.AP(flaloc, 0, [[32768, 64], [1, 32768]]))
        p.admas(G, bass.AP(sha, BASE0 + half * (NT * SEQ) + 0,
                           [[32768, 64], [1, 32768]]),
                bass.AP(baseloc, 0, [[32768, 64], [1, 32768]]))
        exchange(2)

    def ph_mix(mode, xsrc, xdst, wo_name):
        A1.reset(); A2.reset(); A4.reset()
        wo_sb = A1.take([128, 16 * HID], BF16)
        bs_sb = A2.take([128, 16 * NT], BF16)
        fla_t = [A4.take([128, HID], BF16) for _ in range(2)]
        mixed = A4.take([128, HID], F32)
        lnout = A4.take([128, HID], F32)
        xo = A4.take([128, HID], F32)
        fs = A4.take([128, NT], F32)
        bo_bc = A4.take([128, HID], F32) if mode == "l1" else None

        for c4 in range(4):
            p.admas(G, wo_sb[:].rearrange("p (c j) -> p c j", c=16)
                    [:, 4 * c4:4 * c4 + 4, :],
                    ins[wo_name][:, 4 * c4:4 * c4 + 4, :])
        for hh in range(2):
            p.admas(G, bs_sb[:, hh * 8 * NT:(hh + 1) * 8 * NT]
                    .rearrange("p (cb t) -> p cb t", cb=8),
                    bass.AP(sha, BASE0 + hh * (NT * SEQ) + half * NT,
                            [[SEQ, 128], [128 * SEQ, 8], [1, NT]]))
        if mode == "l0":
            for hh in range(2):
                p.admas(G, bass.AP(flaloc, hh * NT, [[SEQ, NT], [1, NT]]),
                        bass.AP(sha, FLA0 + hh * SHX_REG + half * (NT * NT),
                                [[NT, NT], [1, NT]]))
            p.w(G, "dgpsimd")
        if mode == "l1":
            p.admas(SY, bo_bc[:], ins["bo"][:, :].broadcast_to([128, HID]))
        wbs_load = p.dmark("gpsimd")
        ps_use = [0, 0]
        fla_use = [0, 0]

        for t in range(8):
            bi = t % 2
            p.w(SY, "vector", "scalar")
            p.admas(SY, xrt[bi][:], xsrc[t * 128:(t + 1) * 128, :])
            if mode == "l0":
                p.wait_at(G, "vector", fla_use[bi])
                p.admas(G, fla_t[bi][:],
                        bass.AP(flaloc, t * 128 * SEQ, [[SEQ, 128], [1, SEQ]]))
            fla_mark = p.dmark("gpsimd")
            p.w(V, "scalar")
            for n in range(2):
                bank = n % 2
                p.wait_at(T, "vector", ps_use[bank])
                if t == 0 and n == 0:
                    p.wait_at(T, "dgpsimd", wbs_load)
                for c in range(16):
                    mm_l = bs_sb[:, c * NT + t * 128: c * NT + (t + 1) * 128]
                    mm_r = wo_sb[:, c * HID + n * 1024: c * HID + n * 1024 + 512]
                    mm_r2 = wo_sb[:, c * HID + n * 1024 + 512:
                                  c * HID + n * 1024 + 1024]
                    p.a(T, lambda tt, o=PS[2 * bank][:, 0:512], a=mm_l, b=mm_r,
                        st=(c == 0), sp=(c == 15): tt.matmul(o, a, b,
                                                             start=st, stop=sp))
                    p.a(T, lambda tt, o=PS[2 * bank + 1][:, 0:512], a=mm_l,
                        b=mm_r2, st=(c == 0), sp=(c == 15):
                        tt.matmul(o, a, b, start=st, stop=sp))
                p.w(V, "tensor", "dsync")
                p.wait_at(V, "dgpsimd", fla_mark)
                if mode == "l0":
                    p.a(V, lambda v, bi=bi, n=n: v.tensor_scalar(
                        fs[:], fla_t[bi][:, n * 1024:(n + 1) * 1024], 0.01,
                        None, ALU.mult))
                    p.w(V, "self")
                    p.a(V, lambda v, n=n, bank=bank: v.scalar_tensor_tensor(
                        mixed[:, n * 1024:(n + 1) * 1024], PSB[bank][:, 0:1024],
                        0.99, fs[:], ALU.mult, ALU.add))
                else:
                    p.a(V, lambda v, n=n, bank=bank: v.tensor_tensor(
                        mixed[:, n * 1024:(n + 1) * 1024], PSB[bank][:, 0:1024],
                        bo_bc[:, n * 1024:(n + 1) * 1024], ALU.add))
                ps_use[bank] = p.mark("vector")
            fla_use[bi] = p.mark("vector")
            p.w(V, "self")
            p.w(S, "vector")
            emit_ln(p, mixed[:], lnout[:], HID, tmp,
                    p.auto["vector"], p.auto["scalar"])
            p.w(V, "self", "dsync")
            p.a(V, lambda v, bi=bi: v.tensor_add(xo[:], xrt[bi][:],
                                                 lnout[:]))
            p.w(SY, "vector")
            p.admas(SY, xdst[t * 128:(t + 1) * 128, :], xo[:])

    def ph_moe():
        A1.reset(); A2.reset(); A4.reset(); AW.reset()
        acc = A1.take([128, 8 * HID], F32)
        xlnt = A2.take([128, 16 * NT], BF16)
        gh = A4.take([128, 44 * 512], BF16)
        xlnf = A4.take([128, HID], F32)
        stmp = A4.take([128, NT], BF16)
        ytmp = A4.take([128, 512], F32)
        comb = A4.take([128, 8 * 16], F32)
        probs = A4.take([128, 16], F32)
        wrk = A4.take([128, 16], F32)
        eqt = A4.take([128, 16], F32)
        mx = A4.take([128, 4], F32)
        sig_sb = A4.take([128, 8], F32)
        s4 = A4.take([128, 1], F32)
        rs4 = A4.take([128, 1], F32)
        negm = A4.take([128, 1], F32)
        lnout = xlnf
        wring = [AW.take([128, 16 * 128], BF16) for _ in range(2)]
        dring = [AW.take([128, 1024], BF16) for _ in range(2)]
        gwt_sb = AW.take([128, 16 * 16], BF16)
        sgv_sb = AW.take([128, 16], BF16)

        p.admas(G, gwt_sb[:].rearrange("p (c j) -> p c j", c=16),
                ins["gw_t"][:, :, :])
        p.admas(G, sgv_sb[:], ins["sgv_l"][:, :])

        for t in range(8):
            bi = t % 2
            p.w(SY, "vector", "scalar")
            p.admas(SY, xrt[bi][:], xres1[t * 128:(t + 1) * 128, :])
            p.w(V, "dsync")
            p.w(S, "dsync")
            p.w(V, "tensor")
            emit_ln(p, xrt[bi][:], xlnf[:], HID, tmp,
                    p.auto["vector"], p.auto["scalar"])
            p.w(T, "vector", "dsync")
            for c in range(16):
                cb = c % 2
                p.w(T, "scalar")
                p.a(T, lambda tt, c=c, cb=cb: tt.transpose(
                    PS[2][:, cb * 128:cb * 128 + 128],
                    xlnf[:, c * 128:(c + 1) * 128], idn_f[:]))
                p.w(S, "tensor")
                scp(xlnt[:, c * NT + t * 128: c * NT + (t + 1) * 128],
                    PS[2][:, cb * 128:cb * 128 + 128])
            p.w(T, "scalar", "dgpsimd")
            for c in range(16):
                mm(PS[0][:, 0:16], xlnt[:, c * NT + t * 128: c * NT + (t + 1) * 128],
                   gwt_sb[:, c * 16:(c + 1) * 16], start=(c == 0),
                   stop=(c == 15))
            p.w(V, "tensor")
            p.a(V, lambda v: v.reduce_max(negm[:], PS[0][:, 0:16], axis=AX.X,
                                          negate=True))
            p.w(S, "vector")
            act(probs[:], PS[0][:, 0:16], AF.Exp, bias=negm[:])
            p.w(V, "scalar")
            p.a(V, lambda v: v.tensor_copy(wrk[:], probs[:]))
            for jj in range(4):
                p.w(V, "self")
                p.a(V, lambda v, jj=jj: v.reduce_max(mx[:, jj:jj + 1], wrk[:],
                                                     axis=AX.X))
                if jj < 3:
                    p.w(V, "self")
                    p.a(V, lambda v, jj=jj: v.tensor_scalar(
                        eqt[:], wrk[:], mx[:, jj:jj + 1], None, ALU.is_ge))
                    p.w(V, "self")
                    p.a(V, lambda v: v.scalar_tensor_tensor(
                        wrk[:], eqt[:], -1e30, wrk[:], ALU.mult, ALU.add))
            p.w(V, "self")
            p.a(V, lambda v: v.tensor_scalar(eqt[:], probs[:], mx[:, 3:4],
                                             None, ALU.is_ge))
            p.w(V, "self")
            p.a(V, lambda v: v.tensor_mul(wrk[:], probs[:], eqt[:]))
            p.w(V, "self")
            p.a(V, lambda v: v.reduce_sum(s4[:], wrk[:], axis=AX.X))
            p.w(V, "self")
            p.a(V, lambda v: v.reciprocal(rs4[:], s4[:]))
            p.w(V, "self")
            p.a(V, lambda v, t=t: v.tensor_scalar(
                comb[:, t * 16:(t + 1) * 16], wrk[:], rs4[:], None, ALU.mult))
            p.w(T, "vector")
            for c in range(16):
                mm(PS[1][:, 0:1], xlnt[:, c * NT + t * 128: c * NT + (t + 1) * 128],
                   sgv_sb[:, c:c + 1], start=(c == 0), stop=(c == 15))
            p.w(S, "tensor")
            act(sig_sb[:, t:t + 1], PS[1][:, 0:1], AF.Sigmoid)

        for blk in range(2):
            tok0 = blk * 512
            sm2 = [p.mark("scalar")] * 2

            def loadA(m, rb):
                p.admas(G, wring[rb][:].rearrange("p (c j) -> p c j", c=16),
                        ins["sgu_t"][:, m, :, :])

            def compA(m, rb, tok0=tok0, sm2=sm2):
                p.wait_at(T, "scalar", sm2[rb])
                for c in range(16):
                    mm(PS[2 * rb][:, 0:512],
                       wring[rb][:, c * 128:(c + 1) * 128],
                       xlnt[:, c * NT + tok0: c * NT + tok0 + 512],
                       start=(c == 0), stop=(c == 15))
                p.w(S, "tensor")
                scp(gh[:, m * 512:(m + 1) * 512], PS[2 * rb][:, 0:512])
                sm2[rb] = p.mark("scalar")

            ring(44, loadA, compA)
            vm2 = [p.mark("vector")] * 2

            def loadB(m, rb):
                p.admas(G, wring[rb][:].rearrange("p (c j) -> p c j", c=16),
                        ins["sgu_t"][:, 44 + m, :, :])

            def compB(m, rb, tok0=tok0, vm2=vm2):
                if m < 2:
                    p.w(T, "scalar")
                p.wait_at(T, "vector", vm2[rb])
                for c in range(16):
                    mm(PS[2 * rb][:, 0:512],
                       wring[rb][:, c * 128:(c + 1) * 128],
                       xlnt[:, c * NT + tok0: c * NT + tok0 + 512],
                       start=(c == 0), stop=(c == 15))
                p.w(S, "tensor", "vector")
                act(stmp[:, 0:512], gh[:, m * 512:(m + 1) * 512], AF.Silu)
                p.w(V, "scalar", "tensor")
                p.a(V, lambda v, m=m, rb=rb: v.tensor_tensor(
                    gh[:, m * 512:(m + 1) * 512], stmp[:, 0:512],
                    PS[2 * rb][:, 0:512], ALU.mult))
                vm2[rb] = p.mark("vector")

            ring(44, loadB, compB)
            for n in range(4):
                p.w(T, "scalar", "vector")

                def loadC(k2, rb, n=n):
                    p.admas(G, dring[rb][:].rearrange(
                        "p (kk j) -> p kk j", kk=2),
                            ins["sdn_t"][k2 * 256:(k2 + 1) * 256,
                                         n * 512:(n + 1) * 512]
                            .rearrange("(kk p) j -> p kk j", p=128))

                def compC(k2, rb):
                    for kk in range(2):
                        k = 2 * k2 + kk
                        for ts in range(4):
                            mm(PS[4 + ts][:, 0:512],
                               gh[:, k * 512 + ts * 128: k * 512 + (ts + 1) * 128],
                               dring[rb][:, kk * 512:(kk + 1) * 512],
                               start=(k == 0), stop=(k == 43))

                ring(22, loadC, compC)
                p.w(S, "tensor")
                for ts in range(4):
                    gts = blk * 4 + ts
                    act(acc[:, gts * HID + n * 512: gts * HID + n * 512 + 512],
                        PS[4 + ts][:, 0:512], AF.Copy,
                        scale=sig_sb[:, gts:gts + 1])

        eg_s = [0, 0]
        eu_v = [0, 0]
        for e in range(16):
            eg_s[0] = eg_s[1] = p.mark("scalar")

            def loadD(m, rb, e=e):
                p.admas(G, wring[rb][:].rearrange("p (c j) -> p c j", c=16),
                        ins["egu_t"][e, :, m, :, :])

            def compD(m, rb):
                p.wait_at(T, "scalar", eg_s[rb])
                for c in range(16):
                    mm(PS[2 * rb][:, 0:512],
                       wring[rb][:, c * 128:(c + 1) * 128],
                       xlnt[:, c * NT: c * NT + 512],
                       start=(c == 0), stop=(c == 15))
                    mm(PS[2 * rb + 1][:, 0:512],
                       wring[rb][:, c * 128:(c + 1) * 128],
                       xlnt[:, c * NT + 512: c * NT + NT],
                       start=(c == 0), stop=(c == 15))
                p.w(S, "tensor")
                scp(gh[:, m * NT: (m + 1) * NT], PSB[rb][:, 0:1024])
                eg_s[rb] = p.mark("scalar")

            ring(11, loadD, compD)

            def loadE(m, rb, e=e):
                p.admas(G, wring[rb][:].rearrange("p (c j) -> p c j", c=16),
                        ins["egu_t"][e, :, 11 + m, :, :])

            def compE(m, rb):
                p.wait_at(T, "vector", eu_v[rb])
                p.wait_at(T, "scalar", eg_s[rb])
                for c in range(16):
                    mm(PS[2 * rb][:, 0:512],
                       wring[rb][:, c * 128:(c + 1) * 128],
                       xlnt[:, c * NT: c * NT + 512],
                       start=(c == 0), stop=(c == 15))
                    mm(PS[2 * rb + 1][:, 0:512],
                       wring[rb][:, c * 128:(c + 1) * 128],
                       xlnt[:, c * NT + 512: c * NT + NT],
                       start=(c == 0), stop=(c == 15))
                p.w(S, "tensor", "vector")
                act(stmp[:], gh[:, m * NT: (m + 1) * NT], AF.Silu)
                p.w(V, "scalar")
                p.a(V, lambda v, m=m, rb=rb: v.tensor_tensor(
                    gh[:, m * NT: (m + 1) * NT],
                    stmp[:], PSB[rb][:, 0:1024], ALU.mult))
                eu_v[rb] = p.mark("vector")

            ring(11, loadE, compE)
            for n in range(4):
                p.w(T, "scalar", "vector")

                def loadF(k2, rb, e=e, n=n):
                    nk = 2 if k2 < 5 else 1
                    p.admas(G, dring[rb][:, 0:nk * 512]
                            .rearrange("p (kk j) -> p kk j", kk=nk),
                            ins["edn_t"][e, k2 * 256: k2 * 256 + nk * 128,
                                         n * 512:(n + 1) * 512]
                            .rearrange("(kk p) j -> p kk j", p=128))

                def compF(k2, rb):
                    nk = 2 if k2 < 5 else 1
                    for kk in range(nk):
                        k = 2 * k2 + kk
                        for ts in range(8):
                            mm(PS[ts][:, 0:512],
                               gh[:, k * NT + ts * 128: k * NT + (ts + 1) * 128],
                               dring[rb][:, kk * 512:(kk + 1) * 512],
                               start=(k == 0), stop=(k == 10))

                ring(6, loadF, compF)
                p.w(S, "tensor", "vector")
                for ts in range(8):
                    act(ytmp[:], PS[ts][:, 0:512], AF.Copy,
                        scale=comb[:, ts * 16 + e: ts * 16 + e + 1])
                    p.w(V, "scalar")
                    dst = acc[:, ts * HID + n * 512: ts * HID + n * 512 + 512]
                    p.a(V, lambda v, dst=dst: v.tensor_tensor(dst, dst,
                                                              ytmp[:], ALU.add))
                    p.w(S, "vector")
                p.w(V, "self")

        p.join()
        A4.reset()
        lnout4 = A4.take([128, HID], F32)
        xo4 = A4.take([128, HID], F32)
        for t in range(8):
            bi = t % 2
            p.w(SY, "vector", "scalar")
            p.admas(SY, xrt[bi][:], xres1[t * 128:(t + 1) * 128, :])
            p.w(V, "dsync", "self")
            p.w(S, "dsync")
            emit_ln(p, acc[:, t * HID:(t + 1) * HID], lnout4[:], HID, tmp,
                    p.auto["vector"], p.auto["scalar"])
            p.w(V, "self", "dsync")
            p.a(V, lambda v, bi=bi: v.tensor_add(xo4[:], xrt[bi][:],
                                                 lnout4[:]))
            p.w(SY, "vector")
            p.admas(SY, xres2[t * 128:(t + 1) * 128, :], xo4[:])

    def ph_l1_attn():
        A1.reset(); A2.reset(); A4.reset(); AW.reset()
        xl = A1.take([128, 16 * SEQ], BF16)
        w_sb = [AW.take([128, 16 * 128], BF16) for _ in range(3)]
        q_fm = A4.take([128, SEQ], BF16)
        k_fm = A4.take([128, SEQ], BF16)
        v_fm = A4.take([128, SEQ], BF16)
        v_tm = A4.take([128, SEQ], BF16)
        qk_el = A4.take([128, SEQ], BF16)
        obase = A4.take([128, SEQ], BF16)
        r_bf = A4.take([1, SEQ], BF16)
        nbd = A4.take([1, SEQ], F32)
        rrow_f = A4.take([1, 512], F32)
        rr = A4.take([1, 512], F32)
        rbc = A4.take([128, 512], F32)
        ptile = [A4.take([128, 512], BF16) for _ in range(2)]
        alibi = A4.take([128, 8 * 16], F32)
        bqc = A4.take([128, 8], F32)
        bkc = A4.take([128, 8], F32)
        bvc = A4.take([128, 8], F32)

        load_xl(xl)
        for nm, dst in [("bq_l", bqc), ("bk_l", bkc), ("bv_l", bvc),
                        ("alibi_c", alibi)]:
            p.admas(SY, dst[:], ins[nm][:, :])

        wgt_use = [0]
        wload = [0]
        for hl in range(8):
            p.wait_at(G, "tensor", wgt_use[0])
            for wi, wn in enumerate(["wq1", "wk1", "wv1"]):
                p.admas(G, w_sb[wi][:].rearrange("p (c j) -> p c j", c=16),
                        ins[wn][:, hl, :, :])
            wload[0] = p.dmark("gpsimd")
            p.w(SY, "vector")
            p.admas(SY, nbd[0:1, :], ins["negbd"][hl:hl + 1, :])
            if hl == 0:
                p.w(T, "dsync")
            for wi, dst, bc in [(0, q_fm, bqc), (1, k_fm, bkc), (2, v_fm, bvc)]:
                for n in range(2):
                    p.wait_at(T, "dgpsimd", wload[0])
                    p.w(T, "scalar", "vector")
                    for c in range(16):
                        mm(PS[0][:, 0:512], w_sb[wi][:, c * 128:(c + 1) * 128],
                           xl[:, c * SEQ + n * 1024: c * SEQ + n * 1024 + 512],
                           start=(c == 0), stop=(c == 15))
                        mm(PS[1][:, 0:512], w_sb[wi][:, c * 128:(c + 1) * 128],
                           xl[:, c * SEQ + n * 1024 + 512:
                              c * SEQ + n * 1024 + 1024],
                           start=(c == 0), stop=(c == 15))
                    p.w(S, "tensor")
                    p.a(S, lambda s, dsl=dst[:, n * 1024:(n + 1) * 1024],
                        bcc=bc[:, hl:hl + 1]: s.add(dsl, PSB[0][:, 0:1024], bcc))
            wgt_use[0] = p.mark("tensor")
            for c in range(16):
                p.w(T, "scalar")
                p.a(T, lambda t, c=c: t.transpose(
                    PTB[c % 2], v_fm[:, c * 128:(c + 1) * 128], idn_bf[:]))
                p.w(S, "tensor")
                p.a(S, lambda s, c=c: s.copy(v_tm[:, c * 128:(c + 1) * 128],
                                             PTB[c % 2]))
            p.w(V, "scalar")
            p.a(V, lambda v: v.tensor_mul(qk_el[:], q_fm[:], k_fm[:]))
            for n in range(4):
                p.w(T, "vector", "scalar")
                mm(PS[2][0:1, 0:512], ones_cb[:, 0:1],
                   qk_el[:, n * 512:(n + 1) * 512])
                p.w(V, "tensor", "dsync")
                p.a(V, lambda v, n=n: v.scalar_tensor_tensor(
                    rrow_f[0:1, :], PS[2][0:1, 0:512], -1.0,
                    nbd[0:1, n * 512:(n + 1) * 512], ALU.mult, ALU.add))
                p.w(S, "vector")
                p.a(S, lambda s, n=n: s.copy(r_bf[0:1, n * 512:(n + 1) * 512],
                                             rrow_f[0:1, :]))
            p.w(V, "dgpsimd")
            for qb in range(4):
                nkc = (qb + 1) * 4
                for kc in range(nkc):
                    sb = 4 + (kc % 2)
                    p.w(T, "scalar", "vector")
                    mm(PS[sb][:, 0:512], k_fm[:, kc * 128:(kc + 1) * 128],
                       q_fm[:, qb * 512:(qb + 1) * 512], start=True, stop=False)
                    mm(PS[sb][:, 0:512], ones_rb[0:1, :],
                       r_bf[0:1, qb * 512:(qb + 1) * 512],
                       start=False, stop=(kc < qb * 4))
                    if kc >= qb * 4:
                        mm(PS[sb][:, 0:512], idn_bf[:], bmask[kc - qb * 4][:],
                           start=False, stop=True)
                    p.w(S, "tensor")
                    act(ptile[kc % 2][:], PS[sb][:, 0:512], AF.Exp,
                        bias=alibi[:, hl * 16 + kc: hl * 16 + kc + 1],
                        scale=DSCALE)
                    p.w(T, "scalar")
                    mm(PS[2][0:1, 0:512], ones_cb[:, 0:1], ptile[kc % 2][:],
                       start=(kc == 0), stop=(kc == nkc - 1))
                    mm(PS[3][:, 0:512], v_tm[:, kc * 128:(kc + 1) * 128],
                       ptile[kc % 2][:], start=(kc == 0), stop=(kc == nkc - 1))
                p.w(V, "tensor")
                p.a(V, lambda v: v.reciprocal(rr[0:1, 0:512],
                                              PS[2][0:1, 0:512]))
                p.w(T, "vector")
                mm(PS[1][:, 0:512], ones_r[0:1, :], rr[0:1, 0:512])
                p.w(S, "tensor")
                p.a(S, lambda s: s.copy(rbc[:], PS[1][:, 0:512]))
                p.w(V, "scalar")
                p.a(V, lambda v, qb=qb: v.tensor_tensor(
                    obase[:, qb * 512:(qb + 1) * 512], PS[3][:, 0:512],
                    rbc[:], ALU.mult))
            p.w(G, "vector")
            p.admas(G, bass.AP(baseloc, hl * 128 * SEQ,
                               [[SEQ, 128], [1, SEQ]]), obase[:])
        p.w(G, "dgpsimd")
        p.admas(G, bass.AP(sha, BASE0 + half * (NT * SEQ) + 0,
                           [[32768, 64], [1, 32768]]),
                bass.AP(baseloc, 0, [[32768, 64], [1, 32768]]))
        exchange(4)

    def ph_l1ffn():
        A1.reset(); A2.reset(); A4.reset()
        w_sb = A1.take([128, 16 * HID], BF16)
        sc_bc = A2.take([128, HID], F32)
        bi_bc = A2.take([128, HID], F32)
        xlnt = A2.take([128, 16 * 128], BF16)
        xlnf = A4.take([128, HID], F32)
        mixed = A4.take([128, HID], F32)
        lnout = A4.take([128, HID], F32)
        xo = A4.take([128, HID], F32)
        y1 = A4.take([128, NT], F32)

        p.admas(SY, sc_bc[:], ins["scale_r"][:, :].broadcast_to([128, HID]))
        p.admas(SY, bi_bc[:], ins["bias_r"][:, :].broadcast_to([128, HID]))
        for c4 in range(4):
            p.admas(G, w_sb[:].rearrange("p (c j) -> p c j", c=16)
                    [:, 4 * c4:4 * c4 + 4, :],
                    ins["w1t"][:, 4 * c4:4 * c4 + 4, :])
        w_load = p.dmark("gpsimd")
        ps_use = [0, 0]

        for t in range(8):
            bi = t % 2
            p.w(SY, "vector", "scalar")
            p.admas(SY, xrt[bi][:], xres3[t * 128:(t + 1) * 128, :])
            p.w(V, "dsync", "tensor")
            p.w(S, "dsync")
            emit_ln(p, xrt[bi][:], xlnf[:], HID, tmp,
                    p.auto["vector"], p.auto["scalar"])
            p.w(T, "vector")
            for c in range(16):
                cb = c % 2
                p.w(T, "scalar")
                p.a(T, lambda tt, c=c, cb=cb: tt.transpose(
                    PS[4][:, cb * 128:cb * 128 + 128],
                    xlnf[:, c * 128:(c + 1) * 128], idn_f[:]))
                p.w(S, "tensor")
                p.a(S, lambda s, c=c, cb=cb: s.copy(
                    xlnt[:, c * 128:(c + 1) * 128],
                    PS[4][:, cb * 128:cb * 128 + 128]))
            for n in range(2):
                bank = n % 2
                if n == 0:
                    p.w(T, "scalar")
                    if t == 0:
                        p.wait_at(T, "dgpsimd", w_load)
                p.wait_at(T, "vector", ps_use[bank])
                for c in range(16):
                    mm(PS[2 * bank][:, 0:512], xlnt[:, c * 128:(c + 1) * 128],
                       w_sb[:, c * HID + n * 1024: c * HID + n * 1024 + 512],
                       start=(c == 0), stop=(c == 15))
                    mm(PS[2 * bank + 1][:, 0:512], xlnt[:, c * 128:(c + 1) * 128],
                       w_sb[:, c * HID + n * 1024 + 512: c * HID + n * 1024 + 1024],
                       start=(c == 0), stop=(c == 15))
                p.w(V, "tensor", "dsync")
                p.a(V, lambda v, n=n, bank=bank: v.tensor_tensor(
                    y1[:], PSB[bank][:, 0:1024], sc_bc[:, n * 1024:(n + 1) * 1024],
                    ALU.mult))
                p.w(V, "self")
                p.a(V, lambda v, n=n: v.tensor_tensor(
                    mixed[:, n * 1024:(n + 1) * 1024], y1[:],
                    bi_bc[:, n * 1024:(n + 1) * 1024], ALU.add))
                ps_use[bank] = p.mark("vector")
            p.w(V, "self")
            p.w(S, "vector")
            emit_ln(p, mixed[:], lnout[:], HID, tmp,
                    p.auto["vector"], p.auto["scalar"])
            p.w(V, "self", "dsync")
            p.a(V, lambda v, bi=bi: v.tensor_add(xo[:], xrt[bi][:],
                                                 lnout[:]))
            p.w(SY, "vector")
            p.admas(SY, xout[t * 128:(t + 1) * 128, :], xo[:])
        p.wait(SY, p.adma[SY], p.adma[SY].n)
        p.wait(G, p.adma[G], p.adma[G].n)

    ph_lntr(ins["xin"], 1)
    ph_l0_attn()
    p.join()
    ph_mix("l0", ins["xin"], xres1, "wo0")
    p.join()
    ph_moe()
    p.join()
    ph_lntr(xres2, 3)
    ph_l1_attn()
    p.join()
    ph_mix("l1", xres2, xres3, "wo1")
    p.join()
    ph_l1ffn()
    if DEBUG_TAPS:
        p.w(G, "dsync")
        for nm, src in [("dbg_x1", xres1), ("dbg_x2", xres2),
                        ("dbg_x3", xres3)]:
            d = nc.dram_tensor(nm, [NT, HID], F32, kind="ExternalOutput")
            p.admas(G, d[:, :], src[:, :])
        dshx = nc.dram_tensor("dbg_shx", [2 * SHX_REG], BF16,
                              kind="ExternalOutput")
        p.admas(G, bass.AP(dshx, 0, [[32768, 128], [1, 32768]]),
                bass.AP(shx, 0, [[32768, 128], [1, 32768]]))
        dsha = nc.dram_tensor("dbg_sha", [SHA_SIZE], BF16,
                              kind="ExternalOutput")
        p.admas(G, bass.AP(dsha, 0, [[32768, 256], [1, 32768]]),
                bass.AP(sha, 0, [[32768, 256], [1, 32768]]))
        p.wait(G, p.adma[G], p.adma[G].n)
    p.finalize()
    return nc



BF = ml_dtypes.bfloat16


def make_maps(kw):
    C = host_constants()
    f32 = np.float32
    g = {k: np.ascontiguousarray(np.asarray(v, dtype=f32)) for k, v in kw.items()}
    X = g["L0_attn_hidden_states"]
    H = 16
    slopes = (2.0 ** (-8.0 / H)) ** np.arange(1, H + 1)
    dist = np.arange(2048, dtype=np.float64) - 2047.0
    egu_t = np.ascontiguousarray(
        g["L0_ffn_expert_gate_up"].transpose(0, 2, 1)
        .reshape(16, 16, 128, 22, 128).transpose(0, 2, 3, 1, 4).astype(BF))
    edn_t = np.ascontiguousarray(g["L0_ffn_expert_down"].transpose(0, 2, 1).astype(BF))
    sgu_t = np.ascontiguousarray(
        g["L0_ffn_shared_gate_up"].T
        .reshape(16, 128, 88, 128).transpose(1, 2, 0, 3).astype(BF))
    sdn_t = np.ascontiguousarray(g["L0_ffn_shared_down"].T.astype(BF))
    sgv_l = np.ascontiguousarray(g["L0_ffn_shared_gate_vec"][0].reshape(16, 128).T)
    gw_t = np.ascontiguousarray(
        g["L0_ffn_gate_w"].T.reshape(16, 128, 16).transpose(1, 0, 2).astype(BF))
    w1t = np.ascontiguousarray(
        g["L1_ffn_weight"].T.reshape(16, 128, 2048).transpose(1, 0, 2).astype(BF))
    nonce = np.array([[np.random.randint(1, 2 ** 28)]], dtype=np.int32)

    def _wl(w, nj):
        return np.ascontiguousarray(
            w.reshape(16, 128, nj, 128).transpose(1, 2, 0, 3).astype(BF))
    wo0_l = np.ascontiguousarray(
        g["L0_attn_Wo"].reshape(16, 128, 2048).transpose(1, 0, 2).astype(BF))
    wo1_l = np.ascontiguousarray(
        g["L1_attn_Wo"].reshape(16, 128, 2048).transpose(1, 0, 2).astype(BF))

    maps = []
    for c in range(8):
        b, h = c // 2, c % 2
        hsl = slice(h * 1024, (h + 1) * 1024)
        alibi_c = np.zeros((128, 8 * 16), np.float32)
        negbd = np.zeros((8, 2048), np.float32)
        for hh in range(8):
            gh_ = h * 8 + hh
            ac = slopes[gh_] * dist
            alibi_c[:, hh * 16:(hh + 1) * 16] = ac.reshape(16, 128).T
            negbd[hh] = -(slopes[gh_] * dist) / DSCALE
        m = {
            "xin": np.ascontiguousarray(X[b, hsl]),
            "nonce": nonce,
            "wq0": _wl(g["L0_attn_Wq"][:, hsl], 8),
            "wk0": _wl(g["L0_attn_Wk"][:, h * 512:(h + 1) * 512], 4),
            "wv0": _wl(g["L0_attn_Wv"][:, h * 512:(h + 1) * 512], 4),
            "wg": g["L0_attn_Wg"],
            "bg": g["L0_attn_bg"].reshape(1, 1),
            "wo0": wo0_l,
            "gw_t": gw_t, "egu_t": egu_t, "edn_t": edn_t, "sgu_t": sgu_t,
            "sdn_t": sdn_t, "sgv_l": sgv_l,
            "wq1": _wl(g["L1_attn_Wq"][:, hsl], 8),
            "wk1": _wl(g["L1_attn_Wk"][:, hsl], 8),
            "wv1": _wl(g["L1_attn_Wv"][:, hsl], 8),
            "bq_l": np.ascontiguousarray(g["L1_attn_bq"][hsl].reshape(8, 128).T),
            "bk_l": np.ascontiguousarray(g["L1_attn_bk"][hsl].reshape(8, 128).T),
            "bv_l": np.ascontiguousarray(g["L1_attn_bv"][hsl].reshape(8, 128).T),
            "alibi_c": alibi_c, "negbd": negbd,
            "wo1": wo1_l,
            "bo": g["L1_attn_bo"].reshape(1, -1),
            "w1t": w1t,
            "scale_r": g["L1_ffn_scale_weight"].reshape(1, -1),
            "bias_r": g["L1_ffn_bias"].reshape(1, -1),
        }
        for n in ["c_idn", "c_idn_f32", "c_uincl", "c_ones_row",
                  "c_ones_row_bf", "c_ones_col_bf", "c_m_sl", "c_m_su",
                  "c_m_iu", "c_bmask0", "c_bmask1", "c_bmask2", "c_bmask3"]:
            m[n] = C[n]
        maps.append(m)
    return maps


def kernel(**kw):
    maps = make_maps(kw)
    nc = bass.Bass("TRN2", target_bir_lowering=False)
    build_fused(nc)
    res = run_bass_kernel_spmd(nc, maps, core_ids=list(range(8)))
    out = np.stack([
        np.concatenate([res.results[2 * b]["xout"],
                        res.results[2 * b + 1]["xout"]], axis=0)
        for b in range(4)])
    return out.astype(np.float32)
